# version 05
# baseline (speedup 1.0000x reference)
import numpy as np
import ml_dtypes
import concourse.bass as bass
import concourse.mybir as mybir
from concourse.bass_utils import run_bass_kernel_spmd

F32 = mybir.dt.float32
BF16 = mybir.dt.bfloat16
Act = mybir.ActivationFunctionType
Alu = mybir.AluOpType

B, L, D = 4, 8192, 1024
H, DK = 16, 64
ED, NE = 512, 8
NCORE = 8
T = 4096
TT = 512
NTILE = T // TT
NBLK = TT // 128
NC = 8
EPS_LN = 1e-5
W2_SCRATCH = False

ENGS = ("sync", "gpsimd", "tensor", "vector", "scalar")

C_ID = 0
C_ONES = 128
C_BFD = 256
C_BSD = 257
C_BTF1 = 258
C_BTF2 = 259
C_EPSLN = 260
C_ZERO = 261
C_EPS6 = 262
C_EPS18 = 263
CW = 288

CB_ONESCOL = 0
CB_BLK = 8
CB_ONESROW = 24
CB_SEL8 = 152
CB_BLK16 = 152 + 8 * 128
CBW = CB_BLK16 + 8 * 16


class Prog:

    def __init__(self, nc, es):
        self.nc = nc
        self.es = es
        self.ops = {e: [] for e in ENGS}
        self.sems = {}
        self.cnt = {}
        self.dma_sems = set()
        self._hw = {e: {} for e in ENGS}

    def sem(self, name):
        if name not in self.sems:
            self.sems[name] = self.es.enter_context(self.nc.semaphore(name))
            self.cnt[name] = 0
        return name

    def cur(self, name):
        self.sem(name)
        return self.cnt[name]

    def wait(self, eng, sem, val=None):
        self.sem(sem)
        v = self.cnt[sem] if val is None else val
        if v <= 0:
            return
        if self._hw[eng].get(sem, 0) >= v:
            return
        self._hw[eng][sem] = v
        self.ops[eng].append(["w", sem, v])

    def drain(self, eng):
        ops = self.ops[eng]
        if ops and ops[-1][0] == "d":
            return
        ops.append(["d"])

    def do(self, eng, fn, inc=None, n=1):
        if inc is not None:
            self.sem(inc)
            self.cnt[inc] += n
            if n == 16:
                self.dma_sems.add(inc)
        self.ops[eng].append(["o", fn, inc, n, self.cnt[inc] if inc else 0])
        return self.cnt[inc] if inc is not None else None

    def fence(self, tag=""):
        rel = {}
        for e in ENGS:
            for it in reversed(self.ops[e]):
                if it[0] == "o":
                    if it[2] is None:
                        self.sem(f"fence_{e}")
                        self.cnt[f"fence_{e}"] += 1
                        it[2] = f"fence_{e}"
                        it[3] = 1
                        it[4] = self.cnt[f"fence_{e}"]
                    rel[e] = (it[2], it[4])
                    break
        for e in ENGS:
            for o in ENGS:
                if o != e and o in rel:
                    self.wait(e, rel[o][0], rel[o][1])
            for s in sorted(self.dma_sems):
                self.wait(e, s)

    def emit(self, block):
        for eng_name in ENGS:
            ops = self.ops[eng_name]

            def make(ops=ops):
                def section(e):
                    for item in ops:
                        k = item[0]
                        if k == "w":
                            e.wait_ge(self.sems[item[1]], item[2])
                        elif k == "d":
                            e.drain()
                        else:
                            inst = item[1](e)
                            if item[2] is not None:
                                inst.then_inc(self.sems[item[2]], item[3])
                return section

            getattr(block, eng_name)(make())


def build_program(enable=("p0", "ffn1", "attn", "ffn2", "p4"), n_pairs=4):
    nc = bass.Bass("TRN2", target_bir_lowering=False)

    x_in = nc.dram_tensor("x", [T, D], F32, kind="ExternalInput").ap()
    y_out = nc.dram_tensor("y", [T, D], F32, kind="ExternalOutput").ap()

    wimg = {}
    for name in ("wq", "wk", "wv", "wo"):
        wimg[name] = nc.dram_tensor(name, [128, NC * D], F32, kind="ExternalInput").ap()
    w1f = nc.dram_tensor("w1f", [128, 9 * NC * ED], F32, kind="ExternalInput").ap()
    w2f = nc.dram_tensor("w2f", [128, 9 * 4 * D], F32, kind="ExternalInput").ap()
    wrf = nc.dram_tensor("wrf", [128, NC * NE], F32, kind="ExternalInput").ap()
    wbf = nc.dram_tensor("wbf", [128, NC * H], F32, kind="ExternalInput").ap()
    wfdf = nc.dram_tensor("wfdf", [128, NC * H], F32, kind="ExternalInput").ap()
    wsdf = nc.dram_tensor("wsdf", [128, NC * H], F32, kind="ExternalInput").ap()
    wtf1f = nc.dram_tensor("wtf1f", [128, 32], F32, kind="ExternalInput").ap()
    wtf2f = nc.dram_tensor("wtf2f", [128, 4 * H], F32, kind="ExternalInput").ap()
    consts = nc.dram_tensor("consts", [128, CW], F32, kind="ExternalInput").ap()
    cbf_in = nc.dram_tensor("cbf", [128, CBW], BF16, kind="ExternalInput").ap()

    r_sc = nc.dram_tensor("r_sc", [128, NC * T], F32)
    w2bf = nc.dram_tensor("w2bf", [128, 9 * 4 * D], BF16)

    cc_in = nc.dram_tensor("cc_in", [128, 1028], F32)
    cc_out = nc.dram_tensor("cc_out", [256, 1028], F32)

    dbg = "dbg" in enable
    if dbg:
        d_flux = nc.dram_tensor("d_flux", [16, T], F32, kind="ExternalOutput").ap()
        d_facc = nc.dram_tensor("d_facc", [16, 8], F32, kind="ExternalOutput").ap()
        d_accM = nc.dram_tensor("d_accM", [128, 1024], F32, kind="ExternalOutput").ap()
        d_mn = nc.dram_tensor("d_mn", [128, 1024], F32, kind="ExternalOutput").ap()
        d_qfm = nc.dram_tensor("d_qfm", [128, NC * TT], F32, kind="ExternalOutput").ap()
        d_ofm = nc.dram_tensor("d_ofm", [128, NC * TT], F32, kind="ExternalOutput").ap()
        d_mb = nc.dram_tensor("d_mb", [128, 1024], F32, kind="ExternalOutput").ap()
        d_rab = nc.dram_tensor("d_rab", [16, 8], F32, kind="ExternalOutput").ap()
        d_dcol = nc.dram_tensor("d_dcol", [16, 4], F32, kind="ExternalOutput").ap()
        d_ebc = nc.dram_tensor("d_ebc", [128, 16], F32, kind="ExternalOutput").ap()
        d_dbc = nc.dram_tensor("d_dbc", [128, 16], F32, kind="ExternalOutput").ap()

    from contextlib import ExitStack
    with ExitStack() as es:
        def sb(name, shape, dt):
            return es.enter_context(nc.sbuf_tensor(name, shape, dt))

        P = Prog(nc, es)

        cst = sb("cst", [128, CW], F32)
        cbf = sb("cbfs", [128, CBW], BF16)
        ident = cst.ap()[:, C_ID:C_ID + 128]
        ones_col_f32 = cst.ap()[:, C_ONES:C_ONES + 1]
        ones_row_f32 = cst.ap()[0:1, C_ONES:C_ONES + 128]
        ones_row_bf = cbf.ap()[0:1, CB_ONESROW:CB_ONESROW + 128]
        pb = [es.enter_context(nc.psum_tensor(f"bank{i}", [128, 512], F32)) for i in range(8)]

        P.do("sync", lambda e: e.dma_start(out=cst[:], in_=consts), inc="cl", n=16)
        P.do("sync", lambda e: e.dma_start(out=cbf[:], in_=cbf_in), inc="cl", n=16)

        def r_view_blk(blk):
            return r_sc.ap().rearrange("p (c t) -> p c t", c=NC)[:, :, blk * 128:(blk + 1) * 128]

        def r_view_tile(tt):
            return r_sc.ap().rearrange("p (c t) -> p c t", c=NC)[:, :, tt * TT:(tt + 1) * TT]

        from contextlib import ExitStack as _ES0

        def transpose_pipe(tag, banks, src_dma, dst_dma):
            sbtag, tag = tag, "tp"
            with _ES0() as tpes:
                a_buf = tpes.enter_context(nc.sbuf_tensor(f"{sbtag}_a", [128, 2 * D], F32))
                b_buf = tpes.enter_context(nc.sbuf_tensor(f"{sbtag}_b", [128, 2 * D], F32))
                for blk in range(T // 128):
                    par = blk % 2
                    asl = a_buf.ap()[:, par * D:(par + 1) * D]
                    bsl = b_buf.ap()[:, par * D:(par + 1) * D]
                    P.wait("sync", f"{tag}_pe{par}")
                    P.do("sync", lambda e, blk=blk, asl=asl: src_dma(e, blk, asl),
                         inc=f"{tag}_ld{par}", n=16)
                    P.wait("tensor", f"{tag}_ld{par}")
                    P.wait("tensor", "cl", 32)
                    P.wait("tensor", f"{tag}_ve{par}")
                    for c in range(NC):
                        pt = banks[par * 2 + c // 4].ap()[:, (c % 4) * 128:(c % 4) * 128 + 128]
                        P.do("tensor",
                             lambda e, pt=pt, asl=asl, c=c: e.transpose(pt, asl[:, c * 128:(c + 1) * 128], ident),
                             inc=f"{tag}_pe{par}")
                    P.wait("vector", f"{tag}_pe{par}")
                    P.wait("vector", f"{tag}_st{par}")
                    for c in range(NC):
                        pt = banks[par * 2 + c // 4].ap()[:, (c % 4) * 128:(c % 4) * 128 + 128]
                        P.do("vector",
                             lambda e, pt=pt, bsl=bsl, c=c: e.tensor_copy(bsl[:, c * 128:(c + 1) * 128], pt),
                             inc=f"{tag}_ve{par}" if c == NC - 1 else None)
                    P.wait("scalar", f"{tag}_ve{par}")
                    P.do("scalar", lambda e, blk=blk, bsl=bsl: dst_dma(e, blk, bsl),
                         inc=f"{tag}_st{par}", n=16)
                P.wait("sync", f"{tag}_st0")
                P.wait("sync", f"{tag}_st1")

        from contextlib import ExitStack as _ES

        def ffn_phase(tag, first=False):
            pes = _ES()
            with pes:
                def psb(name, shape, dt):
                    return pes.enter_context(nc.sbuf_tensor(name, shape, dt))
                sbtag, tag = tag, "f"

                w1i = psb(f"{sbtag}_w1i", [128, 9 * NC * ED], BF16)
                wri = psb(f"{sbtag}_wri", [128, NC * NE], BF16)
                w2b = psb(f"{sbtag}_w2b", [128, 2 * 4 * D], BF16)
                r_t = psb(f"{sbtag}_rt", [128, 2 * NC * TT], F32)
                lnb = psb(f"{sbtag}_lnb", [128, NC * TT], BF16)
                probb = psb(f"{sbtag}_probb", [128, NE * TT], BF16)
                h_sb = psb(f"{sbtag}_h", [128, 9 * 4 * TT], BF16)
                tmpa = psb(f"{sbtag}_tmpa", [128, 4 * TT], BF16)
                invs = psb(f"{sbtag}_invs", [128, 2 * TT], F32)
                ltm = psb(f"{sbtag}_ltm", [128, 64], F32)
                prow = psb(f"{sbtag}_prow", [8, TT], BF16)
                tmpo = psb(f"{sbtag}_tmpo", [128, TT], F32)
                mu_row = psb(f"{sbtag}_mu", [1, TT], F32)
                ex2_row = psb(f"{sbtag}_ex2", [1, TT], F32)
                inv_row = psb(f"{sbtag}_inv", [1, TT], F32)
                m1v = psb(f"{sbtag}_m1", [128, 8], F32)

                rb = lnb.ap()[:, :]
                sq = probb.ap()[:, :]
                invb_sb = invs.ap()[:, 0:TT]
                inv2h_sb = invs.ap()[:, TT:2 * TT]

                wl_base0 = P.cur(f"{tag}_wl")
                P.do("gpsimd", lambda e: e.dma_start(out=wri[:], in_=wrf), inc=f"{tag}_wl", n=16)
                w1_order = [8, 0, 1, 2, 3, 4, 5, 6, 7]
                wl_at = {}
                for pos, ee in enumerate(w1_order):
                    P.wait("gpsimd", f"{tag}_wl")
                    P.do("gpsimd",
                         lambda e, ee=ee: e.dma_start(
                             out=w1i.ap()[:, ee * NC * ED:(ee + 1) * NC * ED],
                             in_=w1f[:, ee * NC * ED:(ee + 1) * NC * ED]),
                         inc=f"{tag}_wl", n=16)
                    wl_at[ee] = P.cur(f"{tag}_wl")
                wl_done = P.cur(f"{tag}_wl")
                wl_router = wl_base0 + 16

                w2cnt = [0]
                grel = []
                w2l_base = [P.cur(f"{tag}_w2l0"), P.cur(f"{tag}_w2l1")]
                w2ls_base = [P.cur(f"{tag}_w2ls0"), P.cur(f"{tag}_w2ls1")]
                w2_src = {}
                w2w_base = P.cur(f"{tag}_w2w")

                def w2_load():
                    g_idx = w2cnt[0]
                    par = g_idx % 2
                    e_idx = g_idx % 9
                    use_fp32 = (not W2_SCRATCH) or (first and g_idx < 9)
                    eng = "gpsimd" if use_fp32 else "sync"
                    if g_idx >= 2:
                        s, v = grel[g_idx - 2]
                        P.wait(eng, s, v)
                        if W2_SCRATCH and first and g_idx - 2 < 9:
                            P.wait(eng, f"{tag}_w2w", w2w_base + 16 * (g_idx - 1))
                    if use_fp32:
                        P.do("gpsimd",
                             lambda e, par=par, e_idx=e_idx: e.dma_start(
                                 out=w2b.ap()[:, par * 4 * D:(par + 1) * 4 * D],
                                 in_=w2f[:, e_idx * 4 * D:(e_idx + 1) * 4 * D]),
                             inc=f"{tag}_w2l{par}", n=16)
                        w2_src[g_idx] = (f"{tag}_w2l{par}", P.cur(f"{tag}_w2l{par}"))
                        if W2_SCRATCH and first:
                            P.wait("scalar", f"{tag}_w2w", P.cur(f"{tag}_w2w"))
                            P.wait("scalar", f"{tag}_w2l{par}", P.cur(f"{tag}_w2l{par}"))
                            P.do("scalar",
                                 lambda e, par=par, e_idx=e_idx: e.dma_start(
                                     out=w2bf[:, e_idx * 4 * D:(e_idx + 1) * 4 * D],
                                     in_=w2b.ap()[:, par * 4 * D:(par + 1) * 4 * D]),
                                 inc=f"{tag}_w2w", n=16)
                    else:
                        if first:
                            P.wait("sync", f"{tag}_w2w", w2w_base + 16 * 9)
                        P.do("sync",
                             lambda e, par=par, e_idx=e_idx: e.dma_start(
                                 out=w2b.ap()[:, par * 4 * D:(par + 1) * 4 * D],
                                 in_=w2bf[:, e_idx * 4 * D:(e_idx + 1) * 4 * D]),
                             inc=f"{tag}_w2ls{par}", n=16)
                        w2_src[g_idx] = (f"{tag}_w2ls{par}", P.cur(f"{tag}_w2ls{par}"))
                    w2cnt[0] += 1

                w2_load()
                w2_load()
                prev_veout = 0
                prev_veout_b = 0
                prev_peout = 0
                e_order = [8, 0, 1, 2, 3, 4, 5, 6, 7]
                tile_ctx = {}

                def h_stage(e_i, idx):
                    bks = pb[0:4] if idx % 2 == 0 else pb[4:8]
                    P.wait("tensor", f"{tag}_wl", tile_ctx["wl_at"][e_i])
                    P.wait("tensor", f"{tag}_sc_relu", max(0, P.cur(f"{tag}_sc_relu") - 1))
                    if idx % 2 == 0:
                        P.wait("tensor", f"{tag}_ve_var", tile_ctx["var_done"])
                        if idx != 0:
                            P.wait("tensor", f"{tag}_ve_lnb", tile_ctx["lnb_done"])
                    else:
                        P.wait("tensor", f"{tag}_ve_prow", tile_ctx["prow_done"])
                        P.wait("tensor", f"{tag}_b5free", P.cur(f"{tag}_b5free"))
                        P.wait("tensor", f"{tag}_b67free0", P.cur(f"{tag}_b67free0"))
                        P.wait("tensor", f"{tag}_b67free1", P.cur(f"{tag}_b67free1"))
                    lnb = tile_ctx["lnb"]
                    w1i = tile_ctx["w1i"]
                    tmpa = tile_ctx["tmpa"]
                    probb = tile_ctx["probb"]
                    h_sb = tile_ctx["h_sb"]
                    for hc in range(4):
                        for c in range(NC):
                            col = (e_i * NC + c) * ED + hc * 128
                            if idx == 0 and hc == 0:
                                P.wait("tensor", f"{tag}_ve_lnb", tile_ctx["lnb_base"] + c + 1)
                            P.do("tensor",
                                 lambda e, hc=hc, col=col, bks=bks, c=c: e.matmul(
                                     bks[hc].ap()[:, :],
                                     w1i.ap()[:, col:col + 128],
                                     lnb.ap()[:, c * TT:(c + 1) * TT],
                                     start=(c == 0), stop=(c == NC - 1)),
                                 inc=f"{tag}_pe_h" if (hc == 3 and c == NC - 1) else None)
                    peh_done = P.cur(f"{tag}_pe_h")
                    P.wait("scalar", f"{tag}_pe_h", peh_done)
                    P.wait("scalar", f"{tag}_ve_h", P.cur(f"{tag}_ve_h"))
                    for hc in range(4):
                        P.do("scalar",
                             lambda e, hc=hc, bks=bks: e.activation(
                                 tmpa.ap()[:, hc * TT:(hc + 1) * TT], bks[hc].ap()[:, :], Act.Relu),
                             inc=f"{tag}_sc_relu" if hc == 3 else None)
                    relu_done = P.cur(f"{tag}_sc_relu")
                    P.wait("vector", f"{tag}_sc_relu", relu_done)
                    hsl = h_sb.ap()[:, e_i * 4 * TT:(e_i + 1) * 4 * TT]
                    if e_i < NE:
                        pbb = probb.ap()[:, e_i * TT:(e_i + 1) * TT].unsqueeze(1) \
                                   .broadcast_to((128, 4, TT))
                        P.do("vector",
                             lambda e, pbb=pbb, hsl=hsl: e.tensor_tensor(
                                 hsl.rearrange("p (a t) -> p a t", a=4),
                                 tmpa.ap()[:, :].rearrange("p (a t) -> p a t", a=4),
                                 pbb, Alu.mult))
                        P.drain("vector")
                        P.do("vector",
                             lambda e, hsl=hsl: e.tensor_mul(hsl, hsl, tmpa.ap()[:, :]),
                             inc=f"{tag}_ve_h")
                    else:
                        P.do("vector",
                             lambda e, hsl=hsl: e.tensor_mul(hsl, tmpa.ap()[:, :], tmpa.ap()[:, :]),
                             inc=f"{tag}_ve_h")
                    P.drain("vector")

                rs_base = P.cur(f"{tag}_rs")
                rl_base = [P.cur(f"{tag}_rl0"), P.cur(f"{tag}_rl1")]

                def r_load(tt):
                    par = tt % 2
                    rt_ = r_t.ap()[:, par * NC * TT:(par + 1) * NC * TT]
                    P.wait("sync", f"{tag}_rs", rs_base + 16 * max(0, tt - 1))
                    P.do("sync",
                         lambda e, tt=tt, rt_=rt_: e.dma_start(
                             out=rt_.rearrange("p (c t) -> p c t", c=NC), in_=r_view_tile(tt)),
                         inc=f"{tag}_rl{par}", n=16)
                    return rt_

                def s1_rbsq(tt, rt_, peh_guard, veh_guard):
                    par = tt % 2
                    P.wait("vector", f"{tag}_rl{par}", rl_base[par] + 16 * (tt // 2 + 1))
                    P.wait("vector", f"{tag}_pe_h", peh_guard)
                    P.wait("vector", f"{tag}_ve_h", veh_guard)
                    P.do("vector", lambda e, rt_=rt_: e.tensor_copy(rb, rt_))
                    P.drain("vector")
                    P.do("vector", lambda e: e.tensor_mul(sq, rb, rb), inc=f"{tag}_ve_sq")
                    return P.cur(f"{tag}_ve_sq")

                rt_next = r_load(0)
                sq_next = s1_rbsq(0, rt_next, 0, 0)
                for tt in range(NTILE):
                    rt = rt_next
                    sq_done = sq_next

                    P.wait("tensor", "cl", 32)
                    P.wait("tensor", f"{tag}_ve_sq", sq_done)
                    P.wait("tensor", f"{tag}_ve_out", min(prev_veout, prev_veout_b + 2))
                    for c in range(NC):
                        P.do("tensor",
                             lambda e, c=c: e.matmul(pb[0].ap()[0:1, :],
                                                     cbf.ap()[:, CB_ONESCOL:CB_ONESCOL + 1],
                                                     rb[:, c * TT:(c + 1) * TT],
                                                     start=(c == 0), stop=(c == NC - 1)))
                    for c in range(NC):
                        P.do("tensor",
                             lambda e, c=c: e.matmul(pb[1].ap()[0:1, :],
                                                     cbf.ap()[:, CB_ONESCOL:CB_ONESCOL + 1],
                                                     sq[:, c * TT:(c + 1) * TT],
                                                     start=(c == 0), stop=(c == NC - 1)),
                             inc=f"{tag}_pe_st" if c == NC - 1 else None)
                    stats_done = P.cur(f"{tag}_pe_st")

                    P.wait("vector", f"{tag}_pe_st", stats_done)
                    P.do("vector", lambda e: e.tensor_scalar_mul(mu_row[:], pb[0].ap()[0:1, :], 1.0 / D))
                    P.do("vector", lambda e: e.tensor_scalar_mul(ex2_row[:], pb[1].ap()[0:1, :], 1.0 / D))
                    P.drain("vector")
                    P.do("vector", lambda e: e.tensor_mul(tmpo.ap()[0:1, :], mu_row[:], mu_row[:]))
                    P.drain("vector")
                    P.do("vector", lambda e: e.tensor_sub(ex2_row[:], ex2_row[:], tmpo.ap()[0:1, :]),
                         inc=f"{tag}_ve_var")
                    var_done = P.cur(f"{tag}_ve_var")
                    P.wait("scalar", f"{tag}_ve_var", var_done)
                    P.do("scalar", lambda e: e.activation(ex2_row[:], ex2_row[:], Act.Ln,
                                                          bias=cst.ap()[0:1, C_EPSLN:C_EPSLN + 1]))
                    P.drain("scalar")
                    P.do("scalar", lambda e: e.activation(inv_row[:], ex2_row[:], Act.Exp, scale=-0.5),
                         inc=f"{tag}_sc_inv")
                    inv_done = P.cur(f"{tag}_sc_inv")

                    P.wait("tensor", f"{tag}_ve_var", var_done)
                    P.wait("tensor", f"{tag}_sc_inv", inv_done)
                    P.wait("tensor", f"{tag}_ve_out", min(prev_veout, prev_veout_b + 4))
                    P.do("tensor", lambda e: e.matmul(pb[2].ap()[:, :], ones_row_f32, mu_row[:],
                                                      start=True, stop=True))
                    P.do("tensor", lambda e: e.matmul(pb[3].ap()[:, :], ones_row_f32, inv_row[:],
                                                      start=True, stop=True), inc=f"{tag}_pe_bc")
                    bc_done = P.cur(f"{tag}_pe_bc")

                    P.wait("vector", f"{tag}_pe_bc", bc_done)
                    P.wait("vector", f"{tag}_pe_st", stats_done)
                    P.do("vector", lambda e: e.tensor_copy(invb_sb, pb[3].ap()[:, :]))
                    P.drain("vector")
                    P.do("vector", lambda e: e.scalar_tensor_tensor(inv2h_sb, pb[3].ap()[:, :], 0.5,
                                                                    invb_sb, Alu.mult, Alu.mult))
                    lnb_base = P.cur(f"{tag}_ve_lnb")
                    for c in range(NC):
                        P.do("vector",
                             lambda e, c=c, rt=rt: e.tensor_sub(lnb.ap()[:, c * TT:(c + 1) * TT],
                                                                rt[:, c * TT:(c + 1) * TT],
                                                                pb[2].ap()[:, :]),
                             inc=f"{tag}_ve_lnb")
                        P.drain("vector")
                    lnb_done = P.cur(f"{tag}_ve_lnb")
                    tile_ctx.update(wl_done=wl_done, wl_at=wl_at, var_done=var_done,
                                    lnb_done=lnb_done, lnb_base=lnb_base, lnb=lnb, w1i=w1i,
                                    tmpa=tmpa, probb=probb, h_sb=h_sb)

                    P.wait("tensor", f"{tag}_wl", wl_router)
                    P.wait("tensor", f"{tag}_ve_out", min(prev_veout, prev_veout_b + 5))
                    for c in range(NC):
                        P.wait("tensor", f"{tag}_ve_lnb", lnb_base + c + 1)
                        P.do("tensor",
                             lambda e, c=c: e.matmul(pb[4].ap()[0:8, :], wri.ap()[:, c * NE:(c + 1) * NE],
                                                     lnb.ap()[:, c * TT:(c + 1) * TT],
                                                     start=(c == 0), stop=(c == NC - 1)),
                             inc=f"{tag}_pe_lg" if c == NC - 1 else None)
                    lg_done = P.cur(f"{tag}_pe_lg")
                    h_stage(8, 0)
                    P.wait("vector", f"{tag}_pe_lg", lg_done)
                    P.drain("vector")
                    P.do("vector", lambda e: e.tensor_mul(tmpo.ap()[0:8, :], pb[4].ap()[0:8, :],
                                                          invb_sb[0:8, :]),
                         inc=f"{tag}_ve_lg")
                    velg_done = P.cur(f"{tag}_ve_lg")
                    P.wait("tensor", f"{tag}_ve_lg", velg_done)
                    P.wait("tensor", f"{tag}_b5free", P.cur(f"{tag}_b5free"))
                    P.wait("tensor", f"{tag}_ve_out", min(prev_veout, prev_veout_b + 6))
                    for j in range(4):
                        P.do("tensor",
                             lambda e, j=j: e.transpose(pb[5].ap()[:, j * 8:(j + 1) * 8],
                                                        tmpo.ap()[0:8, j * 128:(j + 1) * 128],
                                                        ident[0:8, 0:8]),
                             inc=f"{tag}_pe_lt" if j == 3 else None)
                    lt_done = P.cur(f"{tag}_pe_lt")
                    P.wait("vector", f"{tag}_pe_lt", lt_done)
                    P.do("vector", lambda e: e.tensor_copy(ltm.ap()[:, 0:32], pb[5].ap()[:, 0:32]),
                         inc=f"{tag}_b5free")
                    P.drain("vector")
                    lv = ltm.ap()[:, 0:32].rearrange("p (a b) -> p a b", b=8)
                    mask = ltm.ap()[:, 32:64]
                    P.do("vector", lambda e: e.tensor_reduce(m1v.ap()[:, 0:4], lv,
                                                             mybir.AxisListType.X, Alu.max))
                    P.drain("vector")
                    m1rep = m1v.ap()[:, 0:4].unsqueeze(2).broadcast_to((128, 4, 8))
                    P.do("vector", lambda e: e.tensor_tensor(
                        mask.rearrange("p (a b) -> p a b", b=8), lv, m1rep, Alu.is_equal))
                    P.drain("vector")
                    P.do("vector", lambda e: e.scalar_tensor_tensor(
                        ltm.ap()[:, 0:32], mask, -1e30, ltm.ap()[:, 0:32], Alu.mult, Alu.add))
                    P.drain("vector")
                    P.do("vector", lambda e: e.tensor_reduce(
                        m1v.ap()[:, 4:8], ltm.ap()[:, 0:32].rearrange("p (a b) -> p a b", b=8),
                        mybir.AxisListType.X, Alu.max))
                    P.drain("vector")
                    P.do("vector", lambda e: e.tensor_sub(m1v.ap()[:, 4:8], m1v.ap()[:, 4:8],
                                                          m1v.ap()[:, 0:4]),
                         inc=f"{tag}_ve_m2")
                    m2_done = P.cur(f"{tag}_ve_m2")
                    P.wait("scalar", f"{tag}_ve_m2", m2_done)
                    P.do("scalar", lambda e: e.activation(m1v.ap()[:, 4:8], m1v.ap()[:, 4:8], Act.Exp),
                         inc=f"{tag}_sc_g")
                    g_done = P.cur(f"{tag}_sc_g")
                    P.wait("vector", f"{tag}_sc_g", g_done)
                    P.do("vector", lambda e: e.tensor_scalar_add(m1v.ap()[:, 4:8], m1v.ap()[:, 4:8], 1.0))
                    P.drain("vector")
                    P.do("vector", lambda e: e.reciprocal(m1v.ap()[:, 0:4], m1v.ap()[:, 4:8]))
                    P.drain("vector")
                    P.do("vector", lambda e: e.tensor_single_scalar(m1v.ap()[:, 0:4], m1v.ap()[:, 0:4],
                                                                    1.0, Alu.is_equal))
                    P.drain("vector")
                    is1rep = m1v.ap()[:, 0:4].unsqueeze(2).broadcast_to((128, 4, 8))
                    P.do("vector", lambda e: e.tensor_tensor(
                        ltm.ap()[:, 0:32].rearrange("p (a b) -> p a b", b=8),
                        mask.rearrange("p (a b) -> p a b", b=8), is1rep, Alu.mult),
                        inc=f"{tag}_ve_ptm")
                    ptm_done = P.cur(f"{tag}_ve_ptm")
                    P.wait("tensor", f"{tag}_ve_ptm", ptm_done)
                    for j in range(4):
                        P.do("tensor",
                             lambda e, j=j: e.transpose(pb[4].ap()[0:8, j * 128:(j + 1) * 128],
                                                        ltm.ap()[:, j * 8:(j + 1) * 8], ident),
                             inc=f"{tag}_pe_pt" if j == 3 else None)
                    pt_done = P.cur(f"{tag}_pe_pt")
                    P.wait("vector", f"{tag}_pe_pt", pt_done)
                    P.do("vector", lambda e: e.tensor_copy(prow[:], pb[4].ap()[0:8, 0:TT]),
                         inc=f"{tag}_ve_prow")
                    prow_done = P.cur(f"{tag}_ve_prow")
                    tile_ctx["prow_done"] = prow_done
                    P.wait("tensor", f"{tag}_ve_prow", prow_done)
                    P.wait("tensor", f"{tag}_ve_out", prev_veout)
                    P.wait("vector", f"{tag}_pe_st", stats_done)
                    for e_i in range(NE):
                        bk = pb[6 + e_i % 2]
                        P.wait("tensor", f"{tag}_b67free{e_i % 2}", P.cur(f"{tag}_b67free{e_i % 2}"))
                        sel = cbf.ap()[0:8, CB_SEL8 + e_i * 128:CB_SEL8 + (e_i + 1) * 128]
                        P.do("tensor",
                             lambda e, bk=bk, sel=sel: e.matmul(bk.ap()[:, :], sel, prow.ap()[:, :],
                                                                start=True, stop=True),
                             inc=f"{tag}_pe_pb{e_i % 2}")
                        P.wait("vector", f"{tag}_pe_pb{e_i % 2}", P.cur(f"{tag}_pe_pb{e_i % 2}"))
                        P.do("vector",
                             lambda e, bk=bk, e_i=e_i: e.tensor_copy(
                                 probb.ap()[:, e_i * TT:(e_i + 1) * TT], bk.ap()[:, :]),
                             inc=f"{tag}_b67free{e_i % 2}")
                    P.drain("vector")

                    for idx, e_i in enumerate(e_order):
                        if idx == 0:
                            continue
                        h_stage(e_i, idx)
                    if tt + 1 < NTILE:
                        rt_next = r_load(tt + 1)
                        sq_next = s1_rbsq(tt + 1, rt_next, P.cur(f"{tag}_pe_h"),
                                          P.cur(f"{tag}_ve_h"))
                    veh_done = P.cur(f"{tag}_ve_h")

                    P.wait("tensor", f"{tag}_ve_h", veh_done)
                    for e_i in range(9):
                        g_idx = tt * 9 + e_i
                        wpar = g_idx % 2
                        P.wait("tensor", *w2_src[g_idx])
                        for oc in range(NC):
                            for hc in range(4):
                                wcol = wpar * 4 * D + hc * D + oc * 128
                                last_of_group = (oc == NC - 1 and hc == 3)
                                if e_i == 8 and last_of_group:
                                    inc = f"{tag}_pe_out"
                                elif e_i < 8 and last_of_group:
                                    inc = f"{tag}_w2c"
                                else:
                                    inc = None
                                P.do("tensor",
                                     lambda e, oc=oc, hc=hc, wcol=wcol, e_i=e_i: e.matmul(
                                         pb[oc].ap()[:, :],
                                         w2b.ap()[:, wcol:wcol + 128],
                                         h_sb.ap()[:, (e_i * 4 + hc) * TT:(e_i * 4 + hc + 1) * TT],
                                         start=(e_i == 0 and hc == 0), stop=(e_i == 8 and hc == 3)),
                                     inc=inc)
                        if e_i < 8:
                            grel.append((f"{tag}_w2c", P.cur(f"{tag}_w2c")))
                        else:
                            grel.append((f"{tag}_pe_out", P.cur(f"{tag}_pe_out")))
                        if w2cnt[0] < NTILE * 9:
                            w2_load()
                    peout_done = P.cur(f"{tag}_pe_out")

                    P.wait("vector", f"{tag}_pe_out", peout_done)
                    veout_base = P.cur(f"{tag}_ve_out")

                    def s9_oc(oc, rt=rt):
                        P.do("vector",
                             lambda e, oc=oc: e.tensor_mul(tmpo[:], pb[oc].ap()[:, :], inv2h_sb))
                        P.drain("vector")
                        P.do("vector",
                             lambda e, oc=oc, rt=rt: e.tensor_add(rt[:, oc * TT:(oc + 1) * TT],
                                                                  rt[:, oc * TT:(oc + 1) * TT],
                                                                  tmpo[:]),
                             inc=f"{tag}_ve_out")
                        P.drain("vector")

                    for oc in range(NC):
                        s9_oc(oc)
                    veout_done = P.cur(f"{tag}_ve_out")
                    prev_veout = veout_done
                    prev_veout_b = veout_base
                    prev_peout = peout_done

                    P.wait("scalar", f"{tag}_ve_out", veout_done)
                    P.wait("scalar", f"{tag}_rs", rs_base + 16 * tt)
                    P.do("scalar",
                         lambda e, tt=tt, rt=rt: e.dma_start(
                             out=r_view_tile(tt), in_=rt.rearrange("p (c t) -> p c t", c=NC)),
                         inc=f"{tag}_rs", n=16)

                P.wait("sync", f"{tag}_rs")

        def attn_phase(tag="a"):
            pes = _ES()
            with pes:
                def psb(name, shape, dt):
                    return pes.enter_context(nc.sbuf_tensor(name, shape, dt))
                sbtag = tag

                lnb_all = psb(f"{tag}_lnb", [128, NC * T], BF16)
                prows = psb(f"{tag}_prows", [16, T], F32)
                acc_M = psb(f"{tag}_accM", [128, H * DK], F32)
                facc = psb(f"{tag}_facc", [16, 8], F32)
                r_t = psb(f"{sbtag}_rt", [128, 2 * NC * TT], F32)
                asc = psb(f"{tag}_asc", [128, NC * TT], BF16)
                rtile = psb(f"{tag}_rtile", [128, TT], F32)
                rtile2 = psb(f"{tag}_rtile2", [128, TT], F32)
                fwr = psb(f"{tag}_fwr", [16, TT], F32)
                swr = psb(f"{tag}_swr", [16, TT], F32)
                qnr = psb(f"{tag}_qnr", [16, TT], F32)
                mu_row = psb(f"{sbtag}_mu", [1, TT], F32)
                ex2_row = psb(f"{sbtag}_ex2", [1, TT], F32)
                t_row = psb(f"{tag}_trow", [1, TT], F32)
                inv_row = psb(f"{sbtag}_inv", [1, TT], F32)
                ebc = psb(f"{tag}_ebc", [128, 16], F32)
                etr = psb(f"{tag}_etr", [32, 128], F32)
                azeros = psb(f"{tag}_az", [128, TT], F32)

                rb = asc.ap()[:, :]

                P.do("gpsimd", lambda e: e.memset(acc_M[:], 0.0), inc=f"{tag}_gz")
                P.do("gpsimd", lambda e: e.memset(facc[:], 0.0), inc=f"{tag}_gz")
                P.do("gpsimd", lambda e: e.memset(azeros[:], 0.0), inc=f"{tag}_gz")
                P.do("gpsimd", lambda e: e.memset(etr[:], 0.0), inc=f"{tag}_gz")
                P.do("gpsimd", lambda e: e.memset(rtile[:], 0.0), inc=f"{tag}_gz")
                P.do("gpsimd", lambda e: e.memset(rtile2[:], 0.0), inc=f"{tag}_gz")

                def bc_prep(col_f, col_s):
                    P.do("vector", lambda e, col_f=col_f: e.tensor_copy(etr.ap()[0:16, 0:1], col_f))
                    P.do("vector", lambda e, col_s=col_s: e.tensor_copy(etr.ap()[0:16, 32:33], col_s))
                    P.drain("vector")
                    P.do("vector", lambda e: e.transpose(etr.ap()[:, 64:96], etr.ap()[:, 0:32]))
                    P.do("vector", lambda e: e.transpose(etr.ap()[:, 96:128], etr.ap()[:, 32:64]),
                         inc=f"{tag}_ve_bc")
                    vb = P.cur(f"{tag}_ve_bc")
                    P.drain("vector")
                    return vb

                def bc_fs(col_f, col_s, bank, out_sb, vb=None):
                    if vb is None:
                        vb = bc_prep(col_f, col_s)
                    P.wait("tensor", f"{tag}_ve_bc", vb)
                    P.do("tensor", lambda e, bank=bank: e.matmul(
                        bank.ap()[0:64, 0:16], ones_row_f32[0:1, 0:64], etr.ap()[0:1, 64:80],
                        start=True, stop=True))
                    P.do("tensor", lambda e, bank=bank: e.matmul(
                        bank.ap()[64:128, 0:16], ones_row_f32[0:1, 0:64], etr.ap()[0:1, 96:112],
                        start=True, stop=True, tile_position=(0, 64)), inc=f"{tag}_pe_bcm")
                    pbcm = P.cur(f"{tag}_pe_bcm")
                    P.wait("vector", f"{tag}_pe_bcm", pbcm)
                    P.do("vector", lambda e, bank=bank, out_sb=out_sb: e.tensor_copy(
                        out_sb, bank.ap()[:, 0:16]), inc=f"{tag}_ve_bce")
                    P.drain("vector")
                    return P.cur(f"{tag}_ve_bce")

                def ln_stats_lnb(tt):
                    pass

                s1 = _ES()
                with s1:
                    def s1b(name, shape, dt):
                        return s1.enter_context(nc.sbuf_tensor(name, shape, dt))

                    wk_i = s1b(f"{tag}_wk", [128, NC * D], BF16)
                    wv_i = s1b(f"{tag}_wv", [128, NC * D], BF16)
                    wb_i = s1b(f"{tag}_wb", [128, NC * H], BF16)
                    wfd_i = s1b(f"{tag}_wfd", [128, NC * H], BF16)
                    wsd_i = s1b(f"{tag}_wsd", [128, NC * H], BF16)
                    wt1_i = s1b(f"{tag}_wt1", [128, 32], BF16)
                    wt2_i = s1b(f"{tag}_wt2", [128, 4 * H], BF16)
                    k_tm = s1b(f"{tag}_ktm", [128, D], BF16)
                    v_tm = s1b(f"{tag}_vtm", [128, D], BF16)
                    kbh32 = s1b(f"{tag}_kbh", [128, D], F32)
                    kk = s1b(f"{tag}_kk", [128, H * 128], BF16)
                    kbh_fm = s1b(f"{tag}_kfm", [128, NC * TT], BF16)
                    hmid_sb = s1b(f"{tag}_hmid", [128, 4 * TT], BF16)
                    k2t = hmid_sb
                    kninv = s1b(f"{tag}_kninv", [128, 4 * H], F32)
                    packT = s1b(f"{tag}_packT", [128, 96], F32)
                    c1t = s1b(f"{tag}_c1", [128, H], F32)

                    for wi, wf_ in ((wk_i, wimg["wk"]), (wv_i, wimg["wv"]), (wb_i, wbf),
                                    (wfd_i, wfdf), (wsd_i, wsdf), (wt1_i, wtf1f), (wt2_i, wtf2f)):
                        P.do("gpsimd", lambda e, wi=wi, wf_=wf_: e.dma_start(out=wi[:], in_=wf_),
                             inc=f"{tag}_wl", n=16)
                    wl1 = P.cur(f"{tag}_wl")

                    pv = dict(pe_st=0, pe_bc=0, pe_hm=0, sc_row=0, pe_tin=0, ve_r2=0,
                              ve_acc=0, pe_fx=0, ve_bce=0, pe_row=0)
                    prev_lnb = [0]
                    arl_base = [P.cur("f_rl0"), P.cur("f_rl1")]
                    sq = kbh_fm.ap()[:, :]

                    def a_r_load(tt):
                        par = tt % 2
                        rt_ = r_t.ap()[:, par * NC * TT:(par + 1) * NC * TT]
                        P.wait("sync", "f_rs")
                        P.wait("sync", f"{tag}_ve_lnb", prev_lnb[0])
                        P.do("sync",
                             lambda e, tt=tt, rt_=rt_: e.dma_start(
                                 out=rt_.rearrange("p (c t) -> p c t", c=NC), in_=r_view_tile(tt)),
                             inc="f_rl" + str(par), n=16)
                        return rt_

                    def a_s1(tt, rt_):
                        par = tt % 2
                        P.wait("vector", "f_rl" + str(par), arl_base[par] + 16 * (tt // 2 + 1))
                        P.wait("vector", f"{tag}_pe_st", pv["pe_st"])
                        P.wait("vector", f"{tag}_pe_hm", pv["pe_hm"])
                        P.do("vector", lambda e, rt_=rt_: e.tensor_copy(rb, rt_))
                        P.drain("vector")
                        P.do("vector", lambda e: e.tensor_mul(sq, rb, rb), inc=f"{tag}_ve_sq")
                        return P.cur(f"{tag}_ve_sq")

                    rt_next = a_r_load(0)
                    sq_next = a_s1(0, rt_next)
                    for tt in range(NTILE):
                        rt = rt_next
                        sq_done = sq_next

                        P.wait("tensor", "cl", 32)
                        P.wait("tensor", f"{tag}_ve_sq", sq_done)
                        P.wait("tensor", f"{tag}_sc_row", pv["sc_row"])
                        for c in range(NC):
                            P.do("tensor",
                                 lambda e, c=c: e.matmul(pb[0].ap()[0:1, :],
                                                         cbf.ap()[:, CB_ONESCOL:CB_ONESCOL + 1],
                                                         rb[:, c * TT:(c + 1) * TT],
                                                         start=(c == 0), stop=(c == NC - 1)))
                        for c in range(NC):
                            P.do("tensor",
                                 lambda e, c=c, sq=sq: e.matmul(
                                     pb[1].ap()[0:1, :], cbf.ap()[:, CB_ONESCOL:CB_ONESCOL + 1],
                                     sq[:, c * TT:(c + 1) * TT],
                                     start=(c == 0), stop=(c == NC - 1)),
                                 inc=f"{tag}_pe_st" if c == NC - 1 else None)
                        pv["pe_st"] = P.cur(f"{tag}_pe_st")

                        P.wait("vector", f"{tag}_pe_st", pv["pe_st"])
                        P.do("vector", lambda e: e.tensor_scalar_mul(mu_row[:], pb[0].ap()[0:1, :], 1.0 / D))
                        P.do("vector", lambda e: e.tensor_scalar_mul(ex2_row[:], pb[1].ap()[0:1, :], 1.0 / D))
                        P.drain("vector")
                        P.do("vector", lambda e: e.tensor_mul(t_row[:], mu_row[:], mu_row[:]))
                        P.drain("vector")
                        P.do("vector", lambda e: e.tensor_sub(ex2_row[:], ex2_row[:], t_row[:]),
                             inc=f"{tag}_ve_var")
                        var_done = P.cur(f"{tag}_ve_var")
                        P.wait("scalar", f"{tag}_ve_var", var_done)
                        P.do("scalar", lambda e: e.activation(ex2_row[:], ex2_row[:], Act.Ln,
                                                              bias=cst.ap()[0:1, C_EPSLN:C_EPSLN + 1]))
                        P.drain("scalar")
                        P.do("scalar", lambda e: e.activation(inv_row[:], ex2_row[:], Act.Exp, scale=-0.5),
                             inc=f"{tag}_sc_inv")
                        inv_done = P.cur(f"{tag}_sc_inv")

                        P.wait("tensor", f"{tag}_ve_var", var_done)
                        P.wait("tensor", f"{tag}_sc_inv", inv_done)
                        P.wait("tensor", f"{tag}_ve_lnb", P.cur(f"{tag}_ve_lnb"))
                        P.wait("tensor", f"{tag}_ve_r2", pv["ve_r2"])
                        P.wait("tensor", f"{tag}_ve_kv", P.cur(f"{tag}_ve_kv"))
                        P.do("tensor", lambda e: e.matmul(pb[2].ap()[:, :], ones_row_f32, mu_row[:],
                                                          start=True, stop=True))
                        P.do("tensor", lambda e: e.matmul(pb[3].ap()[0:16, :], ones_row_f32[0:1, 0:16],
                                                          inv_row[:], start=True, stop=True),
                             inc=f"{tag}_pe_bc")
                        pv["pe_bc"] = P.cur(f"{tag}_pe_bc")

                        P.wait("vector", f"{tag}_pe_bc", pv["pe_bc"])
                        for c in range(NC):
                            P.do("vector",
                                 lambda e, c=c, tt=tt, rt=rt: e.tensor_sub(
                                     lnb_all.ap()[:, c * T + tt * TT:c * T + tt * TT + TT],
                                     rt[:, c * TT:(c + 1) * TT], pb[2].ap()[:, :]),
                                 inc=f"{tag}_ve_lnb" if c == NC - 1 else None)
                        lnb_done = P.cur(f"{tag}_ve_lnb")
                        prev_lnb[0] = lnb_done
                        P.drain("vector")

                        P.wait("tensor", f"{tag}_wl", wl1)
                        P.wait("tensor", f"{tag}_ve_lnb", lnb_done)
                        P.wait("tensor", f"{tag}_ve_tin", P.cur(f"{tag}_ve_tin"))
                        for bi, wi in ((0, wb_i), (1, wfd_i), (4, wsd_i)):
                            for c in range(NC):
                                P.do("tensor",
                                     lambda e, bi=bi, wi=wi, c=c, tt=tt: e.matmul(
                                         pb[bi].ap()[0:16, :],
                                         wi.ap()[:, c * H:(c + 1) * H],
                                         lnb_all.ap()[:, c * T + tt * TT:c * T + tt * TT + TT],
                                         start=(c == 0), stop=(c == NC - 1)),
                                     inc=f"{tag}_pe_row" if (bi == 4 and c == NC - 1) else None)
                        pv["pe_row"] = P.cur(f"{tag}_pe_row")

                        P.wait("scalar", f"{tag}_pe_row", pv["pe_row"])
                        P.wait("scalar", f"{tag}_pe_tin", pv["pe_tin"])
                        P.wait("scalar", f"{tag}_ve_r2", pv["ve_r2"])
                        P.wait("scalar", f"{tag}_gz", P.cur(f"{tag}_gz"))
                        P.do("scalar", lambda e: e.activation(rtile.ap()[0:16, :], pb[0].ap()[0:16, :],
                                                              Act.Sigmoid))
                        P.do("scalar", lambda e: e.activation(rtile.ap()[32:48, :], pb[1].ap()[0:16, :],
                                                              Act.Sigmoid,
                                                              bias=cst.ap()[0:16, C_BFD:C_BFD + 1]))
                        P.do("scalar", lambda e: e.activation(rtile.ap()[64:80, :], pb[4].ap()[0:16, :],
                                                              Act.Sigmoid,
                                                              bias=cst.ap()[0:16, C_BSD:C_BSD + 1]))
                        P.drain("scalar")
                        P.do("scalar", lambda e: e.activation(rtile.ap()[32:48, :], rtile.ap()[32:48, :],
                                                              Act.Ln,
                                                              bias=cst.ap()[32:48, C_EPS6:C_EPS6 + 1]))
                        P.do("scalar", lambda e: e.activation(rtile.ap()[64:80, :], rtile.ap()[64:80, :],
                                                              Act.Ln,
                                                              bias=cst.ap()[64:80, C_EPS6:C_EPS6 + 1]),
                             inc=f"{tag}_sc_row")
                        pv["sc_row"] = P.cur(f"{tag}_sc_row")

                        P.wait("vector", f"{tag}_sc_row", pv["sc_row"])
                        P.wait("vector", f"{tag}_gz", P.cur(f"{tag}_gz"))
                        P.do("vector", lambda e: e.tensor_tensor_scan(
                            rtile.ap()[96:112, :], azeros.ap()[32:48, :], rtile.ap()[32:48, :], 0.0,
                            Alu.add, Alu.add))
                        P.drain("vector")
                        P.do("vector", lambda e: e.tensor_tensor_scan(
                            rtile2.ap()[64:80, :], azeros.ap()[64:80, :], rtile.ap()[64:80, :], 0.0,
                            Alu.add, Alu.add), inc=f"{tag}_ve_cs")
                        cs_done = P.cur(f"{tag}_ve_cs")
                        P.drain("vector")

                        P.wait("scalar", f"{tag}_ve_cs", cs_done)
                        P.do("scalar", lambda e: e.activation(fwr.ap()[:, :], rtile.ap()[96:112, :],
                                                              Act.Exp, scale=-1.0,
                                                              bias=rtile.ap()[96:112, TT - 1:TT]))
                        P.do("scalar", lambda e: e.activation(swr.ap()[:, :], rtile2.ap()[64:80, :],
                                                              Act.Exp, scale=-1.0,
                                                              bias=rtile2.ap()[64:80, TT - 1:TT]))
                        P.drain("scalar")
                        P.do("scalar", lambda e: e.activation(facc.ap()[:, 4:5],
                                                              rtile.ap()[96:112, TT - 1:TT], Act.Exp))
                        P.do("scalar", lambda e: e.activation(facc.ap()[:, 5:6],
                                                              rtile2.ap()[64:80, TT - 1:TT], Act.Exp),
                             inc=f"{tag}_sc_fw")
                        fw_done = P.cur(f"{tag}_sc_fw")

                        P.wait("vector", f"{tag}_sc_fw", fw_done)
                        P.do("vector", lambda e: e.tensor_copy(etr.ap()[0:16, 1:2],
                                                               rtile.ap()[96:112, TT - 1:TT]))
                        P.do("vector", lambda e: e.tensor_copy(etr.ap()[0:16, 2:3],
                                                               rtile2.ap()[64:80, TT - 1:TT]))
                        P.drain("vector")
                        P.do("vector", lambda e: e.tensor_add(facc.ap()[:, 0:1], facc.ap()[:, 0:1],
                                                              etr.ap()[0:16, 1:2]))
                        P.do("vector", lambda e: e.tensor_add(facc.ap()[:, 1:2], facc.ap()[:, 1:2],
                                                              etr.ap()[0:16, 2:3]))
                        P.drain("vector")
                        P.do("vector", lambda e: e.reduce_sum(facc.ap()[:, 6:7], fwr.ap()[:, :],
                                                              mybir.AxisListType.X))
                        P.do("vector", lambda e: e.reduce_sum(facc.ap()[:, 7:8], swr.ap()[:, :],
                                                              mybir.AxisListType.X))
                        P.drain("vector")
                        P.do("vector", lambda e: e.scalar_tensor_tensor(
                            facc.ap()[:, 2:3], facc.ap()[:, 2:3], facc.ap()[:, 4:5],
                            facc.ap()[:, 6:7], Alu.mult, Alu.add))
                        P.do("vector", lambda e: e.scalar_tensor_tensor(
                            facc.ap()[:, 3:4], facc.ap()[:, 3:4], facc.ap()[:, 5:6],
                            facc.ap()[:, 7:8], Alu.mult, Alu.add))
                        P.drain("vector")
                        vb_tile = bc_prep(facc.ap()[:, 4:5], facc.ap()[:, 5:6])
                        P.do("vector", lambda e: e.tensor_mul(rtile.ap()[32:48, :], rtile.ap()[0:16, :],
                                                              fwr.ap()[:, :]))
                        P.do("vector", lambda e: e.tensor_mul(rtile.ap()[64:80, :], rtile.ap()[0:16, :],
                                                              swr.ap()[:, :]))
                        P.drain("vector")
                        P.do("vector", lambda e: e.tensor_mul(rtile.ap()[32:48, :], rtile.ap()[32:48, :],
                                                              pb[3].ap()[0:16, :]))
                        P.do("vector", lambda e: e.tensor_mul(rtile.ap()[64:80, :], rtile.ap()[64:80, :],
                                                              pb[3].ap()[0:16, :]), inc=f"{tag}_ve_r2")
                        pv["ve_r2"] = P.cur(f"{tag}_ve_r2")
                        P.drain("vector")

                        for blk in range(NBLK):
                            lcol = tt * TT + blk * 128
                            P.wait("tensor", f"{tag}_sc_row", pv["sc_row"])
                            P.wait("tensor", f"{tag}_ve_kv", max(0, P.cur(f"{tag}_ve_kv") - 1))
                            for c in range(NC):
                                for half in range(2):
                                    P.do("tensor",
                                         lambda e, c=c, half=half, lcol=lcol: e.matmul(
                                             pb[half].ap()[:, :],
                                             lnb_all.ap()[:, c * T + lcol:c * T + lcol + 128],
                                             wk_i.ap()[:, c * D + half * 512:c * D + (half + 1) * 512],
                                             start=(c == 0), stop=(c == NC - 1)),
                                         inc=f"{tag}_pe_k" if (c == NC - 1 and half == 1) else None)
                            pek = P.cur(f"{tag}_pe_k")
                            P.wait("vector", f"{tag}_pe_k", pek)
                            P.wait("vector", f"{tag}_pe_M", P.cur(f"{tag}_pe_M"))
                            P.do("vector", lambda e: e.tensor_copy(k_tm.ap()[:, 0:512], pb[0].ap()[:, :]))
                            P.do("vector", lambda e: e.tensor_copy(k_tm.ap()[:, 512:1024],
                                                                   pb[1].ap()[:, :]),
                                 inc=f"{tag}_ve_kv")
                            P.drain("vector")
                            P.wait("vector", f"{tag}_pe_fx", pv["pe_fx"])
                            P.do("vector", lambda e: e.tensor_mul(k2t.ap()[:, 0:D], k_tm[:], k_tm[:]))
                            P.drain("vector")
                            P.do("vector", lambda e, blk=blk: e.tensor_reduce(
                                kninv.ap()[:, blk * H:(blk + 1) * H],
                                k2t.ap()[:, 0:D].rearrange("p (h d) -> p h d", h=H),
                                mybir.AxisListType.X, Alu.add))
                            P.drain("vector")
                            P.do("vector", lambda e, blk=blk: e.tensor_scalar_max(
                                kninv.ap()[:, blk * H:(blk + 1) * H],
                                kninv.ap()[:, blk * H:(blk + 1) * H], 1e-18), inc=f"{tag}_ve_kn")
                            kn_done = P.cur(f"{tag}_ve_kn")
                            P.wait("scalar", f"{tag}_ve_kn", kn_done)
                            P.do("scalar", lambda e, blk=blk: e.activation(
                                kninv.ap()[:, blk * H:(blk + 1) * H],
                                kninv.ap()[:, blk * H:(blk + 1) * H], Act.Ln))
                            P.drain("scalar")
                            P.do("scalar", lambda e, blk=blk: e.activation(
                                kninv.ap()[:, blk * H:(blk + 1) * H],
                                kninv.ap()[:, blk * H:(blk + 1) * H], Act.Exp, scale=-0.5),
                                inc=f"{tag}_sc_kn")
                            skn = P.cur(f"{tag}_sc_kn")
                            P.wait("tensor", f"{tag}_ve_lnb", lnb_done)
                            P.wait("tensor", f"{tag}_ve_r2", pv["ve_r2"])
                            P.wait("tensor", f"{tag}_ve_kv", max(0, P.cur(f"{tag}_ve_kv") - 1))
                            for c in range(NC):
                                for half in range(2):
                                    P.do("tensor",
                                         lambda e, c=c, half=half, lcol=lcol: e.matmul(
                                             pb[2 + half].ap()[:, :],
                                             lnb_all.ap()[:, c * T + lcol:c * T + lcol + 128],
                                             wv_i.ap()[:, c * D + half * 512:c * D + (half + 1) * 512],
                                             start=(c == 0), stop=(c == NC - 1)),
                                         inc=f"{tag}_pe_v" if (c == NC - 1 and half == 1) else None)
                            pevd = P.cur(f"{tag}_pe_v")
                            P.wait("vector", f"{tag}_pe_v", pevd)
                            P.do("vector", lambda e: e.tensor_copy(v_tm.ap()[:, 0:512], pb[2].ap()[:, :]))
                            P.do("vector", lambda e: e.tensor_copy(v_tm.ap()[:, 512:1024],
                                                                   pb[3].ap()[:, :]),
                                 inc=f"{tag}_ve_kv")
                            P.drain("vector")
                            P.wait("tensor", f"{tag}_ve_r2", pv["ve_r2"])
                            P.wait("tensor", f"{tag}_ve_tin", P.cur(f"{tag}_ve_tin"))
                            P.wait("tensor", f"{tag}_ve_kf", P.cur(f"{tag}_ve_kf"))
                            P.do("tensor", lambda e, blk=blk: e.transpose(
                                pb[4].ap()[:, 0:96], rtile.ap()[0:96, blk * 128:(blk + 1) * 128],
                                ident[0:96, 0:96]), inc=f"{tag}_pe_tin")
                            pv["pe_tin"] = P.cur(f"{tag}_pe_tin")
                            P.wait("vector", f"{tag}_pe_tin", pv["pe_tin"])
                            P.do("vector", lambda e: e.tensor_copy(packT[:], pb[4].ap()[:, 0:96]),
                                 inc=f"{tag}_ve_tin")
                            P.drain("vector")
                            P.wait("vector", f"{tag}_sc_kn", skn)
                            P.wait("vector", f"{tag}_pe_M", P.cur(f"{tag}_pe_M"))
                            P.do("vector", lambda e, blk=blk: e.tensor_mul(
                                c1t[:], packT.ap()[:, 0:16], kninv.ap()[:, blk * H:(blk + 1) * H]))
                            P.drain("vector")
                            P.do("vector", lambda e: e.tensor_tensor(
                                kbh32.ap().rearrange("p (h d) -> p h d", h=H),
                                k_tm.ap().rearrange("p (h d) -> p h d", h=H),
                                c1t.ap().unsqueeze(2).broadcast_to((128, H, DK)), Alu.mult),
                                inc=f"{tag}_ve_kb")
                            kb_done = P.cur(f"{tag}_ve_kb")
                            P.drain("vector")
                            P.do("vector", lambda e: e.tensor_tensor(
                                kk.ap().rearrange("p (h x d) -> p h x d", h=H, x=2)[:, :, 0, :],
                                kbh32.ap().rearrange("p (h d) -> p h d", h=H),
                                packT.ap()[:, 32:48].unsqueeze(2).broadcast_to((128, H, DK)),
                                Alu.mult))
                            P.do("vector", lambda e: e.tensor_tensor(
                                kk.ap().rearrange("p (h x d) -> p h x d", h=H, x=2)[:, :, 1, :],
                                kbh32.ap().rearrange("p (h d) -> p h d", h=H),
                                packT.ap()[:, 64:80].unsqueeze(2).broadcast_to((128, H, DK)),
                                Alu.mult), inc=f"{tag}_ve_kk")
                            kk_done = P.cur(f"{tag}_ve_kk")
                            P.drain("vector")
                            P.wait("tensor", f"{tag}_ve_kb", kb_done)
                            P.wait("tensor", f"{tag}_ve_tin", P.cur(f"{tag}_ve_tin"))
                            P.wait("tensor", f"{tag}_ve_kf", P.cur(f"{tag}_ve_kf"))
                            P.wait("tensor", f"{tag}_ve_bce", pv["ve_bce"])
                            P.wait("tensor", f"{tag}_sc_fx", P.cur(f"{tag}_sc_fx"))
                            for c in range(NC):
                                P.do("tensor",
                                     lambda e, c=c: e.transpose(
                                         pb[4 + c // 4].ap()[:, (c % 4) * 128:(c % 4) * 128 + 128],
                                         kbh32.ap()[:, c * 128:(c + 1) * 128], ident),
                                     inc=f"{tag}_pe_kt" if c == NC - 1 else None)
                            kt_done = P.cur(f"{tag}_pe_kt")
                            P.wait("vector", f"{tag}_pe_kt", kt_done)
                            for c in range(NC):
                                P.do("vector",
                                     lambda e, c=c, blk=blk: e.tensor_copy(
                                         kbh_fm.ap()[:, c * TT + blk * 128:c * TT + blk * 128 + 128],
                                         pb[4 + c // 4].ap()[:, (c % 4) * 128:(c % 4) * 128 + 128]),
                                     inc=f"{tag}_ve_kf" if c == NC - 1 else None)
                            P.drain("vector")

                            P.wait("tensor", f"{tag}_ve_kk", kk_done)
                            P.wait("tensor", f"{tag}_ve_acc", pv["ve_acc"])
                            for h in range(H):
                                bk = pb[6 + h // 8]
                                st_ = (blk == 0 and h % 8 == 0)
                                sp_ = (blk == 3 and h % 8 == 7)
                                P.do("tensor",
                                     lambda e, h=h, bk=bk, st_=st_, sp_=sp_: e.matmul(
                                         bk.ap()[:, (h % 8) * DK:(h % 8 + 1) * DK],
                                         kk.ap()[:, h * 128:(h + 1) * 128],
                                         v_tm.ap()[:, h * DK:(h + 1) * DK],
                                         start=st_, stop=sp_),
                                     inc=f"{tag}_pe_M" if h == H - 1 else None)
                        kf_all = P.cur(f"{tag}_ve_kf")
                        P.wait("tensor", f"{tag}_ve_kf", kf_all)
                        P.wait("tensor", f"{tag}_sc_hm", P.cur(f"{tag}_sc_hm"))
                        P.wait("tensor", f"{tag}_ve_kv", P.cur(f"{tag}_ve_kv"))
                        P.wait("tensor", f"{tag}_ve_lnb", lnb_done)
                        for h in range(H):
                            c, par = h // 2, h % 2
                            bk = pb[2 + h // 4]
                            P.do("tensor",
                                 lambda e, h=h, c=c, par=par, bk=bk: e.matmul(
                                     bk.ap()[32 * (h % 4):32 * (h % 4) + 32, :],
                                     wt1_i.ap()[64 * par:64 * par + 64, :],
                                     kbh_fm.ap()[64 * par:64 * par + 64, c * TT:(c + 1) * TT],
                                     start=True, stop=True,
                                     tile_position=(64 * par, 32 * (h % 4))),
                                 inc=f"{tag}_pe_hm" if h == H - 1 else None)
                        pv["pe_hm"] = P.cur(f"{tag}_pe_hm")
                        if tt + 1 < NTILE:
                            rt_next = a_r_load(tt + 1)
                            sq_next = a_s1(tt + 1, rt_next)
                        P.wait("scalar", f"{tag}_pe_hm", pv["pe_hm"])
                        P.wait("scalar", f"{tag}_pe_fx", pv["pe_fx"])
                        for g in range(4):
                            P.do("scalar",
                                 lambda e, g=g: e.activation(
                                     hmid_sb.ap()[:, g * TT:(g + 1) * TT], pb[2 + g].ap()[:, :],
                                     Act.Silu, bias=cst.ap()[:, C_BTF1:C_BTF1 + 1]),
                                 inc=f"{tag}_sc_hm" if g == 3 else None)
                        hm_done = P.cur(f"{tag}_sc_hm")
                        P.wait("tensor", f"{tag}_sc_hm", hm_done)
                        P.wait("tensor", f"{tag}_ve_bce", pv["ve_bce"])
                        for g in range(4):
                            P.do("tensor",
                                 lambda e, g=g: e.matmul(
                                     pb[5].ap()[0:16, :], wt2_i.ap()[:, g * H:(g + 1) * H],
                                     hmid_sb.ap()[:, g * TT:(g + 1) * TT],
                                     start=(g == 0), stop=(g == 3)),
                                 inc=f"{tag}_pe_fx" if g == 3 else None)
                        pv["pe_fx"] = P.cur(f"{tag}_pe_fx")
                        P.wait("scalar", f"{tag}_pe_fx", pv["pe_fx"])
                        P.do("scalar", lambda e, tt=tt: e.activation(
                            prows.ap()[:, tt * TT:(tt + 1) * TT], pb[5].ap()[0:16, :], Act.Sigmoid,
                            bias=cst.ap()[0:16, C_BTF2:C_BTF2 + 1]), inc=f"{tag}_sc_fx")
                        fx_done = P.cur(f"{tag}_sc_fx")
                        P.wait("vector", f"{tag}_sc_fx", fx_done)
                        P.do("vector", lambda e, tt=tt: e.tensor_scalar(
                            prows.ap()[:, tt * TT:(tt + 1) * TT],
                            prows.ap()[:, tt * TT:(tt + 1) * TT],
                            0.01, 0.99, Alu.max, Alu.min, accum_out=None), inc=f"{tag}_ve_fx")
                        P.drain("vector")

                        P.wait("tensor", f"{tag}_sc_fx", fx_done)
                        P.wait("tensor", f"{tag}_ve_kf", P.cur(f"{tag}_ve_kf"))
                        pv["ve_bce"] = bc_fs(facc.ap()[:, 4:5], facc.ap()[:, 5:6], pb[5],
                                             ebc.ap()[:, :], vb=vb_tile)
                        P.wait("vector", f"{tag}_pe_M", P.cur(f"{tag}_pe_M"))
                        P.do("vector", lambda e: e.tensor_tensor(
                            acc_M.ap().rearrange("p (h d) -> p h d", h=H),
                            acc_M.ap().rearrange("p (h d) -> p h d", h=H),
                            ebc.ap().unsqueeze(2).broadcast_to((128, H, DK)), Alu.mult))
                        P.drain("vector")
                        P.do("vector", lambda e: e.tensor_add(acc_M.ap()[:, 0:512],
                                                              acc_M.ap()[:, 0:512], pb[6].ap()[:, :]))
                        P.do("vector", lambda e: e.tensor_add(acc_M.ap()[:, 512:1024],
                                                              acc_M.ap()[:, 512:1024], pb[7].ap()[:, :]),
                             inc=f"{tag}_ve_acc")
                        pv["ve_acc"] = P.cur(f"{tag}_ve_acc")
                        P.drain("vector")

                for s_ in ("pe_k", "pe_v", "pe_M", "pe_kt", "pe_hm", "pe_fx", "sc_hm",
                           "ve_kk", "ve_kf", "ve_acc", "ve_kv", "ve_tin", "sc_kn", "ve_kn",
                           "pe_tin", "sc_fx", "ve_fx", "pe_row", "sc_row"):
                    P.wait("gpsimd", f"{tag}_{s_}", P.cur(f"{tag}_{s_}"))
                P.wait("gpsimd", f"{tag}_ve_acc", P.cur(f"{tag}_ve_acc"))
                P.do("gpsimd", lambda e: e.dma_start(out=cc_in[:, 0:1024], in_=acc_M[:]),
                     inc=f"{tag}_ccin", n=16)
                P.do("gpsimd", lambda e: e.dma_start(out=cc_in[0:16, 1024:1028], in_=facc.ap()[:, 0:4]),
                     inc=f"{tag}_ccin", n=16)
                P.wait("gpsimd", "cl", 32)
                P.do("gpsimd", lambda e: e.dma_start(out=cc_in[16:128, 1024:1028],
                                                     in_=cst.ap()[16:128, 272:276]),
                     inc=f"{tag}_ccin", n=16)
                if dbg:
                    P.do("gpsimd", lambda e: e.dma_start(out=d_accM, in_=acc_M[:]),
                         inc=f"{tag}_dbg", n=16)
                    P.do("gpsimd", lambda e: e.dma_start(out=d_facc, in_=facc[:]),
                         inc=f"{tag}_dbg", n=16)
                    P.wait("gpsimd", f"{tag}_ve_fx", P.cur(f"{tag}_ve_fx"))
                    P.do("gpsimd", lambda e: e.dma_start(out=d_flux, in_=prows[:]),
                         inc=f"{tag}_dbg", n=16)
                P.wait("gpsimd", f"{tag}_ccin")
                P.do("gpsimd", lambda e: e.collective_compute(
                    "AllGather", Alu.bypass,
                    replica_groups=[[2 * i, 2 * i + 1] for i in range(n_pairs)],
                    ins=[cc_in.ap()], outs=[cc_out.ap()]), inc=f"{tag}_cc")

                s2 = _ES()
                with s2:
                    def s2b(name, shape, dt):
                        return s2.enter_context(nc.sbuf_tensor(name, shape, dt))

                    wq_i = s2b(f"{tag}_wq", [128, NC * D], BF16)
                    wo_i = s2b(f"{tag}_wo", [128, NC * D], BF16)
                    q_fm = s2b(f"{tag}_qfm", [128, NC * TT], BF16)
                    ma_sb = s2b(f"{tag}_ma", [128, 1024], F32)
                    mb_sb = s2b(f"{tag}_mb", [128, 1024], F32)
                    mn_sb = ma_sb
                    rab = s2b(f"{tag}_rab", [16, 8], F32)
                    bdf = s2b(f"{tag}_bdf", [128, NC * 128], BF16)
                    bds = s2b(f"{tag}_bds", [128, NC * 128], BF16)
                    o32 = s2b(f"{tag}_o32", [128, 1024], F32)
                    gT = s2b(f"{tag}_gT", [128, 64], F32)
                    dcol = s2b(f"{tag}_dcol", [16, 4], F32)
                    dbc = s2b(f"{tag}_dbc", [128, 16], F32)

                    P.do("gpsimd", lambda e: e.dma_start(out=wq_i[:], in_=wimg["wq"]),
                         inc=f"{tag}_wl", n=16)
                    wl2q = P.cur(f"{tag}_wl")
                    P.wait("gpsimd", f"{tag}_wl", wl2q)
                    P.do("gpsimd", lambda e: e.dma_start(out=wo_i[:], in_=wimg["wo"]),
                         inc=f"{tag}_wl", n=16)
                    wl2 = P.cur(f"{tag}_wl")
                    P.wait("gpsimd", f"{tag}_cc")
                    P.do("gpsimd", lambda e: e.dma_start(out=ma_sb[:], in_=cc_out[0:128, 0:1024]),
                         inc=f"{tag}_ccl", n=16)
                    P.do("gpsimd", lambda e: e.dma_start(out=mb_sb[:], in_=cc_out[128:256, 0:1024]),
                         inc=f"{tag}_ccl", n=16)
                    P.do("gpsimd", lambda e: e.dma_start(out=rab.ap()[:, 0:4],
                                                         in_=cc_out[0:16, 1024:1028]),
                         inc=f"{tag}_ccl", n=16)
                    P.do("gpsimd", lambda e: e.dma_start(out=rab.ap()[:, 4:8],
                                                         in_=cc_out[128:144, 1024:1028]),
                         inc=f"{tag}_ccl", n=16)
                    ccl_done = P.cur(f"{tag}_ccl")

                    P.wait("scalar", f"{tag}_ccl", ccl_done)
                    P.do("scalar", lambda e: e.activation(dcol.ap()[:, 0:1], rab.ap()[:, 4:5], Act.Exp))
                    P.do("scalar", lambda e: e.activation(dcol.ap()[:, 1:2], rab.ap()[:, 5:6], Act.Exp),
                         inc=f"{tag}_sc_eb")
                    eb_done = P.cur(f"{tag}_sc_eb")
                    P.wait("vector", f"{tag}_sc_eb", eb_done)
                    P.do("vector", lambda e: e.scalar_tensor_tensor(
                        dcol.ap()[:, 2:3], rab.ap()[:, 2:3], dcol.ap()[:, 0:1], rab.ap()[:, 6:7],
                        Alu.mult, Alu.add))
                    P.do("vector", lambda e: e.scalar_tensor_tensor(
                        dcol.ap()[:, 3:4], rab.ap()[:, 3:4], dcol.ap()[:, 1:2], rab.ap()[:, 7:8],
                        Alu.mult, Alu.add))
                    P.drain("vector")
                    P.do("vector", lambda e: e.tensor_scalar_add(dcol.ap()[:, 2:4], dcol.ap()[:, 2:4],
                                                                 1e-6))
                    P.drain("vector")
                    P.do("vector", lambda e: e.reciprocal(dcol.ap()[:, 2:4], dcol.ap()[:, 2:4]))
                    P.drain("vector")
                    bc_fs(dcol.ap()[:, 0:1], dcol.ap()[:, 1:2], pb[5], ebc.ap()[:, :])
                    dj = bc_fs(dcol.ap()[:, 2:3], dcol.ap()[:, 3:4], pb[5], dbc.ap()[:, :])
                    P.wait("vector", f"{tag}_ccl", ccl_done)
                    P.do("vector", lambda e: e.tensor_tensor(
                        mn_sb.ap().rearrange("p (h d) -> p h d", h=H),
                        mn_sb.ap().rearrange("p (h d) -> p h d", h=H),
                        ebc.ap().unsqueeze(2).broadcast_to((128, H, DK)), Alu.mult))
                    P.drain("vector")
                    P.do("vector", lambda e: e.tensor_add(mn_sb[:], mn_sb[:], mb_sb[:]))
                    P.drain("vector")
                    P.do("vector", lambda e: e.tensor_tensor(
                        mn_sb.ap().rearrange("p (h d) -> p h d", h=H),
                        mn_sb.ap().rearrange("p (h d) -> p h d", h=H),
                        dbc.ap().unsqueeze(2).broadcast_to((128, H, DK)), Alu.mult),
                        inc=f"{tag}_ve_mn")
                    mn_done = P.cur(f"{tag}_ve_mn")
                    P.drain("vector")

                    if dbg:
                        P.wait("gpsimd", f"{tag}_ve_mn", mn_done)
                        P.do("gpsimd", lambda e: e.dma_start(out=d_mn, in_=mn_sb[:]),
                             inc=f"{tag}_dbg", n=16)
                        P.do("gpsimd", lambda e: e.dma_start(out=d_mb, in_=mb_sb[:]),
                             inc=f"{tag}_dbg", n=16)
                        P.do("gpsimd", lambda e: e.dma_start(out=d_rab, in_=rab[:]),
                             inc=f"{tag}_dbg", n=16)
                        P.do("gpsimd", lambda e: e.dma_start(out=d_dcol, in_=dcol[:]),
                             inc=f"{tag}_dbg", n=16)
                        P.do("gpsimd", lambda e: e.dma_start(out=d_ebc, in_=ebc[:]),
                             inc=f"{tag}_dbg", n=16)
                        P.do("gpsimd", lambda e: e.dma_start(out=d_dbc, in_=dbc[:]),
                             inc=f"{tag}_dbg", n=16)
                    P.wait("gpsimd", f"{tag}_ve_mn", mn_done)
                    P.do("gpsimd", lambda e: e.memset(bdf[:], 0.0))
                    P.do("gpsimd", lambda e: e.memset(bds[:], 0.0))
                    P.drain("gpsimd")
                    mn_e = mn_sb.ap().rearrange("p (c x) -> p c x", c=H)
                    P.do("gpsimd", lambda e: e.dma_start(
                        out=bdf.ap().rearrange("p (c x) -> p c x", c=NC)[0:64, :, 0:64],
                        in_=mn_e[0:64, 0::2, :]), inc=f"{tag}_bd", n=16)
                    P.do("gpsimd", lambda e: e.dma_start(
                        out=bdf.ap().rearrange("p (c x) -> p c x", c=NC)[64:128, :, 64:128],
                        in_=mn_e[0:64, 1::2, :]), inc=f"{tag}_bd", n=16)
                    P.do("gpsimd", lambda e: e.dma_start(
                        out=bds.ap().rearrange("p (c x) -> p c x", c=NC)[0:64, :, 0:64],
                        in_=mn_e[64:128, 0::2, :]), inc=f"{tag}_bd", n=16)
                    P.do("gpsimd", lambda e: e.dma_start(
                        out=bds.ap().rearrange("p (c x) -> p c x", c=NC)[64:128, :, 64:128],
                        in_=mn_e[64:128, 1::2, :]), inc=f"{tag}_bd", n=16)
                    bd_done = P.cur(f"{tag}_bd")

                    pv2 = dict(ve_q=0, pe_wo=0, ve_ro=0)
                    rt = r_t.ap()[:, 0:NC * TT]
                    s2rl_base = P.cur("f_rl0")
                    for tt in range(NTILE):
                        if dbg:
                            P.wait("vector", f"{tag}_dbg", P.cur(f"{tag}_dbg"))
                        P.wait("sync", "f_rs")
                        P.wait("sync", f"{tag}_ve_lnb", P.cur(f"{tag}_ve_lnb"))
                        P.do("sync",
                             lambda e, tt=tt, rt=rt: e.dma_start(
                                 out=rt.rearrange("p (c t) -> p c t", c=NC), in_=r_view_tile(tt)),
                             inc="f_rl0", n=16)
                        P.wait("tensor", f"{tag}_wl", wl2q)
                        P.wait("tensor", f"{tag}_ve_ro", pv2["ve_ro"])
                        P.wait("tensor", f"{tag}_ve_o", P.cur(f"{tag}_ve_o"))
                        for oc in range(NC):
                            bk = pb[oc % 2]
                            P.wait("tensor", f"{tag}_ve_q",
                                   max(0, P.cur(f"{tag}_ve_q") - 1))
                            for c in range(NC):
                                P.do("tensor",
                                     lambda e, oc=oc, c=c, tt=tt, bk=bk: e.matmul(
                                         bk.ap()[:, :],
                                         wq_i.ap()[:, c * D + oc * 128:c * D + oc * 128 + 128],
                                         lnb_all.ap()[:, c * T + tt * TT:c * T + tt * TT + TT],
                                         start=(c == 0), stop=(c == NC - 1)),
                                     inc=f"{tag}_pe_q" if c == NC - 1 else None)
                            peq = P.cur(f"{tag}_pe_q")
                            P.wait("vector", f"{tag}_pe_q", peq)
                            P.do("vector",
                                 lambda e, oc=oc, bk=bk: e.tensor_copy(
                                     q_fm.ap()[:, oc * TT:(oc + 1) * TT], bk.ap()[:, :]),
                                 inc=f"{tag}_ve_q")
                        veq = P.cur(f"{tag}_ve_q")
                        P.drain("vector")
                        P.wait("vector", f"{tag}_pe_ss", P.cur(f"{tag}_pe_ss"))
                        for c in range(NC):
                            P.do("vector",
                                 lambda e, c=c: e.tensor_mul(
                                     asc.ap()[:, c * TT:(c + 1) * TT],
                                     q_fm.ap()[:, c * TT:(c + 1) * TT],
                                     q_fm.ap()[:, c * TT:(c + 1) * TT]),
                                 inc=f"{tag}_ve_q2" if c == NC - 1 else None)
                        q2_done = P.cur(f"{tag}_ve_q2")
                        P.drain("vector")
                        P.wait("tensor", f"{tag}_ve_q2", q2_done)
                        P.wait("tensor", f"{tag}_sc_qn", P.cur(f"{tag}_sc_qn"))
                        for c in range(NC):
                            P.do("tensor",
                                 lambda e, c=c: e.matmul(
                                     pb[2].ap()[0:16, :],
                                     cbf.ap()[:, CB_BLK16 + c * 16:CB_BLK16 + (c + 1) * 16],
                                     asc.ap()[:, c * TT:(c + 1) * TT],
                                     start=(c == 0), stop=(c == NC - 1)),
                                 inc=f"{tag}_pe_ss" if c == NC - 1 else None)
                        ss_done = P.cur(f"{tag}_pe_ss")
                        P.wait("vector", f"{tag}_pe_ss", ss_done)
                        P.do("vector", lambda e: e.tensor_scalar_max(qnr.ap()[:, :],
                                                                     pb[2].ap()[0:16, :], 1e-18),
                             inc=f"{tag}_ve_ssm")
                        ssm_done = P.cur(f"{tag}_ve_ssm")
                        P.drain("vector")
                        P.wait("scalar", f"{tag}_ve_ssm", ssm_done)
                        P.do("scalar", lambda e: e.activation(qnr.ap()[:, :], qnr.ap()[:, :], Act.Ln))
                        P.drain("scalar")
                        P.do("scalar", lambda e: e.activation(qnr.ap()[:, :], qnr.ap()[:, :],
                                                              Act.Exp, scale=-0.5), inc=f"{tag}_sc_qn")
                        qn_done = P.cur(f"{tag}_sc_qn")
                        P.wait("vector", f"{tag}_sc_qn", qn_done)
                        P.do("vector", lambda e, tt=tt: e.tensor_scalar(
                            fwr.ap()[:, :], prows.ap()[:, tt * TT:(tt + 1) * TT],
                            0.3, 0.5, Alu.mult, Alu.add, accum_out=None))
                        P.drain("vector")
                        P.do("vector", lambda e: e.tensor_mul(rtile.ap()[0:16, :], fwr.ap()[:, :],
                                                              qnr.ap()[:, :]))
                        P.drain("vector")
                        P.do("vector", lambda e: e.tensor_scalar(
                            fwr.ap()[:, :], fwr.ap()[:, :], -1.0, 1.0,
                            Alu.mult, Alu.add, accum_out=None))
                        P.drain("vector")
                        P.do("vector", lambda e: e.tensor_mul(rtile.ap()[32:48, :], fwr.ap()[:, :],
                                                              qnr.ap()[:, :]), inc=f"{tag}_ve_g")
                        g_done = P.cur(f"{tag}_ve_g")
                        P.drain("vector")

                        for blk in range(NBLK):
                            P.wait("tensor", f"{tag}_ve_g", g_done)
                            P.wait("tensor", f"{tag}_ve_gt", P.cur(f"{tag}_ve_gt"))
                            P.wait("tensor", f"{tag}_ve_of", P.cur(f"{tag}_ve_of"))
                            P.do("tensor", lambda e, blk=blk: e.transpose(
                                pb[3].ap()[:, 0:64], rtile.ap()[0:64, blk * 128:(blk + 1) * 128],
                                ident[0:64, 0:64]), inc=f"{tag}_pe_gt")
                            pegt = P.cur(f"{tag}_pe_gt")
                            P.wait("vector", f"{tag}_pe_gt", pegt)
                            P.do("vector", lambda e: e.tensor_copy(gT[:], pb[3].ap()[:, 0:64]),
                                 inc=f"{tag}_ve_gt")
                            P.drain("vector")
                            P.wait("tensor", f"{tag}_ve_gt", P.cur(f"{tag}_ve_gt"))
                            P.wait("tensor", f"{tag}_ve_o", P.cur(f"{tag}_ve_o"))
                            P.wait("tensor", f"{tag}_bd", bd_done)
                            for c in range(NC):
                                qs = q_fm.ap()[:, c * TT + blk * 128:c * TT + blk * 128 + 128]
                                P.do("tensor",
                                     lambda e, c=c, qs=qs: e.matmul(
                                         pb[4 + c // 4].ap()[:, (c % 4) * 128:(c % 4) * 128 + 128],
                                         qs, bdf.ap()[:, c * 128:(c + 1) * 128],
                                         start=True, stop=True))
                                P.do("tensor",
                                     lambda e, c=c, qs=qs: e.matmul(
                                         pb[6 + c // 4].ap()[:, (c % 4) * 128:(c % 4) * 128 + 128],
                                         qs, bds.ap()[:, c * 128:(c + 1) * 128],
                                         start=True, stop=True),
                                     inc=f"{tag}_pe_bd" if c == NC - 1 else None)
                            bd_mm = P.cur(f"{tag}_pe_bd")
                            P.wait("vector", f"{tag}_pe_bd", bd_mm)
                            for half in range(2):
                                P.do("vector", lambda e, half=half: e.tensor_tensor(
                                    o32.ap()[:, half * 512:(half + 1) * 512]
                                       .rearrange("p (h d) -> p h d", d=DK),
                                    pb[4 + half].ap()[:, :].rearrange("p (h d) -> p h d", d=DK),
                                    gT.ap()[:, half * 8:half * 8 + 8].unsqueeze(2)
                                      .broadcast_to((128, 8, DK)), Alu.mult))
                                P.do("vector", lambda e, half=half: e.tensor_tensor(
                                    pb[half].ap()[:, :].rearrange("p (h d) -> p h d", d=DK),
                                    pb[6 + half].ap()[:, :].rearrange("p (h d) -> p h d", d=DK),
                                    gT.ap()[:, 32 + half * 8:32 + half * 8 + 8].unsqueeze(2)
                                      .broadcast_to((128, 8, DK)), Alu.mult))
                            P.drain("vector")
                            for half in range(2):
                                P.do("vector", lambda e, half=half: e.tensor_add(
                                    o32.ap()[:, half * 512:(half + 1) * 512],
                                    o32.ap()[:, half * 512:(half + 1) * 512],
                                    pb[half].ap()[:, :]),
                                    inc=f"{tag}_ve_o" if half == 1 else None)
                            veo = P.cur(f"{tag}_ve_o")
                            P.drain("vector")
                            P.wait("tensor", f"{tag}_ve_o", veo)
                            P.wait("tensor", f"{tag}_ve_of", P.cur(f"{tag}_ve_of"))
                            for c in range(NC):
                                P.do("tensor",
                                     lambda e, c=c: e.transpose(
                                         pb[2 + c // 4].ap()[:, (c % 4) * 128:(c % 4) * 128 + 128],
                                         o32.ap()[:, c * 128:(c + 1) * 128], ident),
                                     inc=f"{tag}_pe_ot" if c == NC - 1 else None)
                            peot = P.cur(f"{tag}_pe_ot")
                            P.wait("vector", f"{tag}_pe_ot", peot)
                            for c in range(NC):
                                P.do("vector",
                                     lambda e, c=c, blk=blk: e.tensor_copy(
                                         asc.ap()[:, c * TT + blk * 128:c * TT + blk * 128 + 128],
                                         pb[2 + c // 4].ap()[:, (c % 4) * 128:(c % 4) * 128 + 128]),
                                     inc=f"{tag}_ve_of" if c == NC - 1 else None)
                            P.drain("vector")

                        of_all = P.cur(f"{tag}_ve_of")
                        if dbg and tt == 0:
                            P.wait("gpsimd", f"{tag}_ve_q", P.cur(f"{tag}_ve_q"))
                            P.do("gpsimd", lambda e: e.dma_start(out=d_qfm, in_=q_fm[:]),
                                 inc=f"{tag}_dbg", n=16)
                            P.wait("gpsimd", f"{tag}_ve_of", of_all)
                            P.do("gpsimd", lambda e: e.dma_start(out=d_ofm, in_=asc[:]),
                                 inc=f"{tag}_dbg", n=16)
                        P.wait("tensor", f"{tag}_wl", wl2)
                        P.wait("tensor", f"{tag}_ve_of", of_all)
                        for oc in range(NC):
                            bk = pb[oc % 2]
                            P.wait("tensor", f"{tag}_ve_ro", max(0, P.cur(f"{tag}_ve_ro") - 1))
                            for c in range(NC):
                                P.do("tensor",
                                     lambda e, oc=oc, c=c, bk=bk: e.matmul(
                                         bk.ap()[:, :],
                                         wo_i.ap()[:, c * D + oc * 128:c * D + oc * 128 + 128],
                                         asc.ap()[:, c * TT:(c + 1) * TT],
                                         start=(c == 0), stop=(c == NC - 1)),
                                     inc=f"{tag}_pe_wo" if c == NC - 1 else None)
                            pewo = P.cur(f"{tag}_pe_wo")
                            P.wait("vector", f"{tag}_pe_wo", pewo)
                            P.wait("vector", "f_rl0", s2rl_base + 16 * (tt + 1))
                            P.do("vector",
                                 lambda e, oc=oc, bk=bk, rt=rt: e.tensor_add(
                                     rt[:, oc * TT:(oc + 1) * TT], rt[:, oc * TT:(oc + 1) * TT],
                                     bk.ap()[:, :]),
                                 inc=f"{tag}_ve_ro")
                            P.drain("vector")
                        pv2["ve_ro"] = P.cur(f"{tag}_ve_ro")
                        P.wait("scalar", f"{tag}_ve_ro", pv2["ve_ro"])
                        P.do("scalar",
                             lambda e, tt=tt, rt=rt: e.dma_start(
                                 out=r_view_tile(tt), in_=rt.rearrange("p (c t) -> p c t", c=NC)),
                             inc="f_rs", n=16)

                P.wait("sync", "f_rs")

        if "p0" in enable:
            transpose_pipe(
                "p0", pb[0:4],
                lambda e, blk, buf: e.dma_start(out=buf, in_=x_in[blk * 128:(blk + 1) * 128, :]),
                lambda e, blk, buf: e.dma_start(out=r_view_blk(blk),
                                                in_=buf.rearrange("p (c t) -> p c t", c=NC)),
            )
            P.fence()

        if "ffn1" in enable:
            ffn_phase("f1", first=True)
            P.fence()

        if "attn" in enable:
            attn_phase()
            P.fence()

        if "ffn2" in enable:
            ffn_phase("f2")
            P.fence()

        if "p4" in enable:
            transpose_pipe(
                "p4", pb[0:4],
                lambda e, blk, buf: e.dma_start(out=buf.rearrange("p (c t) -> p c t", c=NC),
                                                in_=r_view_blk(blk)),
                lambda e, blk, buf: e.dma_start(out=y_out[blk * 128:(blk + 1) * 128, :], in_=buf),
            )

        P.wait("sync", "tp_st0")
        P.wait("sync", "tp_st1")

        block = es.enter_context(nc.Block())
        P.emit(block)

    return nc



def _prep_inputs(L0_attn_x, L0_attn_Wq, L0_attn_Wk, L0_attn_Wv, L0_attn_Wb,
                 L0_attn_Wo, L0_attn_Wfd, L0_attn_bfd, L0_attn_Wsd, L0_attn_bsd,
                 L0_attn_Wtf1, L0_attn_btf1, L0_attn_Wtf2, L0_attn_btf2,
                 L0_ffn_Ws1, L0_ffn_Ws2, L0_ffn_W1, L0_ffn_W2, L0_ffn_Wr):
    f32 = np.float32

    def img_kc(W):
        K, M = W.shape
        return np.ascontiguousarray(
            W.reshape(K // 128, 128, M).transpose(1, 0, 2).reshape(128, (K // 128) * M)
        ).astype(f32)

    wq = img_kc(np.asarray(L0_attn_Wq))
    wk = img_kc(np.asarray(L0_attn_Wk))
    wv = img_kc(np.asarray(L0_attn_Wv))
    wo = img_kc(np.asarray(L0_attn_Wo))
    wbf = img_kc(np.asarray(L0_attn_Wb))
    wfdf = img_kc(np.asarray(L0_attn_Wfd))
    wsdf = img_kc(np.asarray(L0_attn_Wsd))
    wrf = img_kc(np.asarray(L0_ffn_Wr))

    W1 = np.asarray(L0_ffn_W1)
    Ws1 = np.asarray(L0_ffn_Ws1)
    w1all = np.concatenate([W1, Ws1[None]], axis=0)
    w1f = np.ascontiguousarray(
        w1all.reshape(9, NC, 128, ED).transpose(2, 0, 1, 3).reshape(128, 9 * NC * ED)
    ).astype(f32)

    W2 = np.asarray(L0_ffn_W2)
    Ws2 = np.asarray(L0_ffn_Ws2)
    w2all = np.concatenate([W2, Ws2[None]], axis=0)
    w2f = np.ascontiguousarray(
        w2all.reshape(9, 4, 128, D).transpose(2, 0, 1, 3).reshape(128, 9 * 4 * D)
    ).astype(f32)

    Wtf1 = np.asarray(L0_attn_Wtf1).astype(f32)
    wtf1f = np.concatenate([Wtf1, Wtf1], axis=0)

    Wtf2 = np.asarray(L0_attn_Wtf2).astype(f32).reshape(32)
    bd = np.zeros((512, 16), f32)
    for h in range(16):
        bd[h * 32:(h + 1) * 32, h] = Wtf2
    wtf2f = np.ascontiguousarray(
        bd.reshape(4, 128, 16).transpose(1, 0, 2).reshape(128, 64)
    )

    consts = np.zeros((128, CW), f32)
    consts[:, C_ID:C_ID + 128] = np.eye(128, dtype=f32)
    consts[:, C_ONES:C_ONES + 128] = 1.0
    consts[:, C_BFD] = np.tile(np.asarray(L0_attn_bfd).astype(f32), 8)
    consts[:, C_BSD] = np.tile(np.asarray(L0_attn_bsd).astype(f32), 8)
    consts[:, C_BTF1] = np.tile(np.asarray(L0_attn_btf1).astype(f32), 4)
    consts[:, C_BTF2] = float(np.asarray(L0_attn_btf2).reshape(-1)[0])
    consts[:, C_EPSLN] = EPS_LN
    consts[:, C_ZERO] = 0.0
    consts[:, C_EPS6] = 1e-6
    consts[:, C_EPS18] = 1e-18

    cbf = np.zeros((128, CBW), ml_dtypes.bfloat16)
    cbf[:, CB_ONESCOL] = 1.0
    cbf[0:64, CB_BLK] = 1.0
    cbf[64:128, CB_BLK + 1] = 1.0
    cbf[:, CB_ONESROW:CB_ONESROW + 128] = 1.0
    for e in range(NE):
        cbf[e, CB_SEL8 + e * 128:CB_SEL8 + (e + 1) * 128] = 1.0
    for c in range(NC):
        cbf[0:64, CB_BLK16 + c * 16 + 2 * c] = 1.0
        cbf[64:128, CB_BLK16 + c * 16 + 2 * c + 1] = 1.0

    x = np.asarray(L0_attn_x).astype(f32)
    weights = dict(wq=wq, wk=wk, wv=wv, wo=wo, w1f=w1f, w2f=w2f, wrf=wrf,
                   wbf=wbf, wfdf=wfdf, wsdf=wsdf, wtf1f=wtf1f, wtf2f=wtf2f,
                   consts=consts, cbf=cbf)
    shards = []
    for c in range(NCORE):
        b, h = c // 2, c % 2
        shard = np.ascontiguousarray(x[b, h * T:(h + 1) * T, :])
        m = dict(weights)
        m["x"] = shard
        shards.append(m)
    return shards


def _run(inputs, enable=("p0", "ffn1", "attn", "ffn2", "p4")):
    in_maps = _prep_inputs(**inputs)
    nc = build_program(enable)
    res = run_bass_kernel_spmd(nc, in_maps, core_ids=list(range(NCORE)))
    out = np.empty((B, L, D), np.float32)
    for c in range(NCORE):
        b, h = c // 2, c % 2
        out[b, h * T:(h + 1) * T, :] = res.results[c]["y"]
    return out


def kernel(**inputs):
    return _run(inputs)


# version 06
# speedup vs baseline: 1.0759x; 1.0759x over previous
import numpy as np
import ml_dtypes
import concourse.bass as bass
import concourse.mybir as mybir
from concourse.bass_utils import run_bass_kernel_spmd

F32 = mybir.dt.float32
BF16 = mybir.dt.bfloat16
Act = mybir.ActivationFunctionType
Alu = mybir.AluOpType

B, L, D = 4, 8192, 1024
H, DK = 16, 64
ED, NE = 512, 8
NCORE = 8
T = 4096
TT = 512
NTILE = T // TT
NBLK = TT // 128
NC = 8
EPS_LN = 1e-5
W2_SCRATCH = False

ENGS = ("sync", "gpsimd", "tensor", "vector", "scalar")

C_ID = 0
C_ONES = 128
C_BFD = 256
C_BSD = 257
C_BTF1 = 258
C_BTF2 = 259
C_EPSLN = 260
C_ZERO = 261
C_EPS6 = 262
C_EPS18 = 263
CW = 288

CB_ONESCOL = 0
CB_BLK = 8
CB_ONESROW = 24
CB_SEL8 = 152
CB_BLK16 = 152 + 8 * 128
CBW = CB_BLK16 + 8 * 16


class Prog:

    def __init__(self, nc, es):
        self.nc = nc
        self.es = es
        self.ops = {e: [] for e in ENGS}
        self.sems = {}
        self.cnt = {}
        self.dma_sems = set()
        self._hw = {e: {} for e in ENGS}

    def sem(self, name):
        if name not in self.sems:
            self.sems[name] = self.es.enter_context(self.nc.semaphore(name))
            self.cnt[name] = 0
        return name

    def cur(self, name):
        self.sem(name)
        return self.cnt[name]

    def wait(self, eng, sem, val=None):
        self.sem(sem)
        v = self.cnt[sem] if val is None else val
        if v <= 0:
            return
        if self._hw[eng].get(sem, 0) >= v:
            return
        self._hw[eng][sem] = v
        self.ops[eng].append(["w", sem, v])

    def drain(self, eng):
        ops = self.ops[eng]
        if ops and ops[-1][0] == "d":
            return
        ops.append(["d"])

    def do(self, eng, fn, inc=None, n=1):
        if inc is not None:
            self.sem(inc)
            self.cnt[inc] += n
            if n == 16:
                self.dma_sems.add(inc)
        self.ops[eng].append(["o", fn, inc, n, self.cnt[inc] if inc else 0])
        return self.cnt[inc] if inc is not None else None

    def fence(self, tag=""):
        rel = {}
        for e in ENGS:
            for it in reversed(self.ops[e]):
                if it[0] == "o":
                    if it[2] is None:
                        self.sem(f"fence_{e}")
                        self.cnt[f"fence_{e}"] += 1
                        it[2] = f"fence_{e}"
                        it[3] = 1
                        it[4] = self.cnt[f"fence_{e}"]
                    rel[e] = (it[2], it[4])
                    break
        for e in ENGS:
            for o in ENGS:
                if o != e and o in rel:
                    self.wait(e, rel[o][0], rel[o][1])
            for s in sorted(self.dma_sems):
                self.wait(e, s)

    def emit(self, block):
        for eng_name in ENGS:
            ops = self.ops[eng_name]

            def make(ops=ops):
                def section(e):
                    for item in ops:
                        k = item[0]
                        if k == "w":
                            e.wait_ge(self.sems[item[1]], item[2])
                        elif k == "d":
                            e.drain()
                        else:
                            inst = item[1](e)
                            if item[2] is not None:
                                inst.then_inc(self.sems[item[2]], item[3])
                return section

            getattr(block, eng_name)(make())


def build_program(enable=("p0", "ffn1", "attn", "ffn2", "p4"), n_pairs=4):
    nc = bass.Bass("TRN2", target_bir_lowering=False)

    x_in = nc.dram_tensor("x", [T, D], F32, kind="ExternalInput").ap()
    y_out = nc.dram_tensor("y", [T, D], F32, kind="ExternalOutput").ap()

    wimg = {}
    for name in ("wq", "wk", "wv", "wo"):
        wimg[name] = nc.dram_tensor(name, [128, NC * D], F32, kind="ExternalInput").ap()
    w1f = nc.dram_tensor("w1f", [128, 9 * NC * ED], F32, kind="ExternalInput").ap()
    w2f = nc.dram_tensor("w2f", [128, 9 * 4 * D], F32, kind="ExternalInput").ap()
    wrf = nc.dram_tensor("wrf", [128, NC * NE], F32, kind="ExternalInput").ap()
    wbf = nc.dram_tensor("wbf", [128, NC * H], F32, kind="ExternalInput").ap()
    wfdf = nc.dram_tensor("wfdf", [128, NC * H], F32, kind="ExternalInput").ap()
    wsdf = nc.dram_tensor("wsdf", [128, NC * H], F32, kind="ExternalInput").ap()
    wtf1f = nc.dram_tensor("wtf1f", [128, 32], F32, kind="ExternalInput").ap()
    wtf2f = nc.dram_tensor("wtf2f", [128, 4 * H], F32, kind="ExternalInput").ap()
    consts = nc.dram_tensor("consts", [128, CW], F32, kind="ExternalInput").ap()
    cbf_in = nc.dram_tensor("cbf", [128, CBW], BF16, kind="ExternalInput").ap()

    r_sc = nc.dram_tensor("r_sc", [128, NC * T], F32)
    w2bf = nc.dram_tensor("w2bf", [128, 9 * 4 * D], BF16)

    cc_in = nc.dram_tensor("cc_in", [128, 1028], F32)
    cc_out = nc.dram_tensor("cc_out", [256, 1028], F32)

    dbg = "dbg" in enable
    if dbg:
        d_flux = nc.dram_tensor("d_flux", [16, T], F32, kind="ExternalOutput").ap()
        d_facc = nc.dram_tensor("d_facc", [16, 8], F32, kind="ExternalOutput").ap()
        d_accM = nc.dram_tensor("d_accM", [128, 1024], F32, kind="ExternalOutput").ap()
        d_mn = nc.dram_tensor("d_mn", [128, 1024], F32, kind="ExternalOutput").ap()
        d_qfm = nc.dram_tensor("d_qfm", [128, NC * TT], F32, kind="ExternalOutput").ap()
        d_ofm = nc.dram_tensor("d_ofm", [128, NC * TT], F32, kind="ExternalOutput").ap()
        d_mb = nc.dram_tensor("d_mb", [128, 1024], F32, kind="ExternalOutput").ap()
        d_rab = nc.dram_tensor("d_rab", [16, 8], F32, kind="ExternalOutput").ap()
        d_dcol = nc.dram_tensor("d_dcol", [16, 4], F32, kind="ExternalOutput").ap()
        d_ebc = nc.dram_tensor("d_ebc", [128, 16], F32, kind="ExternalOutput").ap()
        d_dbc = nc.dram_tensor("d_dbc", [128, 16], F32, kind="ExternalOutput").ap()

    from contextlib import ExitStack
    with ExitStack() as es:
        def sb(name, shape, dt):
            return es.enter_context(nc.sbuf_tensor(name, shape, dt))

        P = Prog(nc, es)

        cst = sb("cst", [128, CW], F32)
        cbf = sb("cbfs", [128, CBW], BF16)
        ident = cst.ap()[:, C_ID:C_ID + 128]
        ones_col_f32 = cst.ap()[:, C_ONES:C_ONES + 1]
        ones_row_f32 = cst.ap()[0:1, C_ONES:C_ONES + 128]
        ones_row_bf = cbf.ap()[0:1, CB_ONESROW:CB_ONESROW + 128]
        pb = [es.enter_context(nc.psum_tensor(f"bank{i}", [128, 512], F32)) for i in range(8)]

        P.do("sync", lambda e: e.dma_start(out=cst[:], in_=consts), inc="cl", n=16)
        P.do("sync", lambda e: e.dma_start(out=cbf[:], in_=cbf_in), inc="cl", n=16)

        def r_view_blk(blk):
            return r_sc.ap().rearrange("p (c t) -> p c t", c=NC)[:, :, blk * 128:(blk + 1) * 128]

        def r_view_tile(tt):
            return r_sc.ap().rearrange("p (c t) -> p c t", c=NC)[:, :, tt * TT:(tt + 1) * TT]

        from contextlib import ExitStack as _ES0

        def transpose_pipe(tag, banks, src_dma, dst_dma):
            sbtag, tag = tag, "tp"
            with _ES0() as tpes:
                a_buf = tpes.enter_context(nc.sbuf_tensor(f"{sbtag}_a", [128, 2 * D], F32))
                b_buf = tpes.enter_context(nc.sbuf_tensor(f"{sbtag}_b", [128, 2 * D], F32))
                for blk in range(T // 128):
                    par = blk % 2
                    asl = a_buf.ap()[:, par * D:(par + 1) * D]
                    bsl = b_buf.ap()[:, par * D:(par + 1) * D]
                    P.wait("sync", f"{tag}_pe{par}")
                    P.do("sync", lambda e, blk=blk, asl=asl: src_dma(e, blk, asl),
                         inc=f"{tag}_ld{par}", n=16)
                    P.wait("tensor", f"{tag}_ld{par}")
                    P.wait("tensor", "cl", 32)
                    P.wait("tensor", f"{tag}_ve{par}")
                    for c in range(NC):
                        pt = banks[par * 2 + c // 4].ap()[:, (c % 4) * 128:(c % 4) * 128 + 128]
                        P.do("tensor",
                             lambda e, pt=pt, asl=asl, c=c: e.transpose(pt, asl[:, c * 128:(c + 1) * 128], ident),
                             inc=f"{tag}_pe{par}")
                    P.wait("vector", f"{tag}_pe{par}")
                    P.wait("vector", f"{tag}_st{par}")
                    for c in range(NC):
                        pt = banks[par * 2 + c // 4].ap()[:, (c % 4) * 128:(c % 4) * 128 + 128]
                        P.do("vector",
                             lambda e, pt=pt, bsl=bsl, c=c: e.tensor_copy(bsl[:, c * 128:(c + 1) * 128], pt),
                             inc=f"{tag}_ve{par}" if c == NC - 1 else None)
                    P.wait("scalar", f"{tag}_ve{par}")
                    P.do("scalar", lambda e, blk=blk, bsl=bsl: dst_dma(e, blk, bsl),
                         inc=f"{tag}_st{par}", n=16)
                P.wait("sync", f"{tag}_st0")
                P.wait("sync", f"{tag}_st1")

        from contextlib import ExitStack as _ES

        def ffn_phase(tag, first=False):
            pes = _ES()
            with pes:
                def psb(name, shape, dt):
                    return pes.enter_context(nc.sbuf_tensor(name, shape, dt))
                sbtag, tag = tag, "f"

                w1i = psb(f"{sbtag}_w1i", [128, 9 * NC * ED], BF16)
                wri = psb(f"{sbtag}_wri", [128, NC * NE], BF16)
                w2b = psb(f"{sbtag}_w2b", [128, 2 * 4 * D], BF16)
                r_t = psb(f"{sbtag}_rt", [128, 2 * NC * TT], F32)
                lnb = psb(f"{sbtag}_lnb", [128, NC * TT], BF16)
                probb = psb(f"{sbtag}_probb", [128, NE * TT], BF16)
                h_sb = psb(f"{sbtag}_h", [128, 9 * 4 * TT], BF16)
                tmpa = psb(f"{sbtag}_tmpa", [128, 4 * TT], BF16)
                invs = psb(f"{sbtag}_invs", [128, 2 * TT], F32)
                ltm = psb(f"{sbtag}_ltm", [128, 64], F32)
                prow = psb(f"{sbtag}_prow", [8, TT], BF16)
                tmpo = psb(f"{sbtag}_tmpo", [128, TT], F32)
                mu_row = psb(f"{sbtag}_mu", [1, TT], F32)
                ex2_row = psb(f"{sbtag}_ex2", [1, TT], F32)
                inv_row = psb(f"{sbtag}_inv", [1, TT], F32)
                m1v = psb(f"{sbtag}_m1", [128, 8], F32)

                rb = lnb.ap()[:, :]
                sq = probb.ap()[:, :]
                invb_sb = invs.ap()[:, 0:TT]
                inv2h_sb = invs.ap()[:, TT:2 * TT]

                wl_base0 = P.cur(f"{tag}_wl")
                P.do("gpsimd", lambda e: e.dma_start(out=wri[:], in_=wrf), inc=f"{tag}_wl", n=16)
                w1_order = [8, 0, 1, 2, 3, 4, 5, 6, 7]
                wl_at = {}
                for pos, ee in enumerate(w1_order):
                    P.wait("gpsimd", f"{tag}_wl")
                    P.do("gpsimd",
                         lambda e, ee=ee: e.dma_start(
                             out=w1i.ap()[:, ee * NC * ED:(ee + 1) * NC * ED],
                             in_=w1f[:, ee * NC * ED:(ee + 1) * NC * ED]),
                         inc=f"{tag}_wl", n=16)
                    wl_at[ee] = P.cur(f"{tag}_wl")
                wl_done = P.cur(f"{tag}_wl")
                wl_router = wl_base0 + 16

                w2cnt = [0]
                grel = []
                w2l_base = [P.cur(f"{tag}_w2l0"), P.cur(f"{tag}_w2l1")]
                w2ls_base = [P.cur(f"{tag}_w2ls0"), P.cur(f"{tag}_w2ls1")]
                w2_src = {}
                w2w_base = P.cur(f"{tag}_w2w")

                def w2_load():
                    g_idx = w2cnt[0]
                    par = g_idx % 2
                    e_idx = g_idx % 9
                    use_fp32 = (not W2_SCRATCH) or (first and g_idx < 9)
                    eng = "gpsimd" if use_fp32 else "sync"
                    if g_idx >= 2:
                        s, v = grel[g_idx - 2]
                        P.wait(eng, s, v)
                        if W2_SCRATCH and first and g_idx - 2 < 9:
                            P.wait(eng, f"{tag}_w2w", w2w_base + 16 * (g_idx - 1))
                    if use_fp32:
                        P.do("gpsimd",
                             lambda e, par=par, e_idx=e_idx: e.dma_start(
                                 out=w2b.ap()[:, par * 4 * D:(par + 1) * 4 * D],
                                 in_=w2f[:, e_idx * 4 * D:(e_idx + 1) * 4 * D]),
                             inc=f"{tag}_w2l{par}", n=16)
                        w2_src[g_idx] = (f"{tag}_w2l{par}", P.cur(f"{tag}_w2l{par}"))
                        if W2_SCRATCH and first:
                            P.wait("scalar", f"{tag}_w2w", P.cur(f"{tag}_w2w"))
                            P.wait("scalar", f"{tag}_w2l{par}", P.cur(f"{tag}_w2l{par}"))
                            P.do("scalar",
                                 lambda e, par=par, e_idx=e_idx: e.dma_start(
                                     out=w2bf[:, e_idx * 4 * D:(e_idx + 1) * 4 * D],
                                     in_=w2b.ap()[:, par * 4 * D:(par + 1) * 4 * D]),
                                 inc=f"{tag}_w2w", n=16)
                    else:
                        if first:
                            P.wait("sync", f"{tag}_w2w", w2w_base + 16 * 9)
                        P.do("sync",
                             lambda e, par=par, e_idx=e_idx: e.dma_start(
                                 out=w2b.ap()[:, par * 4 * D:(par + 1) * 4 * D],
                                 in_=w2bf[:, e_idx * 4 * D:(e_idx + 1) * 4 * D]),
                             inc=f"{tag}_w2ls{par}", n=16)
                        w2_src[g_idx] = (f"{tag}_w2ls{par}", P.cur(f"{tag}_w2ls{par}"))
                    w2cnt[0] += 1

                w2_load()
                w2_load()
                prev_veout = 0
                prev_veout_b = 0
                prev_peout = 0
                e_order = [8, 0, 1, 2, 3, 4, 5, 6, 7]
                tile_ctx = {}

                def h_stage(e_i, idx):
                    bks = pb[0:4] if idx % 2 == 0 else pb[4:8]
                    P.wait("tensor", f"{tag}_wl", tile_ctx["wl_at"][e_i])
                    P.wait("tensor", f"{tag}_sc_relu", max(0, P.cur(f"{tag}_sc_relu") - 1))
                    if idx % 2 == 0:
                        P.wait("tensor", f"{tag}_ve_var", tile_ctx["var_done"])
                        if idx != 0:
                            P.wait("tensor", f"{tag}_ve_lnb", tile_ctx["lnb_done"])
                    else:
                        P.wait("tensor", f"{tag}_ve_prow", tile_ctx["prow_done"])
                        P.wait("tensor", f"{tag}_b5free", P.cur(f"{tag}_b5free"))
                        P.wait("tensor", f"{tag}_b67free0", P.cur(f"{tag}_b67free0"))
                        P.wait("tensor", f"{tag}_b67free1", P.cur(f"{tag}_b67free1"))
                    lnb = tile_ctx["lnb"]
                    w1i = tile_ctx["w1i"]
                    tmpa = tile_ctx["tmpa"]
                    probb = tile_ctx["probb"]
                    h_sb = tile_ctx["h_sb"]
                    for hc in range(4):
                        for c in range(NC):
                            col = (e_i * NC + c) * ED + hc * 128
                            if idx == 0 and hc == 0:
                                P.wait("tensor", f"{tag}_ve_lnb", tile_ctx["lnb_base"] + c + 1)
                            P.do("tensor",
                                 lambda e, hc=hc, col=col, bks=bks, c=c: e.matmul(
                                     bks[hc].ap()[:, :],
                                     w1i.ap()[:, col:col + 128],
                                     lnb.ap()[:, c * TT:(c + 1) * TT],
                                     start=(c == 0), stop=(c == NC - 1)),
                                 inc=f"{tag}_pe_h" if (hc == 3 and c == NC - 1) else None)
                    peh_done = P.cur(f"{tag}_pe_h")
                    P.wait("scalar", f"{tag}_pe_h", peh_done)
                    P.wait("scalar", f"{tag}_ve_h", P.cur(f"{tag}_ve_h"))
                    for hc in range(4):
                        P.do("scalar",
                             lambda e, hc=hc, bks=bks: e.activation(
                                 tmpa.ap()[:, hc * TT:(hc + 1) * TT], bks[hc].ap()[:, :], Act.Relu),
                             inc=f"{tag}_sc_relu" if hc == 3 else None)
                    relu_done = P.cur(f"{tag}_sc_relu")
                    P.wait("vector", f"{tag}_sc_relu", relu_done)
                    hsl = h_sb.ap()[:, e_i * 4 * TT:(e_i + 1) * 4 * TT]
                    if e_i < NE:
                        pbb = probb.ap()[:, e_i * TT:(e_i + 1) * TT].unsqueeze(1) \
                                   .broadcast_to((128, 4, TT))
                    else:
                        pbb = inv2h_sb.unsqueeze(1).broadcast_to((128, 4, TT))
                    P.do("vector",
                         lambda e, pbb=pbb, hsl=hsl: e.tensor_tensor(
                             hsl.rearrange("p (a t) -> p a t", a=4),
                             tmpa.ap()[:, :].rearrange("p (a t) -> p a t", a=4),
                             pbb, Alu.mult))
                    P.drain("vector")
                    P.do("vector",
                         lambda e, hsl=hsl: e.tensor_mul(hsl, hsl, tmpa.ap()[:, :]),
                         inc=f"{tag}_ve_h")
                    P.drain("vector")

                rs_base = P.cur(f"{tag}_rs")
                rl_base = [P.cur(f"{tag}_rl0"), P.cur(f"{tag}_rl1")]

                def r_load(tt):
                    par = tt % 2
                    rt_ = r_t.ap()[:, par * NC * TT:(par + 1) * NC * TT]
                    P.wait("sync", f"{tag}_rs", rs_base + 16 * max(0, tt - 1))
                    P.do("sync",
                         lambda e, tt=tt, rt_=rt_: e.dma_start(
                             out=rt_.rearrange("p (c t) -> p c t", c=NC), in_=r_view_tile(tt)),
                         inc=f"{tag}_rl{par}", n=16)
                    return rt_

                def s1_rbsq(tt, rt_, peh_guard, veh_guard):
                    par = tt % 2
                    P.wait("vector", f"{tag}_rl{par}", rl_base[par] + 16 * (tt // 2 + 1))
                    P.wait("vector", f"{tag}_pe_h", peh_guard)
                    P.wait("vector", f"{tag}_ve_h", veh_guard)
                    P.do("vector", lambda e, rt_=rt_: e.tensor_copy(rb, rt_))
                    P.drain("vector")
                    P.do("vector", lambda e: e.tensor_mul(sq, rb, rb), inc=f"{tag}_ve_sq")
                    return P.cur(f"{tag}_ve_sq")

                rt_next = r_load(0)
                sq_next = s1_rbsq(0, rt_next, 0, 0)
                for tt in range(NTILE):
                    rt = rt_next
                    sq_done = sq_next

                    P.wait("tensor", "cl", 32)
                    P.wait("tensor", f"{tag}_ve_sq", sq_done)
                    P.wait("tensor", f"{tag}_ve_out", min(prev_veout, prev_veout_b + 2))
                    for c in range(NC):
                        P.do("tensor",
                             lambda e, c=c: e.matmul(pb[0].ap()[0:1, :],
                                                     cbf.ap()[:, CB_ONESCOL:CB_ONESCOL + 1],
                                                     rb[:, c * TT:(c + 1) * TT],
                                                     start=(c == 0), stop=(c == NC - 1)))
                    for c in range(NC):
                        P.do("tensor",
                             lambda e, c=c: e.matmul(pb[1].ap()[0:1, :],
                                                     cbf.ap()[:, CB_ONESCOL:CB_ONESCOL + 1],
                                                     sq[:, c * TT:(c + 1) * TT],
                                                     start=(c == 0), stop=(c == NC - 1)),
                             inc=f"{tag}_pe_st" if c == NC - 1 else None)
                    stats_done = P.cur(f"{tag}_pe_st")

                    P.wait("vector", f"{tag}_pe_st", stats_done)
                    P.do("vector", lambda e: e.tensor_scalar_mul(mu_row[:], pb[0].ap()[0:1, :], 1.0 / D))
                    P.do("vector", lambda e: e.tensor_scalar_mul(ex2_row[:], pb[1].ap()[0:1, :], 1.0 / D))
                    P.drain("vector")
                    P.do("vector", lambda e: e.tensor_mul(tmpo.ap()[0:1, :], mu_row[:], mu_row[:]))
                    P.drain("vector")
                    P.do("vector", lambda e: e.tensor_sub(ex2_row[:], ex2_row[:], tmpo.ap()[0:1, :]),
                         inc=f"{tag}_ve_var")
                    var_done = P.cur(f"{tag}_ve_var")
                    P.wait("scalar", f"{tag}_ve_var", var_done)
                    P.do("scalar", lambda e: e.activation(ex2_row[:], ex2_row[:], Act.Ln,
                                                          bias=cst.ap()[0:1, C_EPSLN:C_EPSLN + 1]))
                    P.drain("scalar")
                    P.do("scalar", lambda e: e.activation(inv_row[:], ex2_row[:], Act.Exp, scale=-0.5),
                         inc=f"{tag}_sc_inv")
                    inv_done = P.cur(f"{tag}_sc_inv")

                    P.wait("tensor", f"{tag}_ve_var", var_done)
                    P.wait("tensor", f"{tag}_sc_inv", inv_done)
                    P.wait("tensor", f"{tag}_ve_out", min(prev_veout, prev_veout_b + 4))
                    P.do("tensor", lambda e: e.matmul(pb[2].ap()[:, :], ones_row_f32, mu_row[:],
                                                      start=True, stop=True))
                    P.do("tensor", lambda e: e.matmul(pb[3].ap()[:, :], ones_row_f32, inv_row[:],
                                                      start=True, stop=True), inc=f"{tag}_pe_bc")
                    bc_done = P.cur(f"{tag}_pe_bc")

                    P.wait("vector", f"{tag}_pe_bc", bc_done)
                    P.wait("vector", f"{tag}_pe_st", stats_done)
                    P.do("vector", lambda e: e.tensor_copy(invb_sb, pb[3].ap()[:, :]))
                    P.drain("vector")
                    P.do("vector", lambda e: e.scalar_tensor_tensor(inv2h_sb, pb[3].ap()[:, :], 0.5,
                                                                    invb_sb, Alu.mult, Alu.mult))
                    lnb_base = P.cur(f"{tag}_ve_lnb")
                    for c in range(NC):
                        P.do("vector",
                             lambda e, c=c, rt=rt: e.tensor_sub(lnb.ap()[:, c * TT:(c + 1) * TT],
                                                                rt[:, c * TT:(c + 1) * TT],
                                                                pb[2].ap()[:, :]),
                             inc=f"{tag}_ve_lnb")
                        P.drain("vector")
                    lnb_done = P.cur(f"{tag}_ve_lnb")
                    tile_ctx.update(wl_done=wl_done, wl_at=wl_at, var_done=var_done,
                                    lnb_done=lnb_done, lnb_base=lnb_base, lnb=lnb, w1i=w1i,
                                    tmpa=tmpa, probb=probb, h_sb=h_sb)

                    P.wait("tensor", f"{tag}_wl", wl_router)
                    P.wait("tensor", f"{tag}_ve_out", min(prev_veout, prev_veout_b + 5))
                    for c in range(NC):
                        P.wait("tensor", f"{tag}_ve_lnb", lnb_base + c + 1)
                        P.do("tensor",
                             lambda e, c=c: e.matmul(pb[4].ap()[0:8, :], wri.ap()[:, c * NE:(c + 1) * NE],
                                                     lnb.ap()[:, c * TT:(c + 1) * TT],
                                                     start=(c == 0), stop=(c == NC - 1)),
                             inc=f"{tag}_pe_lg" if c == NC - 1 else None)
                    lg_done = P.cur(f"{tag}_pe_lg")
                    h_stage(8, 0)
                    P.wait("vector", f"{tag}_pe_lg", lg_done)
                    P.drain("vector")
                    P.do("vector", lambda e: e.tensor_mul(tmpo.ap()[0:8, :], pb[4].ap()[0:8, :],
                                                          invb_sb[0:8, :]),
                         inc=f"{tag}_ve_lg")
                    velg_done = P.cur(f"{tag}_ve_lg")
                    P.wait("tensor", f"{tag}_ve_lg", velg_done)
                    P.wait("tensor", f"{tag}_b5free", P.cur(f"{tag}_b5free"))
                    P.wait("tensor", f"{tag}_ve_out", min(prev_veout, prev_veout_b + 6))
                    for j in range(4):
                        P.do("tensor",
                             lambda e, j=j: e.transpose(pb[5].ap()[:, j * 8:(j + 1) * 8],
                                                        tmpo.ap()[0:8, j * 128:(j + 1) * 128],
                                                        ident[0:8, 0:8]),
                             inc=f"{tag}_pe_lt" if j == 3 else None)
                    lt_done = P.cur(f"{tag}_pe_lt")
                    P.wait("vector", f"{tag}_pe_lt", lt_done)
                    P.do("vector", lambda e: e.tensor_copy(ltm.ap()[:, 0:32], pb[5].ap()[:, 0:32]),
                         inc=f"{tag}_b5free")
                    P.drain("vector")
                    lv = ltm.ap()[:, 0:32].rearrange("p (a b) -> p a b", b=8)
                    mask = ltm.ap()[:, 32:64]
                    P.do("vector", lambda e: e.tensor_reduce(m1v.ap()[:, 0:4], lv,
                                                             mybir.AxisListType.X, Alu.max))
                    P.drain("vector")
                    m1rep = m1v.ap()[:, 0:4].unsqueeze(2).broadcast_to((128, 4, 8))
                    P.do("vector", lambda e: e.tensor_tensor(
                        mask.rearrange("p (a b) -> p a b", b=8), lv, m1rep, Alu.is_equal))
                    P.drain("vector")
                    P.do("vector", lambda e: e.scalar_tensor_tensor(
                        ltm.ap()[:, 0:32], mask, -1e30, ltm.ap()[:, 0:32], Alu.mult, Alu.add))
                    P.drain("vector")
                    P.do("vector", lambda e: e.tensor_reduce(
                        m1v.ap()[:, 4:8], ltm.ap()[:, 0:32].rearrange("p (a b) -> p a b", b=8),
                        mybir.AxisListType.X, Alu.max))
                    P.drain("vector")
                    P.do("vector", lambda e: e.tensor_sub(m1v.ap()[:, 4:8], m1v.ap()[:, 4:8],
                                                          m1v.ap()[:, 0:4]),
                         inc=f"{tag}_ve_m2")
                    m2_done = P.cur(f"{tag}_ve_m2")
                    P.wait("scalar", f"{tag}_ve_m2", m2_done)
                    P.do("scalar", lambda e: e.activation(m1v.ap()[:, 4:8], m1v.ap()[:, 4:8], Act.Exp),
                         inc=f"{tag}_sc_g")
                    g_done = P.cur(f"{tag}_sc_g")
                    P.wait("vector", f"{tag}_sc_g", g_done)
                    P.do("vector", lambda e: e.tensor_scalar_add(m1v.ap()[:, 4:8], m1v.ap()[:, 4:8], 1.0))
                    P.drain("vector")
                    P.do("vector", lambda e: e.reciprocal(m1v.ap()[:, 0:4], m1v.ap()[:, 4:8]))
                    P.drain("vector")
                    P.do("vector", lambda e: e.tensor_single_scalar(m1v.ap()[:, 0:4], m1v.ap()[:, 0:4],
                                                                    1.0, Alu.is_equal))
                    P.drain("vector")
                    is1rep = m1v.ap()[:, 0:4].unsqueeze(2).broadcast_to((128, 4, 8))
                    P.do("vector", lambda e: e.tensor_tensor(
                        ltm.ap()[:, 0:32].rearrange("p (a b) -> p a b", b=8),
                        mask.rearrange("p (a b) -> p a b", b=8), is1rep, Alu.mult),
                        inc=f"{tag}_ve_ptm")
                    ptm_done = P.cur(f"{tag}_ve_ptm")
                    P.wait("tensor", f"{tag}_ve_ptm", ptm_done)
                    for j in range(4):
                        P.do("tensor",
                             lambda e, j=j: e.transpose(pb[4].ap()[0:8, j * 128:(j + 1) * 128],
                                                        ltm.ap()[:, j * 8:(j + 1) * 8], ident),
                             inc=f"{tag}_pe_pt" if j == 3 else None)
                    pt_done = P.cur(f"{tag}_pe_pt")
                    P.wait("vector", f"{tag}_pe_pt", pt_done)
                    P.do("vector", lambda e: e.tensor_copy(prow[:], pb[4].ap()[0:8, 0:TT]),
                         inc=f"{tag}_ve_prow")
                    prow_done = P.cur(f"{tag}_ve_prow")
                    tile_ctx["prow_done"] = prow_done
                    P.wait("tensor", f"{tag}_ve_prow", prow_done)
                    P.wait("tensor", f"{tag}_ve_out", prev_veout)
                    P.wait("vector", f"{tag}_pe_st", stats_done)
                    for e_i in range(NE):
                        bk = pb[6 + e_i % 2]
                        P.wait("tensor", f"{tag}_b67free{e_i % 2}", P.cur(f"{tag}_b67free{e_i % 2}"))
                        sel = cbf.ap()[0:8, CB_SEL8 + e_i * 128:CB_SEL8 + (e_i + 1) * 128]
                        P.do("tensor",
                             lambda e, bk=bk, sel=sel: e.matmul(bk.ap()[:, :], sel, prow.ap()[:, :],
                                                                start=True, stop=True),
                             inc=f"{tag}_pe_pb{e_i % 2}")
                        P.wait("vector", f"{tag}_pe_pb{e_i % 2}", P.cur(f"{tag}_pe_pb{e_i % 2}"))
                        P.do("vector",
                             lambda e, bk=bk, e_i=e_i: e.tensor_mul(
                                 probb.ap()[:, e_i * TT:(e_i + 1) * TT], bk.ap()[:, :], inv2h_sb),
                             inc=f"{tag}_b67free{e_i % 2}")
                    P.drain("vector")

                    for idx, e_i in enumerate(e_order):
                        if idx == 0:
                            continue
                        h_stage(e_i, idx)
                    if tt + 1 < NTILE:
                        rt_next = r_load(tt + 1)
                        sq_next = s1_rbsq(tt + 1, rt_next, P.cur(f"{tag}_pe_h"),
                                          P.cur(f"{tag}_ve_h"))
                    veh_done = P.cur(f"{tag}_ve_h")

                    P.wait("tensor", f"{tag}_ve_h", veh_done)
                    peout_b8 = P.cur(f"{tag}_pe_out")
                    for e_i in range(9):
                        g_idx = tt * 9 + e_i
                        wpar = g_idx % 2
                        P.wait("tensor", *w2_src[g_idx])
                        for oc in range(NC):
                            for hc in range(4):
                                wcol = wpar * 4 * D + hc * D + oc * 128
                                last_of_group = (oc == NC - 1 and hc == 3)
                                if e_i == 8 and hc == 3:
                                    inc = f"{tag}_pe_out"
                                elif e_i < 8 and last_of_group:
                                    inc = f"{tag}_w2c"
                                else:
                                    inc = None
                                P.do("tensor",
                                     lambda e, oc=oc, hc=hc, wcol=wcol, e_i=e_i: e.matmul(
                                         pb[oc].ap()[:, :],
                                         w2b.ap()[:, wcol:wcol + 128],
                                         h_sb.ap()[:, (e_i * 4 + hc) * TT:(e_i * 4 + hc + 1) * TT],
                                         start=(e_i == 0 and hc == 0), stop=(e_i == 8 and hc == 3)),
                                     inc=inc)
                        if e_i < 8:
                            grel.append((f"{tag}_w2c", P.cur(f"{tag}_w2c")))
                        else:
                            grel.append((f"{tag}_pe_out", P.cur(f"{tag}_pe_out")))
                        if w2cnt[0] < NTILE * 9:
                            w2_load()
                    peout_done = P.cur(f"{tag}_pe_out")

                    veout_base = P.cur(f"{tag}_ve_out")
                    for oc in range(NC):
                        P.wait("vector", f"{tag}_pe_out", peout_b8 + oc + 1)
                        P.do("vector",
                             lambda e, oc=oc, rt=rt: e.tensor_add(rt[:, oc * TT:(oc + 1) * TT],
                                                                  rt[:, oc * TT:(oc + 1) * TT],
                                                                  pb[oc].ap()[:, :]),
                             inc=f"{tag}_ve_out")
                    veout_done = P.cur(f"{tag}_ve_out")
                    prev_veout = veout_done
                    prev_veout_b = veout_base
                    prev_peout = peout_done

                    P.wait("scalar", f"{tag}_ve_out", veout_done)
                    P.wait("scalar", f"{tag}_rs", rs_base + 16 * tt)
                    P.do("scalar",
                         lambda e, tt=tt, rt=rt: e.dma_start(
                             out=r_view_tile(tt), in_=rt.rearrange("p (c t) -> p c t", c=NC)),
                         inc=f"{tag}_rs", n=16)

                P.wait("sync", f"{tag}_rs")

        def attn_phase(tag="a"):
            pes = _ES()
            with pes:
                def psb(name, shape, dt):
                    return pes.enter_context(nc.sbuf_tensor(name, shape, dt))
                sbtag = tag

                lnb_all = psb(f"{tag}_lnb", [128, NC * T], BF16)
                prows = psb(f"{tag}_prows", [16, T], F32)
                acc_M = psb(f"{tag}_accM", [128, H * DK], F32)
                facc = psb(f"{tag}_facc", [16, 8], F32)
                r_t = psb(f"{sbtag}_rt", [128, 2 * NC * TT], F32)
                asc = psb(f"{tag}_asc", [128, NC * TT], BF16)
                rtile = psb(f"{tag}_rtile", [128, TT], F32)
                rtile2 = psb(f"{tag}_rtile2", [128, TT], F32)
                fwr = psb(f"{tag}_fwr", [16, TT], F32)
                swr = psb(f"{tag}_swr", [16, TT], F32)
                qnr = psb(f"{tag}_qnr", [16, TT], F32)
                mu_row = psb(f"{sbtag}_mu", [1, TT], F32)
                ex2_row = psb(f"{sbtag}_ex2", [1, TT], F32)
                t_row = psb(f"{tag}_trow", [1, TT], F32)
                inv_row = psb(f"{sbtag}_inv", [1, TT], F32)
                ebc = psb(f"{tag}_ebc", [128, 16], F32)
                etr = psb(f"{tag}_etr", [32, 128], F32)
                azeros = psb(f"{tag}_az", [128, TT], F32)

                rb = asc.ap()[:, :]

                P.do("gpsimd", lambda e: e.memset(acc_M[:], 0.0), inc=f"{tag}_gz")
                P.do("gpsimd", lambda e: e.memset(facc[:], 0.0), inc=f"{tag}_gz")
                P.do("gpsimd", lambda e: e.memset(azeros[:], 0.0), inc=f"{tag}_gz")
                P.do("gpsimd", lambda e: e.memset(etr[:], 0.0), inc=f"{tag}_gz")
                P.do("gpsimd", lambda e: e.memset(rtile[:], 0.0), inc=f"{tag}_gz")
                P.do("gpsimd", lambda e: e.memset(rtile2[:], 0.0), inc=f"{tag}_gz")

                def bc_prep(col_f, col_s):
                    P.do("vector", lambda e, col_f=col_f: e.tensor_copy(etr.ap()[0:16, 0:1], col_f))
                    P.do("vector", lambda e, col_s=col_s: e.tensor_copy(etr.ap()[0:16, 32:33], col_s))
                    P.drain("vector")
                    P.do("vector", lambda e: e.transpose(etr.ap()[:, 64:96], etr.ap()[:, 0:32]))
                    P.do("vector", lambda e: e.transpose(etr.ap()[:, 96:128], etr.ap()[:, 32:64]),
                         inc=f"{tag}_ve_bc")
                    vb = P.cur(f"{tag}_ve_bc")
                    P.drain("vector")
                    return vb

                def bc_fs(col_f, col_s, bank, out_sb, vb=None):
                    if vb is None:
                        vb = bc_prep(col_f, col_s)
                    P.wait("tensor", f"{tag}_ve_bc", vb)
                    P.do("tensor", lambda e, bank=bank: e.matmul(
                        bank.ap()[0:64, 0:16], ones_row_f32[0:1, 0:64], etr.ap()[0:1, 64:80],
                        start=True, stop=True))
                    P.do("tensor", lambda e, bank=bank: e.matmul(
                        bank.ap()[64:128, 0:16], ones_row_f32[0:1, 0:64], etr.ap()[0:1, 96:112],
                        start=True, stop=True, tile_position=(0, 64)), inc=f"{tag}_pe_bcm")
                    pbcm = P.cur(f"{tag}_pe_bcm")
                    P.wait("vector", f"{tag}_pe_bcm", pbcm)
                    P.do("vector", lambda e, bank=bank, out_sb=out_sb: e.tensor_copy(
                        out_sb, bank.ap()[:, 0:16]), inc=f"{tag}_ve_bce")
                    P.drain("vector")
                    return P.cur(f"{tag}_ve_bce")

                def ln_stats_lnb(tt):
                    pass

                s1 = _ES()
                with s1:
                    def s1b(name, shape, dt):
                        return s1.enter_context(nc.sbuf_tensor(name, shape, dt))

                    wk_i = s1b(f"{tag}_wk", [128, NC * D], BF16)
                    wv_i = s1b(f"{tag}_wv", [128, NC * D], BF16)
                    wb_i = s1b(f"{tag}_wb", [128, NC * H], BF16)
                    wfd_i = s1b(f"{tag}_wfd", [128, NC * H], BF16)
                    wsd_i = s1b(f"{tag}_wsd", [128, NC * H], BF16)
                    wt1_i = s1b(f"{tag}_wt1", [128, 32], BF16)
                    wt2_i = s1b(f"{tag}_wt2", [128, 4 * H], BF16)
                    k_tm = s1b(f"{tag}_ktm", [128, D], BF16)
                    v_tm = s1b(f"{tag}_vtm", [128, D], BF16)
                    kbh32 = s1b(f"{tag}_kbh", [128, D], F32)
                    kk = s1b(f"{tag}_kk", [128, H * 128], BF16)
                    kbh_fm = s1b(f"{tag}_kfm", [128, NC * TT], BF16)
                    hmid_sb = s1b(f"{tag}_hmid", [128, 4 * TT], BF16)
                    k2t = hmid_sb
                    kninv = s1b(f"{tag}_kninv", [128, 4 * H], F32)
                    packT = s1b(f"{tag}_packT", [128, 96], F32)
                    c1t = s1b(f"{tag}_c1", [128, H], F32)

                    for wi, wf_ in ((wk_i, wimg["wk"]), (wv_i, wimg["wv"]), (wb_i, wbf),
                                    (wfd_i, wfdf), (wsd_i, wsdf), (wt1_i, wtf1f), (wt2_i, wtf2f)):
                        P.do("gpsimd", lambda e, wi=wi, wf_=wf_: e.dma_start(out=wi[:], in_=wf_),
                             inc=f"{tag}_wl", n=16)
                    wl1 = P.cur(f"{tag}_wl")

                    pv = dict(pe_st=0, pe_bc=0, pe_hm=0, sc_row=0, pe_tin=0, ve_r2=0,
                              ve_acc=0, pe_fx=0, ve_bce=0, pe_row=0)
                    prev_lnb = [0]
                    arl_base = [P.cur("f_rl0"), P.cur("f_rl1")]
                    sq = kbh_fm.ap()[:, :]

                    def a_r_load(tt):
                        par = tt % 2
                        rt_ = r_t.ap()[:, par * NC * TT:(par + 1) * NC * TT]
                        P.wait("sync", "f_rs")
                        P.wait("sync", f"{tag}_ve_lnb", prev_lnb[0])
                        P.do("sync",
                             lambda e, tt=tt, rt_=rt_: e.dma_start(
                                 out=rt_.rearrange("p (c t) -> p c t", c=NC), in_=r_view_tile(tt)),
                             inc="f_rl" + str(par), n=16)
                        return rt_

                    def a_s1(tt, rt_):
                        par = tt % 2
                        P.wait("vector", "f_rl" + str(par), arl_base[par] + 16 * (tt // 2 + 1))
                        P.wait("vector", f"{tag}_pe_st", pv["pe_st"])
                        P.wait("vector", f"{tag}_pe_hm", pv["pe_hm"])
                        P.do("vector", lambda e, rt_=rt_: e.tensor_copy(rb, rt_))
                        P.drain("vector")
                        P.do("vector", lambda e: e.tensor_mul(sq, rb, rb), inc=f"{tag}_ve_sq")
                        return P.cur(f"{tag}_ve_sq")

                    rt_next = a_r_load(0)
                    sq_next = a_s1(0, rt_next)
                    for tt in range(NTILE):
                        rt = rt_next
                        sq_done = sq_next

                        P.wait("tensor", "cl", 32)
                        P.wait("tensor", f"{tag}_ve_sq", sq_done)
                        P.wait("tensor", f"{tag}_sc_row", pv["sc_row"])
                        for c in range(NC):
                            P.do("tensor",
                                 lambda e, c=c: e.matmul(pb[0].ap()[0:1, :],
                                                         cbf.ap()[:, CB_ONESCOL:CB_ONESCOL + 1],
                                                         rb[:, c * TT:(c + 1) * TT],
                                                         start=(c == 0), stop=(c == NC - 1)))
                        for c in range(NC):
                            P.do("tensor",
                                 lambda e, c=c, sq=sq: e.matmul(
                                     pb[1].ap()[0:1, :], cbf.ap()[:, CB_ONESCOL:CB_ONESCOL + 1],
                                     sq[:, c * TT:(c + 1) * TT],
                                     start=(c == 0), stop=(c == NC - 1)),
                                 inc=f"{tag}_pe_st" if c == NC - 1 else None)
                        pv["pe_st"] = P.cur(f"{tag}_pe_st")

                        P.wait("vector", f"{tag}_pe_st", pv["pe_st"])
                        P.do("vector", lambda e: e.tensor_scalar_mul(mu_row[:], pb[0].ap()[0:1, :], 1.0 / D))
                        P.do("vector", lambda e: e.tensor_scalar_mul(ex2_row[:], pb[1].ap()[0:1, :], 1.0 / D))
                        P.drain("vector")
                        P.do("vector", lambda e: e.tensor_mul(t_row[:], mu_row[:], mu_row[:]))
                        P.drain("vector")
                        P.do("vector", lambda e: e.tensor_sub(ex2_row[:], ex2_row[:], t_row[:]),
                             inc=f"{tag}_ve_var")
                        var_done = P.cur(f"{tag}_ve_var")
                        P.wait("scalar", f"{tag}_ve_var", var_done)
                        P.do("scalar", lambda e: e.activation(ex2_row[:], ex2_row[:], Act.Ln,
                                                              bias=cst.ap()[0:1, C_EPSLN:C_EPSLN + 1]))
                        P.drain("scalar")
                        P.do("scalar", lambda e: e.activation(inv_row[:], ex2_row[:], Act.Exp, scale=-0.5),
                             inc=f"{tag}_sc_inv")
                        inv_done = P.cur(f"{tag}_sc_inv")

                        P.wait("tensor", f"{tag}_ve_var", var_done)
                        P.wait("tensor", f"{tag}_sc_inv", inv_done)
                        P.wait("tensor", f"{tag}_ve_lnb", P.cur(f"{tag}_ve_lnb"))
                        P.wait("tensor", f"{tag}_ve_r2", pv["ve_r2"])
                        P.wait("tensor", f"{tag}_ve_kv", P.cur(f"{tag}_ve_kv"))
                        P.do("tensor", lambda e: e.matmul(pb[2].ap()[:, :], ones_row_f32, mu_row[:],
                                                          start=True, stop=True))
                        P.do("tensor", lambda e: e.matmul(pb[3].ap()[0:16, :], ones_row_f32[0:1, 0:16],
                                                          inv_row[:], start=True, stop=True),
                             inc=f"{tag}_pe_bc")
                        pv["pe_bc"] = P.cur(f"{tag}_pe_bc")

                        P.wait("vector", f"{tag}_pe_bc", pv["pe_bc"])
                        for c in range(NC):
                            P.do("vector",
                                 lambda e, c=c, tt=tt, rt=rt: e.tensor_sub(
                                     lnb_all.ap()[:, c * T + tt * TT:c * T + tt * TT + TT],
                                     rt[:, c * TT:(c + 1) * TT], pb[2].ap()[:, :]),
                                 inc=f"{tag}_ve_lnb" if c == NC - 1 else None)
                        lnb_done = P.cur(f"{tag}_ve_lnb")
                        prev_lnb[0] = lnb_done
                        P.drain("vector")

                        P.wait("tensor", f"{tag}_wl", wl1)
                        P.wait("tensor", f"{tag}_ve_lnb", lnb_done)
                        P.wait("tensor", f"{tag}_ve_tin", P.cur(f"{tag}_ve_tin"))
                        for bi, wi in ((0, wb_i), (1, wfd_i), (4, wsd_i)):
                            for c in range(NC):
                                P.do("tensor",
                                     lambda e, bi=bi, wi=wi, c=c, tt=tt: e.matmul(
                                         pb[bi].ap()[0:16, :],
                                         wi.ap()[:, c * H:(c + 1) * H],
                                         lnb_all.ap()[:, c * T + tt * TT:c * T + tt * TT + TT],
                                         start=(c == 0), stop=(c == NC - 1)),
                                     inc=f"{tag}_pe_row" if (bi == 4 and c == NC - 1) else None)
                        pv["pe_row"] = P.cur(f"{tag}_pe_row")

                        P.wait("scalar", f"{tag}_pe_row", pv["pe_row"])
                        P.wait("scalar", f"{tag}_pe_tin", pv["pe_tin"])
                        P.wait("scalar", f"{tag}_ve_r2", pv["ve_r2"])
                        P.wait("scalar", f"{tag}_gz", P.cur(f"{tag}_gz"))
                        P.do("scalar", lambda e: e.activation(rtile.ap()[0:16, :], pb[0].ap()[0:16, :],
                                                              Act.Sigmoid))
                        P.do("scalar", lambda e: e.activation(rtile.ap()[32:48, :], pb[1].ap()[0:16, :],
                                                              Act.Sigmoid,
                                                              bias=cst.ap()[0:16, C_BFD:C_BFD + 1]))
                        P.do("scalar", lambda e: e.activation(rtile.ap()[64:80, :], pb[4].ap()[0:16, :],
                                                              Act.Sigmoid,
                                                              bias=cst.ap()[0:16, C_BSD:C_BSD + 1]))
                        P.drain("scalar")
                        P.do("scalar", lambda e: e.activation(rtile.ap()[32:48, :], rtile.ap()[32:48, :],
                                                              Act.Ln,
                                                              bias=cst.ap()[32:48, C_EPS6:C_EPS6 + 1]))
                        P.do("scalar", lambda e: e.activation(rtile.ap()[64:80, :], rtile.ap()[64:80, :],
                                                              Act.Ln,
                                                              bias=cst.ap()[64:80, C_EPS6:C_EPS6 + 1]),
                             inc=f"{tag}_sc_row")
                        pv["sc_row"] = P.cur(f"{tag}_sc_row")

                        P.wait("vector", f"{tag}_sc_row", pv["sc_row"])
                        P.wait("vector", f"{tag}_gz", P.cur(f"{tag}_gz"))
                        P.do("vector", lambda e: e.tensor_tensor_scan(
                            rtile.ap()[96:112, :], azeros.ap()[32:48, :], rtile.ap()[32:48, :], 0.0,
                            Alu.add, Alu.add))
                        P.drain("vector")
                        P.do("vector", lambda e: e.tensor_tensor_scan(
                            rtile2.ap()[64:80, :], azeros.ap()[64:80, :], rtile.ap()[64:80, :], 0.0,
                            Alu.add, Alu.add), inc=f"{tag}_ve_cs")
                        cs_done = P.cur(f"{tag}_ve_cs")
                        P.drain("vector")

                        P.wait("scalar", f"{tag}_ve_cs", cs_done)
                        P.do("scalar", lambda e: e.activation(fwr.ap()[:, :], rtile.ap()[96:112, :],
                                                              Act.Exp, scale=-1.0,
                                                              bias=rtile.ap()[96:112, TT - 1:TT]))
                        P.do("scalar", lambda e: e.activation(swr.ap()[:, :], rtile2.ap()[64:80, :],
                                                              Act.Exp, scale=-1.0,
                                                              bias=rtile2.ap()[64:80, TT - 1:TT]))
                        P.drain("scalar")
                        P.do("scalar", lambda e: e.activation(facc.ap()[:, 4:5],
                                                              rtile.ap()[96:112, TT - 1:TT], Act.Exp))
                        P.do("scalar", lambda e: e.activation(facc.ap()[:, 5:6],
                                                              rtile2.ap()[64:80, TT - 1:TT], Act.Exp),
                             inc=f"{tag}_sc_fw")
                        fw_done = P.cur(f"{tag}_sc_fw")

                        P.wait("vector", f"{tag}_sc_fw", fw_done)
                        P.do("vector", lambda e: e.tensor_copy(etr.ap()[0:16, 1:2],
                                                               rtile.ap()[96:112, TT - 1:TT]))
                        P.do("vector", lambda e: e.tensor_copy(etr.ap()[0:16, 2:3],
                                                               rtile2.ap()[64:80, TT - 1:TT]))
                        P.drain("vector")
                        P.do("vector", lambda e: e.tensor_add(facc.ap()[:, 0:1], facc.ap()[:, 0:1],
                                                              etr.ap()[0:16, 1:2]))
                        P.do("vector", lambda e: e.tensor_add(facc.ap()[:, 1:2], facc.ap()[:, 1:2],
                                                              etr.ap()[0:16, 2:3]))
                        P.drain("vector")
                        P.do("vector", lambda e: e.reduce_sum(facc.ap()[:, 6:7], fwr.ap()[:, :],
                                                              mybir.AxisListType.X))
                        P.do("vector", lambda e: e.reduce_sum(facc.ap()[:, 7:8], swr.ap()[:, :],
                                                              mybir.AxisListType.X))
                        P.drain("vector")
                        P.do("vector", lambda e: e.scalar_tensor_tensor(
                            facc.ap()[:, 2:3], facc.ap()[:, 2:3], facc.ap()[:, 4:5],
                            facc.ap()[:, 6:7], Alu.mult, Alu.add))
                        P.do("vector", lambda e: e.scalar_tensor_tensor(
                            facc.ap()[:, 3:4], facc.ap()[:, 3:4], facc.ap()[:, 5:6],
                            facc.ap()[:, 7:8], Alu.mult, Alu.add))
                        P.drain("vector")
                        vb_tile = bc_prep(facc.ap()[:, 4:5], facc.ap()[:, 5:6])
                        P.do("vector", lambda e: e.tensor_mul(rtile.ap()[32:48, :], rtile.ap()[0:16, :],
                                                              fwr.ap()[:, :]))
                        P.do("vector", lambda e: e.tensor_mul(rtile.ap()[64:80, :], rtile.ap()[0:16, :],
                                                              swr.ap()[:, :]))
                        P.drain("vector")
                        P.do("vector", lambda e: e.tensor_mul(rtile.ap()[32:48, :], rtile.ap()[32:48, :],
                                                              pb[3].ap()[0:16, :]))
                        P.do("vector", lambda e: e.tensor_mul(rtile.ap()[64:80, :], rtile.ap()[64:80, :],
                                                              pb[3].ap()[0:16, :]), inc=f"{tag}_ve_r2")
                        pv["ve_r2"] = P.cur(f"{tag}_ve_r2")
                        P.drain("vector")

                        for blk in range(NBLK):
                            lcol = tt * TT + blk * 128
                            P.wait("tensor", f"{tag}_sc_row", pv["sc_row"])
                            P.wait("tensor", f"{tag}_ve_kv", max(0, P.cur(f"{tag}_ve_kv") - 1))
                            for c in range(NC):
                                for half in range(2):
                                    P.do("tensor",
                                         lambda e, c=c, half=half, lcol=lcol: e.matmul(
                                             pb[half].ap()[:, :],
                                             lnb_all.ap()[:, c * T + lcol:c * T + lcol + 128],
                                             wk_i.ap()[:, c * D + half * 512:c * D + (half + 1) * 512],
                                             start=(c == 0), stop=(c == NC - 1)),
                                         inc=f"{tag}_pe_k" if (c == NC - 1 and half == 1) else None)
                            pek = P.cur(f"{tag}_pe_k")
                            P.wait("vector", f"{tag}_pe_k", pek)
                            P.wait("vector", f"{tag}_pe_M", P.cur(f"{tag}_pe_M"))
                            P.do("vector", lambda e: e.tensor_copy(k_tm.ap()[:, 0:512], pb[0].ap()[:, :]))
                            P.do("vector", lambda e: e.tensor_copy(k_tm.ap()[:, 512:1024],
                                                                   pb[1].ap()[:, :]),
                                 inc=f"{tag}_ve_kv")
                            P.drain("vector")
                            P.wait("vector", f"{tag}_pe_fx", pv["pe_fx"])
                            P.do("vector", lambda e: e.tensor_mul(k2t.ap()[:, 0:D], k_tm[:], k_tm[:]))
                            P.drain("vector")
                            P.do("vector", lambda e, blk=blk: e.tensor_reduce(
                                kninv.ap()[:, blk * H:(blk + 1) * H],
                                k2t.ap()[:, 0:D].rearrange("p (h d) -> p h d", h=H),
                                mybir.AxisListType.X, Alu.add))
                            P.drain("vector")
                            P.do("vector", lambda e, blk=blk: e.tensor_scalar_max(
                                kninv.ap()[:, blk * H:(blk + 1) * H],
                                kninv.ap()[:, blk * H:(blk + 1) * H], 1e-18), inc=f"{tag}_ve_kn")
                            kn_done = P.cur(f"{tag}_ve_kn")
                            P.wait("scalar", f"{tag}_ve_kn", kn_done)
                            P.do("scalar", lambda e, blk=blk: e.activation(
                                kninv.ap()[:, blk * H:(blk + 1) * H],
                                kninv.ap()[:, blk * H:(blk + 1) * H], Act.Ln))
                            P.drain("scalar")
                            P.do("scalar", lambda e, blk=blk: e.activation(
                                kninv.ap()[:, blk * H:(blk + 1) * H],
                                kninv.ap()[:, blk * H:(blk + 1) * H], Act.Exp, scale=-0.5),
                                inc=f"{tag}_sc_kn")
                            skn = P.cur(f"{tag}_sc_kn")
                            P.wait("tensor", f"{tag}_ve_lnb", lnb_done)
                            P.wait("tensor", f"{tag}_ve_r2", pv["ve_r2"])
                            P.wait("tensor", f"{tag}_ve_kv", max(0, P.cur(f"{tag}_ve_kv") - 1))
                            for c in range(NC):
                                for half in range(2):
                                    P.do("tensor",
                                         lambda e, c=c, half=half, lcol=lcol: e.matmul(
                                             pb[2 + half].ap()[:, :],
                                             lnb_all.ap()[:, c * T + lcol:c * T + lcol + 128],
                                             wv_i.ap()[:, c * D + half * 512:c * D + (half + 1) * 512],
                                             start=(c == 0), stop=(c == NC - 1)),
                                         inc=f"{tag}_pe_v" if (c == NC - 1 and half == 1) else None)
                            pevd = P.cur(f"{tag}_pe_v")
                            P.wait("vector", f"{tag}_pe_v", pevd)
                            P.do("vector", lambda e: e.tensor_copy(v_tm.ap()[:, 0:512], pb[2].ap()[:, :]))
                            P.do("vector", lambda e: e.tensor_copy(v_tm.ap()[:, 512:1024],
                                                                   pb[3].ap()[:, :]),
                                 inc=f"{tag}_ve_kv")
                            P.drain("vector")
                            P.wait("tensor", f"{tag}_ve_r2", pv["ve_r2"])
                            P.wait("tensor", f"{tag}_ve_tin", P.cur(f"{tag}_ve_tin"))
                            P.wait("tensor", f"{tag}_ve_kf", P.cur(f"{tag}_ve_kf"))
                            P.do("tensor", lambda e, blk=blk: e.transpose(
                                pb[4].ap()[:, 0:96], rtile.ap()[0:96, blk * 128:(blk + 1) * 128],
                                ident[0:96, 0:96]), inc=f"{tag}_pe_tin")
                            pv["pe_tin"] = P.cur(f"{tag}_pe_tin")
                            P.wait("vector", f"{tag}_pe_tin", pv["pe_tin"])
                            P.do("vector", lambda e: e.tensor_copy(packT[:], pb[4].ap()[:, 0:96]),
                                 inc=f"{tag}_ve_tin")
                            P.drain("vector")
                            P.wait("vector", f"{tag}_sc_kn", skn)
                            P.wait("vector", f"{tag}_pe_M", P.cur(f"{tag}_pe_M"))
                            P.do("vector", lambda e, blk=blk: e.tensor_mul(
                                c1t[:], packT.ap()[:, 0:16], kninv.ap()[:, blk * H:(blk + 1) * H]))
                            P.drain("vector")
                            P.do("vector", lambda e: e.tensor_tensor(
                                kbh32.ap().rearrange("p (h d) -> p h d", h=H),
                                k_tm.ap().rearrange("p (h d) -> p h d", h=H),
                                c1t.ap().unsqueeze(2).broadcast_to((128, H, DK)), Alu.mult),
                                inc=f"{tag}_ve_kb")
                            kb_done = P.cur(f"{tag}_ve_kb")
                            P.drain("vector")
                            P.do("vector", lambda e: e.tensor_tensor(
                                kk.ap().rearrange("p (h x d) -> p h x d", h=H, x=2)[:, :, 0, :],
                                kbh32.ap().rearrange("p (h d) -> p h d", h=H),
                                packT.ap()[:, 32:48].unsqueeze(2).broadcast_to((128, H, DK)),
                                Alu.mult))
                            P.do("vector", lambda e: e.tensor_tensor(
                                kk.ap().rearrange("p (h x d) -> p h x d", h=H, x=2)[:, :, 1, :],
                                kbh32.ap().rearrange("p (h d) -> p h d", h=H),
                                packT.ap()[:, 64:80].unsqueeze(2).broadcast_to((128, H, DK)),
                                Alu.mult), inc=f"{tag}_ve_kk")
                            kk_done = P.cur(f"{tag}_ve_kk")
                            P.drain("vector")
                            P.wait("tensor", f"{tag}_ve_kb", kb_done)
                            P.wait("tensor", f"{tag}_ve_tin", P.cur(f"{tag}_ve_tin"))
                            P.wait("tensor", f"{tag}_ve_kf", P.cur(f"{tag}_ve_kf"))
                            P.wait("tensor", f"{tag}_ve_bce", pv["ve_bce"])
                            P.wait("tensor", f"{tag}_sc_fx", P.cur(f"{tag}_sc_fx"))
                            for c in range(NC):
                                P.do("tensor",
                                     lambda e, c=c: e.transpose(
                                         pb[4 + c // 4].ap()[:, (c % 4) * 128:(c % 4) * 128 + 128],
                                         kbh32.ap()[:, c * 128:(c + 1) * 128], ident),
                                     inc=f"{tag}_pe_kt" if c == NC - 1 else None)
                            kt_done = P.cur(f"{tag}_pe_kt")
                            P.wait("vector", f"{tag}_pe_kt", kt_done)
                            for c in range(NC):
                                P.do("vector",
                                     lambda e, c=c, blk=blk: e.tensor_copy(
                                         kbh_fm.ap()[:, c * TT + blk * 128:c * TT + blk * 128 + 128],
                                         pb[4 + c // 4].ap()[:, (c % 4) * 128:(c % 4) * 128 + 128]),
                                     inc=f"{tag}_ve_kf" if c == NC - 1 else None)
                            P.drain("vector")

                            P.wait("tensor", f"{tag}_ve_kk", kk_done)
                            P.wait("tensor", f"{tag}_ve_acc", pv["ve_acc"])
                            for h in range(H):
                                bk = pb[6 + h // 8]
                                st_ = (blk == 0 and h % 8 == 0)
                                sp_ = (blk == 3 and h % 8 == 7)
                                P.do("tensor",
                                     lambda e, h=h, bk=bk, st_=st_, sp_=sp_: e.matmul(
                                         bk.ap()[:, (h % 8) * DK:(h % 8 + 1) * DK],
                                         kk.ap()[:, h * 128:(h + 1) * 128],
                                         v_tm.ap()[:, h * DK:(h + 1) * DK],
                                         start=st_, stop=sp_),
                                     inc=f"{tag}_pe_M" if h == H - 1 else None)
                        kf_all = P.cur(f"{tag}_ve_kf")
                        P.wait("tensor", f"{tag}_ve_kf", kf_all)
                        P.wait("tensor", f"{tag}_sc_hm", P.cur(f"{tag}_sc_hm"))
                        P.wait("tensor", f"{tag}_ve_kv", P.cur(f"{tag}_ve_kv"))
                        P.wait("tensor", f"{tag}_ve_lnb", lnb_done)
                        for h in range(H):
                            c, par = h // 2, h % 2
                            bk = pb[2 + h // 4]
                            P.do("tensor",
                                 lambda e, h=h, c=c, par=par, bk=bk: e.matmul(
                                     bk.ap()[32 * (h % 4):32 * (h % 4) + 32, :],
                                     wt1_i.ap()[64 * par:64 * par + 64, :],
                                     kbh_fm.ap()[64 * par:64 * par + 64, c * TT:(c + 1) * TT],
                                     start=True, stop=True,
                                     tile_position=(64 * par, 32 * (h % 4))),
                                 inc=f"{tag}_pe_hm" if h == H - 1 else None)
                        pv["pe_hm"] = P.cur(f"{tag}_pe_hm")
                        if tt + 1 < NTILE:
                            rt_next = a_r_load(tt + 1)
                            sq_next = a_s1(tt + 1, rt_next)
                        P.wait("scalar", f"{tag}_pe_hm", pv["pe_hm"])
                        P.wait("scalar", f"{tag}_pe_fx", pv["pe_fx"])
                        for g in range(4):
                            P.do("scalar",
                                 lambda e, g=g: e.activation(
                                     hmid_sb.ap()[:, g * TT:(g + 1) * TT], pb[2 + g].ap()[:, :],
                                     Act.Silu, bias=cst.ap()[:, C_BTF1:C_BTF1 + 1]),
                                 inc=f"{tag}_sc_hm" if g == 3 else None)
                        hm_done = P.cur(f"{tag}_sc_hm")
                        P.wait("tensor", f"{tag}_sc_hm", hm_done)
                        P.wait("tensor", f"{tag}_ve_bce", pv["ve_bce"])
                        for g in range(4):
                            P.do("tensor",
                                 lambda e, g=g: e.matmul(
                                     pb[5].ap()[0:16, :], wt2_i.ap()[:, g * H:(g + 1) * H],
                                     hmid_sb.ap()[:, g * TT:(g + 1) * TT],
                                     start=(g == 0), stop=(g == 3)),
                                 inc=f"{tag}_pe_fx" if g == 3 else None)
                        pv["pe_fx"] = P.cur(f"{tag}_pe_fx")
                        P.wait("scalar", f"{tag}_pe_fx", pv["pe_fx"])
                        P.do("scalar", lambda e, tt=tt: e.activation(
                            prows.ap()[:, tt * TT:(tt + 1) * TT], pb[5].ap()[0:16, :], Act.Sigmoid,
                            bias=cst.ap()[0:16, C_BTF2:C_BTF2 + 1]), inc=f"{tag}_sc_fx")
                        fx_done = P.cur(f"{tag}_sc_fx")
                        P.wait("vector", f"{tag}_sc_fx", fx_done)
                        P.do("vector", lambda e, tt=tt: e.tensor_scalar(
                            prows.ap()[:, tt * TT:(tt + 1) * TT],
                            prows.ap()[:, tt * TT:(tt + 1) * TT],
                            0.01, 0.99, Alu.max, Alu.min, accum_out=None), inc=f"{tag}_ve_fx")
                        P.drain("vector")

                        P.wait("tensor", f"{tag}_sc_fx", fx_done)
                        P.wait("tensor", f"{tag}_ve_kf", P.cur(f"{tag}_ve_kf"))
                        pv["ve_bce"] = bc_fs(facc.ap()[:, 4:5], facc.ap()[:, 5:6], pb[5],
                                             ebc.ap()[:, :], vb=vb_tile)
                        P.wait("vector", f"{tag}_pe_M", P.cur(f"{tag}_pe_M"))
                        P.do("vector", lambda e: e.tensor_tensor(
                            acc_M.ap().rearrange("p (h d) -> p h d", h=H),
                            acc_M.ap().rearrange("p (h d) -> p h d", h=H),
                            ebc.ap().unsqueeze(2).broadcast_to((128, H, DK)), Alu.mult))
                        P.drain("vector")
                        P.do("vector", lambda e: e.tensor_add(acc_M.ap()[:, 0:512],
                                                              acc_M.ap()[:, 0:512], pb[6].ap()[:, :]))
                        P.do("vector", lambda e: e.tensor_add(acc_M.ap()[:, 512:1024],
                                                              acc_M.ap()[:, 512:1024], pb[7].ap()[:, :]),
                             inc=f"{tag}_ve_acc")
                        pv["ve_acc"] = P.cur(f"{tag}_ve_acc")
                        P.drain("vector")

                for s_ in ("pe_k", "pe_v", "pe_M", "pe_kt", "pe_hm", "pe_fx", "sc_hm",
                           "ve_kk", "ve_kf", "ve_acc", "ve_kv", "ve_tin", "sc_kn", "ve_kn",
                           "pe_tin", "sc_fx", "ve_fx", "pe_row", "sc_row"):
                    P.wait("gpsimd", f"{tag}_{s_}", P.cur(f"{tag}_{s_}"))
                P.wait("gpsimd", f"{tag}_ve_acc", P.cur(f"{tag}_ve_acc"))
                P.do("gpsimd", lambda e: e.dma_start(out=cc_in[:, 0:1024], in_=acc_M[:]),
                     inc=f"{tag}_ccin", n=16)
                P.do("gpsimd", lambda e: e.dma_start(out=cc_in[0:16, 1024:1028], in_=facc.ap()[:, 0:4]),
                     inc=f"{tag}_ccin", n=16)
                P.wait("gpsimd", "cl", 32)
                P.do("gpsimd", lambda e: e.dma_start(out=cc_in[16:128, 1024:1028],
                                                     in_=cst.ap()[16:128, 272:276]),
                     inc=f"{tag}_ccin", n=16)
                if dbg:
                    P.do("gpsimd", lambda e: e.dma_start(out=d_accM, in_=acc_M[:]),
                         inc=f"{tag}_dbg", n=16)
                    P.do("gpsimd", lambda e: e.dma_start(out=d_facc, in_=facc[:]),
                         inc=f"{tag}_dbg", n=16)
                    P.wait("gpsimd", f"{tag}_ve_fx", P.cur(f"{tag}_ve_fx"))
                    P.do("gpsimd", lambda e: e.dma_start(out=d_flux, in_=prows[:]),
                         inc=f"{tag}_dbg", n=16)
                P.wait("gpsimd", f"{tag}_ccin")
                P.do("gpsimd", lambda e: e.collective_compute(
                    "AllGather", Alu.bypass,
                    replica_groups=[[2 * i, 2 * i + 1] for i in range(n_pairs)],
                    ins=[cc_in.ap()], outs=[cc_out.ap()]), inc=f"{tag}_cc")

                s2 = _ES()
                with s2:
                    def s2b(name, shape, dt):
                        return s2.enter_context(nc.sbuf_tensor(name, shape, dt))

                    wq_i = s2b(f"{tag}_wq", [128, NC * D], BF16)
                    wo_i = s2b(f"{tag}_wo", [128, NC * D], BF16)
                    q_fm = s2b(f"{tag}_qfm", [128, NC * TT], BF16)
                    ma_sb = s2b(f"{tag}_ma", [128, 1024], F32)
                    mb_sb = s2b(f"{tag}_mb", [128, 1024], F32)
                    mn_sb = ma_sb
                    rab = s2b(f"{tag}_rab", [16, 8], F32)
                    bdf = s2b(f"{tag}_bdf", [128, NC * 128], BF16)
                    bds = s2b(f"{tag}_bds", [128, NC * 128], BF16)
                    o32 = s2b(f"{tag}_o32", [128, 1024], F32)
                    gT = s2b(f"{tag}_gT", [128, 64], F32)
                    dcol = s2b(f"{tag}_dcol", [16, 4], F32)
                    dbc = s2b(f"{tag}_dbc", [128, 16], F32)

                    P.do("gpsimd", lambda e: e.dma_start(out=wq_i[:], in_=wimg["wq"]),
                         inc=f"{tag}_wl", n=16)
                    wl2q = P.cur(f"{tag}_wl")
                    P.wait("gpsimd", f"{tag}_wl", wl2q)
                    P.do("gpsimd", lambda e: e.dma_start(out=wo_i[:], in_=wimg["wo"]),
                         inc=f"{tag}_wl", n=16)
                    wl2 = P.cur(f"{tag}_wl")
                    P.wait("gpsimd", f"{tag}_cc")
                    P.do("gpsimd", lambda e: e.dma_start(out=ma_sb[:], in_=cc_out[0:128, 0:1024]),
                         inc=f"{tag}_ccl", n=16)
                    P.do("gpsimd", lambda e: e.dma_start(out=mb_sb[:], in_=cc_out[128:256, 0:1024]),
                         inc=f"{tag}_ccl", n=16)
                    P.do("gpsimd", lambda e: e.dma_start(out=rab.ap()[:, 0:4],
                                                         in_=cc_out[0:16, 1024:1028]),
                         inc=f"{tag}_ccl", n=16)
                    P.do("gpsimd", lambda e: e.dma_start(out=rab.ap()[:, 4:8],
                                                         in_=cc_out[128:144, 1024:1028]),
                         inc=f"{tag}_ccl", n=16)
                    ccl_done = P.cur(f"{tag}_ccl")

                    P.wait("scalar", f"{tag}_ccl", ccl_done)
                    P.do("scalar", lambda e: e.activation(dcol.ap()[:, 0:1], rab.ap()[:, 4:5], Act.Exp))
                    P.do("scalar", lambda e: e.activation(dcol.ap()[:, 1:2], rab.ap()[:, 5:6], Act.Exp),
                         inc=f"{tag}_sc_eb")
                    eb_done = P.cur(f"{tag}_sc_eb")
                    P.wait("vector", f"{tag}_sc_eb", eb_done)
                    P.do("vector", lambda e: e.scalar_tensor_tensor(
                        dcol.ap()[:, 2:3], rab.ap()[:, 2:3], dcol.ap()[:, 0:1], rab.ap()[:, 6:7],
                        Alu.mult, Alu.add))
                    P.do("vector", lambda e: e.scalar_tensor_tensor(
                        dcol.ap()[:, 3:4], rab.ap()[:, 3:4], dcol.ap()[:, 1:2], rab.ap()[:, 7:8],
                        Alu.mult, Alu.add))
                    P.drain("vector")
                    P.do("vector", lambda e: e.tensor_scalar_add(dcol.ap()[:, 2:4], dcol.ap()[:, 2:4],
                                                                 1e-6))
                    P.drain("vector")
                    P.do("vector", lambda e: e.reciprocal(dcol.ap()[:, 2:4], dcol.ap()[:, 2:4]))
                    P.drain("vector")
                    bc_fs(dcol.ap()[:, 0:1], dcol.ap()[:, 1:2], pb[5], ebc.ap()[:, :])
                    dj = bc_fs(dcol.ap()[:, 2:3], dcol.ap()[:, 3:4], pb[5], dbc.ap()[:, :])
                    P.wait("vector", f"{tag}_ccl", ccl_done)
                    P.do("vector", lambda e: e.tensor_tensor(
                        mn_sb.ap().rearrange("p (h d) -> p h d", h=H),
                        mn_sb.ap().rearrange("p (h d) -> p h d", h=H),
                        ebc.ap().unsqueeze(2).broadcast_to((128, H, DK)), Alu.mult))
                    P.drain("vector")
                    P.do("vector", lambda e: e.tensor_add(mn_sb[:], mn_sb[:], mb_sb[:]))
                    P.drain("vector")
                    P.do("vector", lambda e: e.tensor_tensor(
                        mn_sb.ap().rearrange("p (h d) -> p h d", h=H),
                        mn_sb.ap().rearrange("p (h d) -> p h d", h=H),
                        dbc.ap().unsqueeze(2).broadcast_to((128, H, DK)), Alu.mult),
                        inc=f"{tag}_ve_mn")
                    mn_done = P.cur(f"{tag}_ve_mn")
                    P.drain("vector")

                    if dbg:
                        P.wait("gpsimd", f"{tag}_ve_mn", mn_done)
                        P.do("gpsimd", lambda e: e.dma_start(out=d_mn, in_=mn_sb[:]),
                             inc=f"{tag}_dbg", n=16)
                        P.do("gpsimd", lambda e: e.dma_start(out=d_mb, in_=mb_sb[:]),
                             inc=f"{tag}_dbg", n=16)
                        P.do("gpsimd", lambda e: e.dma_start(out=d_rab, in_=rab[:]),
                             inc=f"{tag}_dbg", n=16)
                        P.do("gpsimd", lambda e: e.dma_start(out=d_dcol, in_=dcol[:]),
                             inc=f"{tag}_dbg", n=16)
                        P.do("gpsimd", lambda e: e.dma_start(out=d_ebc, in_=ebc[:]),
                             inc=f"{tag}_dbg", n=16)
                        P.do("gpsimd", lambda e: e.dma_start(out=d_dbc, in_=dbc[:]),
                             inc=f"{tag}_dbg", n=16)
                    P.wait("gpsimd", f"{tag}_ve_mn", mn_done)
                    P.do("gpsimd", lambda e: e.memset(bdf[:], 0.0))
                    P.do("gpsimd", lambda e: e.memset(bds[:], 0.0))
                    P.drain("gpsimd")
                    mn_e = mn_sb.ap().rearrange("p (c x) -> p c x", c=H)
                    P.do("gpsimd", lambda e: e.dma_start(
                        out=bdf.ap().rearrange("p (c x) -> p c x", c=NC)[0:64, :, 0:64],
                        in_=mn_e[0:64, 0::2, :]), inc=f"{tag}_bd", n=16)
                    P.do("gpsimd", lambda e: e.dma_start(
                        out=bdf.ap().rearrange("p (c x) -> p c x", c=NC)[64:128, :, 64:128],
                        in_=mn_e[0:64, 1::2, :]), inc=f"{tag}_bd", n=16)
                    P.do("gpsimd", lambda e: e.dma_start(
                        out=bds.ap().rearrange("p (c x) -> p c x", c=NC)[0:64, :, 0:64],
                        in_=mn_e[64:128, 0::2, :]), inc=f"{tag}_bd", n=16)
                    P.do("gpsimd", lambda e: e.dma_start(
                        out=bds.ap().rearrange("p (c x) -> p c x", c=NC)[64:128, :, 64:128],
                        in_=mn_e[64:128, 1::2, :]), inc=f"{tag}_bd", n=16)
                    bd_done = P.cur(f"{tag}_bd")

                    pv2 = dict(ve_q=0, pe_wo=0, ve_ro=0)
                    rt = r_t.ap()[:, 0:NC * TT]
                    s2rl_base = P.cur("f_rl0")
                    for tt in range(NTILE):
                        if dbg:
                            P.wait("vector", f"{tag}_dbg", P.cur(f"{tag}_dbg"))
                        P.wait("sync", "f_rs")
                        P.wait("sync", f"{tag}_ve_lnb", P.cur(f"{tag}_ve_lnb"))
                        P.do("sync",
                             lambda e, tt=tt, rt=rt: e.dma_start(
                                 out=rt.rearrange("p (c t) -> p c t", c=NC), in_=r_view_tile(tt)),
                             inc="f_rl0", n=16)
                        P.wait("tensor", f"{tag}_wl", wl2q)
                        P.wait("tensor", f"{tag}_ve_ro", pv2["ve_ro"])
                        P.wait("tensor", f"{tag}_ve_o", P.cur(f"{tag}_ve_o"))
                        for oc in range(NC):
                            bk = pb[oc % 2]
                            P.wait("tensor", f"{tag}_ve_q",
                                   max(0, P.cur(f"{tag}_ve_q") - 1))
                            for c in range(NC):
                                P.do("tensor",
                                     lambda e, oc=oc, c=c, tt=tt, bk=bk: e.matmul(
                                         bk.ap()[:, :],
                                         wq_i.ap()[:, c * D + oc * 128:c * D + oc * 128 + 128],
                                         lnb_all.ap()[:, c * T + tt * TT:c * T + tt * TT + TT],
                                         start=(c == 0), stop=(c == NC - 1)),
                                     inc=f"{tag}_pe_q" if c == NC - 1 else None)
                            peq = P.cur(f"{tag}_pe_q")
                            P.wait("vector", f"{tag}_pe_q", peq)
                            P.do("vector",
                                 lambda e, oc=oc, bk=bk: e.tensor_copy(
                                     q_fm.ap()[:, oc * TT:(oc + 1) * TT], bk.ap()[:, :]),
                                 inc=f"{tag}_ve_q")
                        veq = P.cur(f"{tag}_ve_q")
                        P.drain("vector")
                        P.wait("vector", f"{tag}_pe_ss", P.cur(f"{tag}_pe_ss"))
                        for c in range(NC):
                            P.do("vector",
                                 lambda e, c=c: e.tensor_mul(
                                     asc.ap()[:, c * TT:(c + 1) * TT],
                                     q_fm.ap()[:, c * TT:(c + 1) * TT],
                                     q_fm.ap()[:, c * TT:(c + 1) * TT]),
                                 inc=f"{tag}_ve_q2" if c == NC - 1 else None)
                        q2_done = P.cur(f"{tag}_ve_q2")
                        P.drain("vector")
                        P.wait("tensor", f"{tag}_ve_q2", q2_done)
                        P.wait("tensor", f"{tag}_sc_qn", P.cur(f"{tag}_sc_qn"))
                        for c in range(NC):
                            P.do("tensor",
                                 lambda e, c=c: e.matmul(
                                     pb[2].ap()[0:16, :],
                                     cbf.ap()[:, CB_BLK16 + c * 16:CB_BLK16 + (c + 1) * 16],
                                     asc.ap()[:, c * TT:(c + 1) * TT],
                                     start=(c == 0), stop=(c == NC - 1)),
                                 inc=f"{tag}_pe_ss" if c == NC - 1 else None)
                        ss_done = P.cur(f"{tag}_pe_ss")
                        P.wait("vector", f"{tag}_pe_ss", ss_done)
                        P.do("vector", lambda e: e.tensor_scalar_max(qnr.ap()[:, :],
                                                                     pb[2].ap()[0:16, :], 1e-18),
                             inc=f"{tag}_ve_ssm")
                        ssm_done = P.cur(f"{tag}_ve_ssm")
                        P.drain("vector")
                        P.wait("scalar", f"{tag}_ve_ssm", ssm_done)
                        P.do("scalar", lambda e: e.activation(qnr.ap()[:, :], qnr.ap()[:, :], Act.Ln))
                        P.drain("scalar")
                        P.do("scalar", lambda e: e.activation(qnr.ap()[:, :], qnr.ap()[:, :],
                                                              Act.Exp, scale=-0.5), inc=f"{tag}_sc_qn")
                        qn_done = P.cur(f"{tag}_sc_qn")
                        P.wait("vector", f"{tag}_sc_qn", qn_done)
                        P.do("vector", lambda e, tt=tt: e.tensor_scalar(
                            fwr.ap()[:, :], prows.ap()[:, tt * TT:(tt + 1) * TT],
                            0.3, 0.5, Alu.mult, Alu.add, accum_out=None))
                        P.drain("vector")
                        P.do("vector", lambda e: e.tensor_mul(rtile.ap()[0:16, :], fwr.ap()[:, :],
                                                              qnr.ap()[:, :]))
                        P.drain("vector")
                        P.do("vector", lambda e: e.tensor_scalar(
                            fwr.ap()[:, :], fwr.ap()[:, :], -1.0, 1.0,
                            Alu.mult, Alu.add, accum_out=None))
                        P.drain("vector")
                        P.do("vector", lambda e: e.tensor_mul(rtile.ap()[32:48, :], fwr.ap()[:, :],
                                                              qnr.ap()[:, :]), inc=f"{tag}_ve_g")
                        g_done = P.cur(f"{tag}_ve_g")
                        P.drain("vector")

                        for blk in range(NBLK):
                            P.wait("tensor", f"{tag}_ve_g", g_done)
                            P.wait("tensor", f"{tag}_ve_gt", P.cur(f"{tag}_ve_gt"))
                            P.wait("tensor", f"{tag}_ve_of", P.cur(f"{tag}_ve_of"))
                            P.do("tensor", lambda e, blk=blk: e.transpose(
                                pb[3].ap()[:, 0:64], rtile.ap()[0:64, blk * 128:(blk + 1) * 128],
                                ident[0:64, 0:64]), inc=f"{tag}_pe_gt")
                            pegt = P.cur(f"{tag}_pe_gt")
                            P.wait("vector", f"{tag}_pe_gt", pegt)
                            P.do("vector", lambda e: e.tensor_copy(gT[:], pb[3].ap()[:, 0:64]),
                                 inc=f"{tag}_ve_gt")
                            P.drain("vector")
                            P.wait("tensor", f"{tag}_ve_gt", P.cur(f"{tag}_ve_gt"))
                            P.wait("tensor", f"{tag}_ve_o", P.cur(f"{tag}_ve_o"))
                            P.wait("tensor", f"{tag}_bd", bd_done)
                            for c in range(NC):
                                qs = q_fm.ap()[:, c * TT + blk * 128:c * TT + blk * 128 + 128]
                                P.do("tensor",
                                     lambda e, c=c, qs=qs: e.matmul(
                                         pb[4 + c // 4].ap()[:, (c % 4) * 128:(c % 4) * 128 + 128],
                                         qs, bdf.ap()[:, c * 128:(c + 1) * 128],
                                         start=True, stop=True))
                                P.do("tensor",
                                     lambda e, c=c, qs=qs: e.matmul(
                                         pb[6 + c // 4].ap()[:, (c % 4) * 128:(c % 4) * 128 + 128],
                                         qs, bds.ap()[:, c * 128:(c + 1) * 128],
                                         start=True, stop=True),
                                     inc=f"{tag}_pe_bd" if c == NC - 1 else None)
                            bd_mm = P.cur(f"{tag}_pe_bd")
                            P.wait("vector", f"{tag}_pe_bd", bd_mm)
                            for half in range(2):
                                P.do("vector", lambda e, half=half: e.tensor_tensor(
                                    o32.ap()[:, half * 512:(half + 1) * 512]
                                       .rearrange("p (h d) -> p h d", d=DK),
                                    pb[4 + half].ap()[:, :].rearrange("p (h d) -> p h d", d=DK),
                                    gT.ap()[:, half * 8:half * 8 + 8].unsqueeze(2)
                                      .broadcast_to((128, 8, DK)), Alu.mult))
                                P.do("vector", lambda e, half=half: e.tensor_tensor(
                                    pb[half].ap()[:, :].rearrange("p (h d) -> p h d", d=DK),
                                    pb[6 + half].ap()[:, :].rearrange("p (h d) -> p h d", d=DK),
                                    gT.ap()[:, 32 + half * 8:32 + half * 8 + 8].unsqueeze(2)
                                      .broadcast_to((128, 8, DK)), Alu.mult))
                            P.drain("vector")
                            for half in range(2):
                                P.do("vector", lambda e, half=half: e.tensor_add(
                                    o32.ap()[:, half * 512:(half + 1) * 512],
                                    o32.ap()[:, half * 512:(half + 1) * 512],
                                    pb[half].ap()[:, :]),
                                    inc=f"{tag}_ve_o" if half == 1 else None)
                            veo = P.cur(f"{tag}_ve_o")
                            P.drain("vector")
                            P.wait("tensor", f"{tag}_ve_o", veo)
                            P.wait("tensor", f"{tag}_ve_of", P.cur(f"{tag}_ve_of"))
                            for c in range(NC):
                                P.do("tensor",
                                     lambda e, c=c: e.transpose(
                                         pb[2 + c // 4].ap()[:, (c % 4) * 128:(c % 4) * 128 + 128],
                                         o32.ap()[:, c * 128:(c + 1) * 128], ident),
                                     inc=f"{tag}_pe_ot" if c == NC - 1 else None)
                            peot = P.cur(f"{tag}_pe_ot")
                            P.wait("vector", f"{tag}_pe_ot", peot)
                            for c in range(NC):
                                P.do("vector",
                                     lambda e, c=c, blk=blk: e.tensor_copy(
                                         asc.ap()[:, c * TT + blk * 128:c * TT + blk * 128 + 128],
                                         pb[2 + c // 4].ap()[:, (c % 4) * 128:(c % 4) * 128 + 128]),
                                     inc=f"{tag}_ve_of" if c == NC - 1 else None)
                            P.drain("vector")

                        of_all = P.cur(f"{tag}_ve_of")
                        if dbg and tt == 0:
                            P.wait("gpsimd", f"{tag}_ve_q", P.cur(f"{tag}_ve_q"))
                            P.do("gpsimd", lambda e: e.dma_start(out=d_qfm, in_=q_fm[:]),
                                 inc=f"{tag}_dbg", n=16)
                            P.wait("gpsimd", f"{tag}_ve_of", of_all)
                            P.do("gpsimd", lambda e: e.dma_start(out=d_ofm, in_=asc[:]),
                                 inc=f"{tag}_dbg", n=16)
                        P.wait("tensor", f"{tag}_wl", wl2)
                        P.wait("tensor", f"{tag}_ve_of", of_all)
                        for oc in range(NC):
                            bk = pb[oc % 2]
                            P.wait("tensor", f"{tag}_ve_ro", max(0, P.cur(f"{tag}_ve_ro") - 1))
                            for c in range(NC):
                                P.do("tensor",
                                     lambda e, oc=oc, c=c, bk=bk: e.matmul(
                                         bk.ap()[:, :],
                                         wo_i.ap()[:, c * D + oc * 128:c * D + oc * 128 + 128],
                                         asc.ap()[:, c * TT:(c + 1) * TT],
                                         start=(c == 0), stop=(c == NC - 1)),
                                     inc=f"{tag}_pe_wo" if c == NC - 1 else None)
                            pewo = P.cur(f"{tag}_pe_wo")
                            P.wait("vector", f"{tag}_pe_wo", pewo)
                            P.wait("vector", "f_rl0", s2rl_base + 16 * (tt + 1))
                            P.do("vector",
                                 lambda e, oc=oc, bk=bk, rt=rt: e.tensor_add(
                                     rt[:, oc * TT:(oc + 1) * TT], rt[:, oc * TT:(oc + 1) * TT],
                                     bk.ap()[:, :]),
                                 inc=f"{tag}_ve_ro")
                            P.drain("vector")
                        pv2["ve_ro"] = P.cur(f"{tag}_ve_ro")
                        P.wait("scalar", f"{tag}_ve_ro", pv2["ve_ro"])
                        P.do("scalar",
                             lambda e, tt=tt, rt=rt: e.dma_start(
                                 out=r_view_tile(tt), in_=rt.rearrange("p (c t) -> p c t", c=NC)),
                             inc="f_rs", n=16)

                P.wait("sync", "f_rs")

        if "p0" in enable:
            transpose_pipe(
                "p0", pb[0:4],
                lambda e, blk, buf: e.dma_start(out=buf, in_=x_in[blk * 128:(blk + 1) * 128, :]),
                lambda e, blk, buf: e.dma_start(out=r_view_blk(blk),
                                                in_=buf.rearrange("p (c t) -> p c t", c=NC)),
            )
            P.fence()

        if "ffn1" in enable:
            ffn_phase("f1", first=True)
            P.fence()

        if "attn" in enable:
            attn_phase()
            P.fence()

        if "ffn2" in enable:
            ffn_phase("f2")
            P.fence()

        if "p4" in enable:
            transpose_pipe(
                "p4", pb[0:4],
                lambda e, blk, buf: e.dma_start(out=buf.rearrange("p (c t) -> p c t", c=NC),
                                                in_=r_view_blk(blk)),
                lambda e, blk, buf: e.dma_start(out=y_out[blk * 128:(blk + 1) * 128, :], in_=buf),
            )

        P.wait("sync", "tp_st0")
        P.wait("sync", "tp_st1")

        block = es.enter_context(nc.Block())
        P.emit(block)

    return nc



def _prep_inputs(L0_attn_x, L0_attn_Wq, L0_attn_Wk, L0_attn_Wv, L0_attn_Wb,
                 L0_attn_Wo, L0_attn_Wfd, L0_attn_bfd, L0_attn_Wsd, L0_attn_bsd,
                 L0_attn_Wtf1, L0_attn_btf1, L0_attn_Wtf2, L0_attn_btf2,
                 L0_ffn_Ws1, L0_ffn_Ws2, L0_ffn_W1, L0_ffn_W2, L0_ffn_Wr):
    f32 = np.float32

    def img_kc(W):
        K, M = W.shape
        return np.ascontiguousarray(
            W.reshape(K // 128, 128, M).transpose(1, 0, 2).reshape(128, (K // 128) * M)
        ).astype(f32)

    wq = img_kc(np.asarray(L0_attn_Wq))
    wk = img_kc(np.asarray(L0_attn_Wk))
    wv = img_kc(np.asarray(L0_attn_Wv))
    wo = img_kc(np.asarray(L0_attn_Wo))
    wbf = img_kc(np.asarray(L0_attn_Wb))
    wfdf = img_kc(np.asarray(L0_attn_Wfd))
    wsdf = img_kc(np.asarray(L0_attn_Wsd))
    wrf = img_kc(np.asarray(L0_ffn_Wr))

    W1 = np.asarray(L0_ffn_W1)
    Ws1 = np.asarray(L0_ffn_Ws1)
    w1all = np.concatenate([W1, Ws1[None]], axis=0)
    w1f = np.ascontiguousarray(
        w1all.reshape(9, NC, 128, ED).transpose(2, 0, 1, 3).reshape(128, 9 * NC * ED)
    ).astype(f32)

    W2 = np.asarray(L0_ffn_W2)
    Ws2 = np.asarray(L0_ffn_Ws2)
    w2all = np.concatenate([W2, Ws2[None]], axis=0)
    w2f = np.ascontiguousarray(
        w2all.reshape(9, 4, 128, D).transpose(2, 0, 1, 3).reshape(128, 9 * 4 * D)
    ).astype(f32)

    Wtf1 = np.asarray(L0_attn_Wtf1).astype(f32)
    wtf1f = np.concatenate([Wtf1, Wtf1], axis=0)

    Wtf2 = np.asarray(L0_attn_Wtf2).astype(f32).reshape(32)
    bd = np.zeros((512, 16), f32)
    for h in range(16):
        bd[h * 32:(h + 1) * 32, h] = Wtf2
    wtf2f = np.ascontiguousarray(
        bd.reshape(4, 128, 16).transpose(1, 0, 2).reshape(128, 64)
    )

    consts = np.zeros((128, CW), f32)
    consts[:, C_ID:C_ID + 128] = np.eye(128, dtype=f32)
    consts[:, C_ONES:C_ONES + 128] = 1.0
    consts[:, C_BFD] = np.tile(np.asarray(L0_attn_bfd).astype(f32), 8)
    consts[:, C_BSD] = np.tile(np.asarray(L0_attn_bsd).astype(f32), 8)
    consts[:, C_BTF1] = np.tile(np.asarray(L0_attn_btf1).astype(f32), 4)
    consts[:, C_BTF2] = float(np.asarray(L0_attn_btf2).reshape(-1)[0])
    consts[:, C_EPSLN] = EPS_LN
    consts[:, C_ZERO] = 0.0
    consts[:, C_EPS6] = 1e-6
    consts[:, C_EPS18] = 1e-18

    cbf = np.zeros((128, CBW), ml_dtypes.bfloat16)
    cbf[:, CB_ONESCOL] = 1.0
    cbf[0:64, CB_BLK] = 1.0
    cbf[64:128, CB_BLK + 1] = 1.0
    cbf[:, CB_ONESROW:CB_ONESROW + 128] = 1.0
    for e in range(NE):
        cbf[e, CB_SEL8 + e * 128:CB_SEL8 + (e + 1) * 128] = 1.0
    for c in range(NC):
        cbf[0:64, CB_BLK16 + c * 16 + 2 * c] = 1.0
        cbf[64:128, CB_BLK16 + c * 16 + 2 * c + 1] = 1.0

    x = np.asarray(L0_attn_x).astype(f32)
    weights = dict(wq=wq, wk=wk, wv=wv, wo=wo, w1f=w1f, w2f=w2f, wrf=wrf,
                   wbf=wbf, wfdf=wfdf, wsdf=wsdf, wtf1f=wtf1f, wtf2f=wtf2f,
                   consts=consts, cbf=cbf)
    shards = []
    for c in range(NCORE):
        b, h = c // 2, c % 2
        shard = np.ascontiguousarray(x[b, h * T:(h + 1) * T, :])
        m = dict(weights)
        m["x"] = shard
        shards.append(m)
    return shards


def _run(inputs, enable=("p0", "ffn1", "attn", "ffn2", "p4")):
    in_maps = _prep_inputs(**inputs)
    nc = build_program(enable)
    res = run_bass_kernel_spmd(nc, in_maps, core_ids=list(range(NCORE)))
    out = np.empty((B, L, D), np.float32)
    for c in range(NCORE):
        b, h = c // 2, c % 2
        out[b, h * T:(h + 1) * T, :] = res.results[c]["y"]
    return out


def kernel(**inputs):
    return _run(inputs)


# version 07
# speedup vs baseline: 1.0927x; 1.0157x over previous
import numpy as np
import ml_dtypes
import concourse.bass as bass
import concourse.mybir as mybir
from concourse.bass_utils import run_bass_kernel_spmd

F32 = mybir.dt.float32
BF16 = mybir.dt.bfloat16
Act = mybir.ActivationFunctionType
Alu = mybir.AluOpType

B, L, D = 4, 8192, 1024
H, DK = 16, 64
ED, NE = 512, 8
NCORE = 8
T = 4096
TT = 512
NTILE = T // TT
NBLK = TT // 128
NC = 8
EPS_LN = 1e-5
W2_SCRATCH = False

ENGS = ("sync", "gpsimd", "tensor", "vector", "scalar")

C_ID = 0
C_ONES = 128
C_BFD = 256
C_BSD = 257
C_BTF1 = 258
C_BTF2 = 259
C_EPSLN = 260
C_ZERO = 261
C_EPS6 = 262
C_EPS18 = 263
CW = 288

CB_ONESCOL = 0
CB_BLK = 8
CB_ONESROW = 24
CB_SEL8 = 152
CB_BLK16 = 152 + 8 * 128
CBW = CB_BLK16 + 8 * 16


class Prog:

    def __init__(self, nc, es):
        self.nc = nc
        self.es = es
        self.ops = {e: [] for e in ENGS}
        self.sems = {}
        self.cnt = {}
        self.dma_sems = set()
        self._hw = {e: {} for e in ENGS}

    def sem(self, name):
        if name not in self.sems:
            self.sems[name] = self.es.enter_context(self.nc.semaphore(name))
            self.cnt[name] = 0
        return name

    def cur(self, name):
        self.sem(name)
        return self.cnt[name]

    def wait(self, eng, sem, val=None):
        self.sem(sem)
        v = self.cnt[sem] if val is None else val
        if v <= 0:
            return
        if self._hw[eng].get(sem, 0) >= v:
            return
        self._hw[eng][sem] = v
        self.ops[eng].append(["w", sem, v])

    def drain(self, eng):
        ops = self.ops[eng]
        if ops and ops[-1][0] == "d":
            return
        ops.append(["d"])

    def do(self, eng, fn, inc=None, n=1):
        if inc is not None:
            self.sem(inc)
            self.cnt[inc] += n
            if n == 16:
                self.dma_sems.add(inc)
        self.ops[eng].append(["o", fn, inc, n, self.cnt[inc] if inc else 0])
        return self.cnt[inc] if inc is not None else None

    def fence(self, tag=""):
        rel = {}
        for e in ENGS:
            for it in reversed(self.ops[e]):
                if it[0] == "o":
                    if it[2] is None:
                        self.sem(f"fence_{e}")
                        self.cnt[f"fence_{e}"] += 1
                        it[2] = f"fence_{e}"
                        it[3] = 1
                        it[4] = self.cnt[f"fence_{e}"]
                    rel[e] = (it[2], it[4])
                    break
        for e in ENGS:
            for o in ENGS:
                if o != e and o in rel:
                    self.wait(e, rel[o][0], rel[o][1])
            for s in sorted(self.dma_sems):
                self.wait(e, s)

    def emit(self, block):
        for eng_name in ENGS:
            ops = self.ops[eng_name]

            def make(ops=ops):
                def section(e):
                    for item in ops:
                        k = item[0]
                        if k == "w":
                            e.wait_ge(self.sems[item[1]], item[2])
                        elif k == "d":
                            e.drain()
                        else:
                            inst = item[1](e)
                            if item[2] is not None:
                                inst.then_inc(self.sems[item[2]], item[3])
                return section

            getattr(block, eng_name)(make())


def build_program(enable=("p0", "ffn1", "attn", "ffn2", "p4"), n_pairs=4):
    nc = bass.Bass("TRN2", target_bir_lowering=False)

    x_in = nc.dram_tensor("x", [T, D], F32, kind="ExternalInput").ap()
    y_out = nc.dram_tensor("y", [T, D], F32, kind="ExternalOutput").ap()

    wimg = {}
    for name in ("wq", "wk", "wv", "wo"):
        wimg[name] = nc.dram_tensor(name, [128, NC * D], F32, kind="ExternalInput").ap()
    w1f = nc.dram_tensor("w1f", [128, 9 * NC * ED], F32, kind="ExternalInput").ap()
    w2f = nc.dram_tensor("w2f", [128, 9 * 4 * D], F32, kind="ExternalInput").ap()
    wrf = nc.dram_tensor("wrf", [128, NC * NE], F32, kind="ExternalInput").ap()
    wbf = nc.dram_tensor("wbf", [128, NC * H], F32, kind="ExternalInput").ap()
    wfdf = nc.dram_tensor("wfdf", [128, NC * H], F32, kind="ExternalInput").ap()
    wsdf = nc.dram_tensor("wsdf", [128, NC * H], F32, kind="ExternalInput").ap()
    wtf1f = nc.dram_tensor("wtf1f", [128, 32], F32, kind="ExternalInput").ap()
    wtf2f = nc.dram_tensor("wtf2f", [128, 4 * H], F32, kind="ExternalInput").ap()
    consts = nc.dram_tensor("consts", [128, CW], F32, kind="ExternalInput").ap()
    cbf_in = nc.dram_tensor("cbf", [128, CBW], BF16, kind="ExternalInput").ap()

    r_sc = nc.dram_tensor("r_sc", [128, NC * T], F32)
    w2bf = nc.dram_tensor("w2bf", [128, 9 * 4 * D], BF16)

    cc_in = nc.dram_tensor("cc_in", [128, 1028], F32)
    cc_out = nc.dram_tensor("cc_out", [256, 1028], F32)

    dbg = "dbg" in enable
    if dbg:
        d_flux = nc.dram_tensor("d_flux", [16, T], F32, kind="ExternalOutput").ap()
        d_facc = nc.dram_tensor("d_facc", [16, 8], F32, kind="ExternalOutput").ap()
        d_accM = nc.dram_tensor("d_accM", [128, 1024], F32, kind="ExternalOutput").ap()
        d_mn = nc.dram_tensor("d_mn", [128, 1024], F32, kind="ExternalOutput").ap()
        d_qfm = nc.dram_tensor("d_qfm", [128, NC * TT], F32, kind="ExternalOutput").ap()
        d_ofm = nc.dram_tensor("d_ofm", [128, NC * TT], F32, kind="ExternalOutput").ap()
        d_mb = nc.dram_tensor("d_mb", [128, 1024], F32, kind="ExternalOutput").ap()
        d_rab = nc.dram_tensor("d_rab", [16, 8], F32, kind="ExternalOutput").ap()
        d_dcol = nc.dram_tensor("d_dcol", [16, 4], F32, kind="ExternalOutput").ap()
        d_ebc = nc.dram_tensor("d_ebc", [128, 16], F32, kind="ExternalOutput").ap()
        d_dbc = nc.dram_tensor("d_dbc", [128, 16], F32, kind="ExternalOutput").ap()

    from contextlib import ExitStack
    with ExitStack() as es:
        def sb(name, shape, dt):
            return es.enter_context(nc.sbuf_tensor(name, shape, dt))

        P = Prog(nc, es)

        cst = sb("cst", [128, CW], F32)
        cbf = sb("cbfs", [128, CBW], BF16)
        ident = cst.ap()[:, C_ID:C_ID + 128]
        ones_col_f32 = cst.ap()[:, C_ONES:C_ONES + 1]
        ones_row_f32 = cst.ap()[0:1, C_ONES:C_ONES + 128]
        ones_row_bf = cbf.ap()[0:1, CB_ONESROW:CB_ONESROW + 128]
        pb = [es.enter_context(nc.psum_tensor(f"bank{i}", [128, 512], F32)) for i in range(8)]

        P.do("sync", lambda e: e.dma_start(out=cst[:], in_=consts), inc="cl", n=16)
        P.do("sync", lambda e: e.dma_start(out=cbf[:], in_=cbf_in), inc="cl", n=16)

        def r_view_blk(blk):
            return r_sc.ap().rearrange("p (c t) -> p c t", c=NC)[:, :, blk * 128:(blk + 1) * 128]

        def r_view_tile(tt):
            return r_sc.ap().rearrange("p (c t) -> p c t", c=NC)[:, :, tt * TT:(tt + 1) * TT]

        from contextlib import ExitStack as _ES0

        def transpose_pipe(tag, banks, src_dma, dst_dma):
            sbtag, tag = tag, "tp"
            with _ES0() as tpes:
                a_buf = tpes.enter_context(nc.sbuf_tensor(f"{sbtag}_a", [128, 2 * D], F32))
                b_buf = tpes.enter_context(nc.sbuf_tensor(f"{sbtag}_b", [128, 2 * D], F32))
                for blk in range(T // 128):
                    par = blk % 2
                    asl = a_buf.ap()[:, par * D:(par + 1) * D]
                    bsl = b_buf.ap()[:, par * D:(par + 1) * D]
                    P.wait("sync", f"{tag}_pe{par}")
                    P.do("sync", lambda e, blk=blk, asl=asl: src_dma(e, blk, asl),
                         inc=f"{tag}_ld{par}", n=16)
                    P.wait("tensor", f"{tag}_ld{par}")
                    P.wait("tensor", "cl", 32)
                    P.wait("tensor", f"{tag}_ve{par}")
                    for c in range(NC):
                        pt = banks[par * 2 + c // 4].ap()[:, (c % 4) * 128:(c % 4) * 128 + 128]
                        P.do("tensor",
                             lambda e, pt=pt, asl=asl, c=c: e.transpose(pt, asl[:, c * 128:(c + 1) * 128], ident),
                             inc=f"{tag}_pe{par}")
                    P.wait("vector", f"{tag}_pe{par}")
                    P.wait("vector", f"{tag}_st{par}")
                    for c in range(NC):
                        pt = banks[par * 2 + c // 4].ap()[:, (c % 4) * 128:(c % 4) * 128 + 128]
                        P.do("vector",
                             lambda e, pt=pt, bsl=bsl, c=c: e.tensor_copy(bsl[:, c * 128:(c + 1) * 128], pt),
                             inc=f"{tag}_ve{par}" if c == NC - 1 else None)
                    P.wait("scalar", f"{tag}_ve{par}")
                    P.do("scalar", lambda e, blk=blk, bsl=bsl: dst_dma(e, blk, bsl),
                         inc=f"{tag}_st{par}", n=16)
                P.wait("sync", f"{tag}_st0")
                P.wait("sync", f"{tag}_st1")

        from contextlib import ExitStack as _ES

        def ffn_phase(tag, first=False):
            pes = _ES()
            with pes:
                def psb(name, shape, dt):
                    return pes.enter_context(nc.sbuf_tensor(name, shape, dt))
                sbtag, tag = tag, "f"

                w1i = psb(f"{sbtag}_w1i", [128, 9 * NC * ED], BF16)
                wri = psb(f"{sbtag}_wri", [128, NC * NE], BF16)
                w2b = psb(f"{sbtag}_w2b", [128, 2 * 4 * D], BF16)
                r_t = psb(f"{sbtag}_rt", [128, 2 * NC * TT], F32)
                lnb = psb(f"{sbtag}_lnb", [128, NC * TT], BF16)
                probb = psb(f"{sbtag}_probb", [128, NE * TT], BF16)
                h_sb = psb(f"{sbtag}_h", [128, 9 * 4 * TT], BF16)
                tmpa = psb(f"{sbtag}_tmpa", [128, 4 * TT], BF16)
                invs = psb(f"{sbtag}_invs", [128, 2 * TT], F32)
                ltm = psb(f"{sbtag}_ltm", [128, 64], F32)
                prow = psb(f"{sbtag}_prow", [8, TT], BF16)
                tmpo = psb(f"{sbtag}_tmpo", [128, TT], F32)
                mu_row = psb(f"{sbtag}_mu", [1, TT], F32)
                ex2_row = psb(f"{sbtag}_ex2", [1, TT], F32)
                inv_row = psb(f"{sbtag}_inv", [1, TT], F32)
                m1v = psb(f"{sbtag}_m1", [128, 8], F32)

                rb = lnb.ap()[:, :]
                sq = probb.ap()[:, :]
                invb_sb = invs.ap()[:, 0:TT]
                inv2h_sb = invs.ap()[:, TT:2 * TT]

                wl_base0 = P.cur(f"{tag}_wl")
                P.do("gpsimd", lambda e: e.dma_start(out=wri[:], in_=wrf), inc=f"{tag}_wl", n=16)
                w1_order = [8, 0, 1, 2, 3, 4, 5, 6, 7]
                wl_at = {}
                for pos, ee in enumerate(w1_order):
                    P.wait("gpsimd", f"{tag}_wl")
                    P.do("gpsimd",
                         lambda e, ee=ee: e.dma_start(
                             out=w1i.ap()[:, ee * NC * ED:(ee + 1) * NC * ED],
                             in_=w1f[:, ee * NC * ED:(ee + 1) * NC * ED]),
                         inc=f"{tag}_wl", n=16)
                    wl_at[ee] = P.cur(f"{tag}_wl")
                wl_done = P.cur(f"{tag}_wl")
                wl_router = wl_base0 + 16

                w2cnt = [0]
                grel = []
                w2l_base = [P.cur(f"{tag}_w2l0"), P.cur(f"{tag}_w2l1")]
                w2ls_base = [P.cur(f"{tag}_w2ls0"), P.cur(f"{tag}_w2ls1")]
                w2_src = {}
                w2w_base = P.cur(f"{tag}_w2w")

                def w2_load():
                    g_idx = w2cnt[0]
                    par = g_idx % 2
                    e_idx = g_idx % 9
                    if g_idx >= 2:
                        s, v = grel[g_idx - 2]
                        P.wait("gpsimd", s, v)
                    if first:
                        P.wait("gpsimd", "w2cv", 16 * (e_idx + 1))
                    P.do("gpsimd",
                         lambda e, par=par, e_idx=e_idx: e.dma_start(
                             out=w2b.ap()[:, par * 4 * D:(par + 1) * 4 * D],
                             in_=w2bf[:, e_idx * 4 * D:(e_idx + 1) * 4 * D]),
                         inc=f"{tag}_w2l{par}", n=16)
                    w2_src[g_idx] = (f"{tag}_w2l{par}", P.cur(f"{tag}_w2l{par}"))
                    w2cnt[0] += 1

                w2_load()
                w2_load()
                prev_veout = 0
                prev_veout_b = 0
                prev_peout = 0
                e_order = [8, 0, 1, 2, 3, 4, 5, 6, 7]
                tile_ctx = {}

                def h_stage(e_i, idx):
                    bks = pb[0:4] if idx % 2 == 0 else pb[4:8]
                    P.wait("tensor", f"{tag}_wl", tile_ctx["wl_at"][e_i])
                    P.wait("tensor", f"{tag}_sc_relu", max(0, P.cur(f"{tag}_sc_relu") - 1))
                    if idx % 2 == 0:
                        P.wait("tensor", f"{tag}_ve_var", tile_ctx["var_done"])
                        if idx != 0:
                            P.wait("tensor", f"{tag}_ve_lnb", tile_ctx["lnb_done"])
                    else:
                        P.wait("tensor", f"{tag}_ve_prow", tile_ctx["prow_done"])
                        P.wait("tensor", f"{tag}_b5free", P.cur(f"{tag}_b5free"))
                        P.wait("tensor", f"{tag}_b67free0", P.cur(f"{tag}_b67free0"))
                        P.wait("tensor", f"{tag}_b67free1", P.cur(f"{tag}_b67free1"))
                    lnb = tile_ctx["lnb"]
                    w1i = tile_ctx["w1i"]
                    tmpa = tile_ctx["tmpa"]
                    probb = tile_ctx["probb"]
                    h_sb = tile_ctx["h_sb"]
                    for hc in range(4):
                        for c in range(NC):
                            col = (e_i * NC + c) * ED + hc * 128
                            if idx == 0 and hc == 0:
                                P.wait("tensor", f"{tag}_ve_lnb", tile_ctx["lnb_base"] + c + 1)
                            P.do("tensor",
                                 lambda e, hc=hc, col=col, bks=bks, c=c: e.matmul(
                                     bks[hc].ap()[:, :],
                                     w1i.ap()[:, col:col + 128],
                                     lnb.ap()[:, c * TT:(c + 1) * TT],
                                     start=(c == 0), stop=(c == NC - 1)),
                                 inc=f"{tag}_pe_h" if (hc == 3 and c == NC - 1) else None)
                    peh_done = P.cur(f"{tag}_pe_h")
                    P.wait("scalar", f"{tag}_pe_h", peh_done)
                    P.wait("scalar", f"{tag}_ve_h", P.cur(f"{tag}_ve_h"))
                    for hc in range(4):
                        P.do("scalar",
                             lambda e, hc=hc, bks=bks: e.activation(
                                 tmpa.ap()[:, hc * TT:(hc + 1) * TT], bks[hc].ap()[:, :], Act.Relu),
                             inc=f"{tag}_sc_relu" if hc == 3 else None)
                    relu_done = P.cur(f"{tag}_sc_relu")
                    P.wait("vector", f"{tag}_sc_relu", relu_done)
                    hsl = h_sb.ap()[:, e_i * 4 * TT:(e_i + 1) * 4 * TT]
                    if e_i < NE:
                        pbb = probb.ap()[:, e_i * TT:(e_i + 1) * TT].unsqueeze(1) \
                                   .broadcast_to((128, 4, TT))
                    else:
                        pbb = inv2h_sb.unsqueeze(1).broadcast_to((128, 4, TT))
                    P.do("vector",
                         lambda e, pbb=pbb, hsl=hsl: e.tensor_tensor(
                             hsl.rearrange("p (a t) -> p a t", a=4),
                             tmpa.ap()[:, :].rearrange("p (a t) -> p a t", a=4),
                             pbb, Alu.mult))
                    P.drain("vector")
                    P.do("vector",
                         lambda e, hsl=hsl: e.tensor_mul(hsl, hsl, tmpa.ap()[:, :]),
                         inc=f"{tag}_ve_h")
                    P.drain("vector")

                rs_base = P.cur(f"{tag}_rs")
                rl_base = [P.cur(f"{tag}_rl0"), P.cur(f"{tag}_rl1")]

                def r_load(tt):
                    par = tt % 2
                    rt_ = r_t.ap()[:, par * NC * TT:(par + 1) * NC * TT]
                    P.wait("sync", f"{tag}_rs", rs_base + 16 * max(0, tt - 1))
                    P.do("sync",
                         lambda e, tt=tt, rt_=rt_: e.dma_start(
                             out=rt_.rearrange("p (c t) -> p c t", c=NC), in_=r_view_tile(tt)),
                         inc=f"{tag}_rl{par}", n=16)
                    return rt_

                def s1_rbsq(tt, rt_, peh_guard, veh_guard):
                    par = tt % 2
                    P.wait("vector", f"{tag}_rl{par}", rl_base[par] + 16 * (tt // 2 + 1))
                    P.wait("vector", f"{tag}_pe_h", peh_guard)
                    P.wait("vector", f"{tag}_ve_h", veh_guard)
                    P.do("vector", lambda e, rt_=rt_: e.tensor_copy(rb, rt_))
                    P.drain("vector")
                    P.do("vector", lambda e: e.tensor_mul(sq, rb, rb), inc=f"{tag}_ve_sq")
                    return P.cur(f"{tag}_ve_sq")

                rt_next = r_load(0)
                sq_next = s1_rbsq(0, rt_next, 0, 0)
                for tt in range(NTILE):
                    rt = rt_next
                    sq_done = sq_next

                    P.wait("tensor", "cl", 32)
                    P.wait("tensor", f"{tag}_ve_sq", sq_done)
                    P.wait("tensor", f"{tag}_ve_out", min(prev_veout, prev_veout_b + 2))
                    for c in range(NC):
                        P.do("tensor",
                             lambda e, c=c: e.matmul(pb[0].ap()[0:1, :],
                                                     cbf.ap()[:, CB_ONESCOL:CB_ONESCOL + 1],
                                                     rb[:, c * TT:(c + 1) * TT],
                                                     start=(c == 0), stop=(c == NC - 1)))
                    for c in range(NC):
                        P.do("tensor",
                             lambda e, c=c: e.matmul(pb[1].ap()[0:1, :],
                                                     cbf.ap()[:, CB_ONESCOL:CB_ONESCOL + 1],
                                                     sq[:, c * TT:(c + 1) * TT],
                                                     start=(c == 0), stop=(c == NC - 1)),
                             inc=f"{tag}_pe_st" if c == NC - 1 else None)
                    stats_done = P.cur(f"{tag}_pe_st")

                    P.wait("vector", f"{tag}_pe_st", stats_done)
                    P.do("vector", lambda e: e.tensor_scalar_mul(mu_row[:], pb[0].ap()[0:1, :], 1.0 / D))
                    P.do("vector", lambda e: e.tensor_scalar_mul(ex2_row[:], pb[1].ap()[0:1, :], 1.0 / D))
                    P.drain("vector")
                    P.do("vector", lambda e: e.tensor_mul(tmpo.ap()[0:1, :], mu_row[:], mu_row[:]))
                    P.drain("vector")
                    P.do("vector", lambda e: e.tensor_sub(ex2_row[:], ex2_row[:], tmpo.ap()[0:1, :]),
                         inc=f"{tag}_ve_var")
                    var_done = P.cur(f"{tag}_ve_var")
                    P.wait("scalar", f"{tag}_ve_var", var_done)
                    P.do("scalar", lambda e: e.activation(ex2_row[:], ex2_row[:], Act.Ln,
                                                          bias=cst.ap()[0:1, C_EPSLN:C_EPSLN + 1]))
                    P.drain("scalar")
                    P.do("scalar", lambda e: e.activation(inv_row[:], ex2_row[:], Act.Exp, scale=-0.5),
                         inc=f"{tag}_sc_inv")
                    inv_done = P.cur(f"{tag}_sc_inv")

                    P.wait("tensor", f"{tag}_ve_var", var_done)
                    P.wait("tensor", f"{tag}_sc_inv", inv_done)
                    P.wait("tensor", f"{tag}_ve_out", min(prev_veout, prev_veout_b + 4))
                    P.do("tensor", lambda e: e.matmul(pb[2].ap()[:, :], ones_row_f32, mu_row[:],
                                                      start=True, stop=True))
                    P.do("tensor", lambda e: e.matmul(pb[3].ap()[:, :], ones_row_f32, inv_row[:],
                                                      start=True, stop=True), inc=f"{tag}_pe_bc")
                    bc_done = P.cur(f"{tag}_pe_bc")

                    P.wait("vector", f"{tag}_pe_bc", bc_done)
                    P.wait("vector", f"{tag}_pe_st", stats_done)
                    P.do("vector", lambda e: e.tensor_copy(invb_sb, pb[3].ap()[:, :]))
                    P.drain("vector")
                    P.do("vector", lambda e: e.scalar_tensor_tensor(inv2h_sb, pb[3].ap()[:, :], 0.5,
                                                                    invb_sb, Alu.mult, Alu.mult))
                    lnb_base = P.cur(f"{tag}_ve_lnb")
                    for c in range(NC):
                        P.do("vector",
                             lambda e, c=c, rt=rt: e.tensor_sub(lnb.ap()[:, c * TT:(c + 1) * TT],
                                                                rt[:, c * TT:(c + 1) * TT],
                                                                pb[2].ap()[:, :]),
                             inc=f"{tag}_ve_lnb")
                        P.drain("vector")
                    lnb_done = P.cur(f"{tag}_ve_lnb")
                    tile_ctx.update(wl_done=wl_done, wl_at=wl_at, var_done=var_done,
                                    lnb_done=lnb_done, lnb_base=lnb_base, lnb=lnb, w1i=w1i,
                                    tmpa=tmpa, probb=probb, h_sb=h_sb)

                    P.wait("tensor", f"{tag}_wl", wl_router)
                    P.wait("tensor", f"{tag}_ve_out", min(prev_veout, prev_veout_b + 5))
                    for c in range(NC):
                        P.wait("tensor", f"{tag}_ve_lnb", lnb_base + c + 1)
                        P.do("tensor",
                             lambda e, c=c: e.matmul(pb[4].ap()[0:8, :], wri.ap()[:, c * NE:(c + 1) * NE],
                                                     lnb.ap()[:, c * TT:(c + 1) * TT],
                                                     start=(c == 0), stop=(c == NC - 1)),
                             inc=f"{tag}_pe_lg" if c == NC - 1 else None)
                    lg_done = P.cur(f"{tag}_pe_lg")
                    h_stage(8, 0)
                    P.wait("vector", f"{tag}_pe_lg", lg_done)
                    P.drain("vector")
                    P.do("vector", lambda e: e.tensor_mul(tmpo.ap()[0:8, :], pb[4].ap()[0:8, :],
                                                          invb_sb[0:8, :]),
                         inc=f"{tag}_ve_lg")
                    velg_done = P.cur(f"{tag}_ve_lg")
                    P.wait("tensor", f"{tag}_ve_lg", velg_done)
                    P.wait("tensor", f"{tag}_b5free", P.cur(f"{tag}_b5free"))
                    P.wait("tensor", f"{tag}_ve_out", min(prev_veout, prev_veout_b + 6))
                    for j in range(4):
                        P.do("tensor",
                             lambda e, j=j: e.transpose(pb[5].ap()[:, j * 8:(j + 1) * 8],
                                                        tmpo.ap()[0:8, j * 128:(j + 1) * 128],
                                                        ident[0:8, 0:8]),
                             inc=f"{tag}_pe_lt" if j == 3 else None)
                    lt_done = P.cur(f"{tag}_pe_lt")
                    P.wait("vector", f"{tag}_pe_lt", lt_done)
                    P.do("vector", lambda e: e.tensor_copy(ltm.ap()[:, 0:32], pb[5].ap()[:, 0:32]),
                         inc=f"{tag}_b5free")
                    P.drain("vector")
                    lv = ltm.ap()[:, 0:32].rearrange("p (a b) -> p a b", b=8)
                    mask = ltm.ap()[:, 32:64]
                    P.do("vector", lambda e: e.tensor_reduce(m1v.ap()[:, 0:4], lv,
                                                             mybir.AxisListType.X, Alu.max))
                    P.drain("vector")
                    m1rep = m1v.ap()[:, 0:4].unsqueeze(2).broadcast_to((128, 4, 8))
                    P.do("vector", lambda e: e.tensor_tensor(
                        mask.rearrange("p (a b) -> p a b", b=8), lv, m1rep, Alu.is_equal))
                    P.drain("vector")
                    P.do("vector", lambda e: e.scalar_tensor_tensor(
                        ltm.ap()[:, 0:32], mask, -1e30, ltm.ap()[:, 0:32], Alu.mult, Alu.add))
                    P.drain("vector")
                    P.do("vector", lambda e: e.tensor_reduce(
                        m1v.ap()[:, 4:8], ltm.ap()[:, 0:32].rearrange("p (a b) -> p a b", b=8),
                        mybir.AxisListType.X, Alu.max))
                    P.drain("vector")
                    P.do("vector", lambda e: e.tensor_sub(m1v.ap()[:, 4:8], m1v.ap()[:, 4:8],
                                                          m1v.ap()[:, 0:4]),
                         inc=f"{tag}_ve_m2")
                    m2_done = P.cur(f"{tag}_ve_m2")
                    P.wait("scalar", f"{tag}_ve_m2", m2_done)
                    P.do("scalar", lambda e: e.activation(m1v.ap()[:, 4:8], m1v.ap()[:, 4:8], Act.Exp),
                         inc=f"{tag}_sc_g")
                    g_done = P.cur(f"{tag}_sc_g")
                    P.wait("vector", f"{tag}_sc_g", g_done)
                    P.do("vector", lambda e: e.tensor_scalar_add(m1v.ap()[:, 4:8], m1v.ap()[:, 4:8], 1.0))
                    P.drain("vector")
                    P.do("vector", lambda e: e.reciprocal(m1v.ap()[:, 0:4], m1v.ap()[:, 4:8]))
                    P.drain("vector")
                    P.do("vector", lambda e: e.tensor_single_scalar(m1v.ap()[:, 0:4], m1v.ap()[:, 0:4],
                                                                    1.0, Alu.is_equal))
                    P.drain("vector")
                    is1rep = m1v.ap()[:, 0:4].unsqueeze(2).broadcast_to((128, 4, 8))
                    P.do("vector", lambda e: e.tensor_tensor(
                        ltm.ap()[:, 0:32].rearrange("p (a b) -> p a b", b=8),
                        mask.rearrange("p (a b) -> p a b", b=8), is1rep, Alu.mult),
                        inc=f"{tag}_ve_ptm")
                    ptm_done = P.cur(f"{tag}_ve_ptm")
                    P.wait("tensor", f"{tag}_ve_ptm", ptm_done)
                    for j in range(4):
                        P.do("tensor",
                             lambda e, j=j: e.transpose(pb[4].ap()[0:8, j * 128:(j + 1) * 128],
                                                        ltm.ap()[:, j * 8:(j + 1) * 8], ident),
                             inc=f"{tag}_pe_pt" if j == 3 else None)
                    pt_done = P.cur(f"{tag}_pe_pt")
                    P.wait("vector", f"{tag}_pe_pt", pt_done)
                    P.do("vector", lambda e: e.tensor_copy(prow[:], pb[4].ap()[0:8, 0:TT]),
                         inc=f"{tag}_ve_prow")
                    prow_done = P.cur(f"{tag}_ve_prow")
                    tile_ctx["prow_done"] = prow_done
                    P.wait("tensor", f"{tag}_ve_prow", prow_done)
                    P.wait("tensor", f"{tag}_ve_out", prev_veout)
                    P.wait("vector", f"{tag}_pe_st", stats_done)
                    for e_i in range(NE):
                        bk = pb[6 + e_i % 2]
                        P.wait("tensor", f"{tag}_b67free{e_i % 2}", P.cur(f"{tag}_b67free{e_i % 2}"))
                        sel = cbf.ap()[0:8, CB_SEL8 + e_i * 128:CB_SEL8 + (e_i + 1) * 128]
                        P.do("tensor",
                             lambda e, bk=bk, sel=sel: e.matmul(bk.ap()[:, :], sel, prow.ap()[:, :],
                                                                start=True, stop=True),
                             inc=f"{tag}_pe_pb{e_i % 2}")
                        P.wait("vector", f"{tag}_pe_pb{e_i % 2}", P.cur(f"{tag}_pe_pb{e_i % 2}"))
                        P.do("vector",
                             lambda e, bk=bk, e_i=e_i: e.tensor_mul(
                                 probb.ap()[:, e_i * TT:(e_i + 1) * TT], bk.ap()[:, :], inv2h_sb),
                             inc=f"{tag}_b67free{e_i % 2}")
                    P.drain("vector")

                    for idx, e_i in enumerate(e_order):
                        if idx == 0:
                            continue
                        h_stage(e_i, idx)
                    if tt + 1 < NTILE:
                        rt_next = r_load(tt + 1)
                        sq_next = s1_rbsq(tt + 1, rt_next, P.cur(f"{tag}_pe_h"),
                                          P.cur(f"{tag}_ve_h"))
                    veh_done = P.cur(f"{tag}_ve_h")

                    P.wait("tensor", f"{tag}_ve_h", veh_done)
                    peout_b8 = P.cur(f"{tag}_pe_out")
                    for e_i in range(9):
                        g_idx = tt * 9 + e_i
                        wpar = g_idx % 2
                        P.wait("tensor", *w2_src[g_idx])
                        for oc in range(NC):
                            for hc in range(4):
                                wcol = wpar * 4 * D + hc * D + oc * 128
                                last_of_group = (oc == NC - 1 and hc == 3)
                                if e_i == 8 and hc == 3:
                                    inc = f"{tag}_pe_out"
                                elif e_i < 8 and last_of_group:
                                    inc = f"{tag}_w2c"
                                else:
                                    inc = None
                                P.do("tensor",
                                     lambda e, oc=oc, hc=hc, wcol=wcol, e_i=e_i: e.matmul(
                                         pb[oc].ap()[:, :],
                                         w2b.ap()[:, wcol:wcol + 128],
                                         h_sb.ap()[:, (e_i * 4 + hc) * TT:(e_i * 4 + hc + 1) * TT],
                                         start=(e_i == 0 and hc == 0), stop=(e_i == 8 and hc == 3)),
                                     inc=inc)
                        if e_i < 8:
                            grel.append((f"{tag}_w2c", P.cur(f"{tag}_w2c")))
                        else:
                            grel.append((f"{tag}_pe_out", P.cur(f"{tag}_pe_out")))
                        if w2cnt[0] < NTILE * 9:
                            w2_load()
                    peout_done = P.cur(f"{tag}_pe_out")

                    veout_base = P.cur(f"{tag}_ve_out")
                    for oc in range(NC):
                        P.wait("vector", f"{tag}_pe_out", peout_b8 + oc + 1)
                        P.do("vector",
                             lambda e, oc=oc, rt=rt: e.tensor_add(rt[:, oc * TT:(oc + 1) * TT],
                                                                  rt[:, oc * TT:(oc + 1) * TT],
                                                                  pb[oc].ap()[:, :]),
                             inc=f"{tag}_ve_out")
                    veout_done = P.cur(f"{tag}_ve_out")
                    prev_veout = veout_done
                    prev_veout_b = veout_base
                    prev_peout = peout_done

                    P.wait("scalar", f"{tag}_ve_out", veout_done)
                    P.wait("scalar", f"{tag}_rs", rs_base + 16 * tt)
                    P.do("scalar",
                         lambda e, tt=tt, rt=rt: e.dma_start(
                             out=r_view_tile(tt), in_=rt.rearrange("p (c t) -> p c t", c=NC)),
                         inc=f"{tag}_rs", n=16)

                P.wait("sync", f"{tag}_rs")

        def attn_phase(tag="a"):
            pes = _ES()
            with pes:
                def psb(name, shape, dt):
                    return pes.enter_context(nc.sbuf_tensor(name, shape, dt))
                sbtag = tag

                lnb_all = psb(f"{tag}_lnb", [128, NC * T], BF16)
                prows = psb(f"{tag}_prows", [16, T], F32)
                acc_M = psb(f"{tag}_accM", [128, H * DK], F32)
                facc = psb(f"{tag}_facc", [16, 8], F32)
                r_t = psb(f"{sbtag}_rt", [128, 2 * NC * TT], F32)
                asc = psb(f"{tag}_asc", [128, NC * TT], BF16)
                rtile = psb(f"{tag}_rtile", [128, TT], F32)
                rtile2 = psb(f"{tag}_rtile2", [128, TT], F32)
                fwr = psb(f"{tag}_fwr", [16, TT], F32)
                swr = psb(f"{tag}_swr", [16, TT], F32)
                qnr = psb(f"{tag}_qnr", [16, TT], F32)
                mu_row = psb(f"{sbtag}_mu", [1, TT], F32)
                ex2_row = psb(f"{sbtag}_ex2", [1, TT], F32)
                t_row = psb(f"{tag}_trow", [1, TT], F32)
                inv_row = psb(f"{sbtag}_inv", [1, TT], F32)
                ebc = psb(f"{tag}_ebc", [128, 16], F32)
                etr = psb(f"{tag}_etr", [32, 128], F32)
                azeros = psb(f"{tag}_az", [128, TT], F32)

                rb = asc.ap()[:, :]

                P.do("gpsimd", lambda e: e.memset(acc_M[:], 0.0), inc=f"{tag}_gz")
                P.do("gpsimd", lambda e: e.memset(facc[:], 0.0), inc=f"{tag}_gz")
                P.do("gpsimd", lambda e: e.memset(azeros[:], 0.0), inc=f"{tag}_gz")
                P.do("gpsimd", lambda e: e.memset(etr[:], 0.0), inc=f"{tag}_gz")
                P.do("gpsimd", lambda e: e.memset(rtile[:], 0.0), inc=f"{tag}_gz")
                P.do("gpsimd", lambda e: e.memset(rtile2[:], 0.0), inc=f"{tag}_gz")

                def bc_prep(col_f, col_s):
                    P.do("vector", lambda e, col_f=col_f: e.tensor_copy(etr.ap()[0:16, 0:1], col_f))
                    P.do("vector", lambda e, col_s=col_s: e.tensor_copy(etr.ap()[0:16, 32:33], col_s))
                    P.drain("vector")
                    P.do("vector", lambda e: e.transpose(etr.ap()[:, 64:96], etr.ap()[:, 0:32]))
                    P.do("vector", lambda e: e.transpose(etr.ap()[:, 96:128], etr.ap()[:, 32:64]),
                         inc=f"{tag}_ve_bc")
                    vb = P.cur(f"{tag}_ve_bc")
                    P.drain("vector")
                    return vb

                def bc_fs(col_f, col_s, bank, out_sb, vb=None):
                    if vb is None:
                        vb = bc_prep(col_f, col_s)
                    P.wait("tensor", f"{tag}_ve_bc", vb)
                    P.do("tensor", lambda e, bank=bank: e.matmul(
                        bank.ap()[0:64, 0:16], ones_row_f32[0:1, 0:64], etr.ap()[0:1, 64:80],
                        start=True, stop=True))
                    P.do("tensor", lambda e, bank=bank: e.matmul(
                        bank.ap()[64:128, 0:16], ones_row_f32[0:1, 0:64], etr.ap()[0:1, 96:112],
                        start=True, stop=True, tile_position=(0, 64)), inc=f"{tag}_pe_bcm")
                    pbcm = P.cur(f"{tag}_pe_bcm")
                    P.wait("vector", f"{tag}_pe_bcm", pbcm)
                    P.do("vector", lambda e, bank=bank, out_sb=out_sb: e.tensor_copy(
                        out_sb, bank.ap()[:, 0:16]), inc=f"{tag}_ve_bce")
                    P.drain("vector")
                    return P.cur(f"{tag}_ve_bce")

                def ln_stats_lnb(tt):
                    pass

                s1 = _ES()
                with s1:
                    def s1b(name, shape, dt):
                        return s1.enter_context(nc.sbuf_tensor(name, shape, dt))

                    wk_i = s1b(f"{tag}_wk", [128, NC * D], BF16)
                    wv_i = s1b(f"{tag}_wv", [128, NC * D], BF16)
                    wb_i = s1b(f"{tag}_wb", [128, NC * H], BF16)
                    wfd_i = s1b(f"{tag}_wfd", [128, NC * H], BF16)
                    wsd_i = s1b(f"{tag}_wsd", [128, NC * H], BF16)
                    wt1_i = s1b(f"{tag}_wt1", [128, 32], BF16)
                    wt2_i = s1b(f"{tag}_wt2", [128, 4 * H], BF16)
                    k_tm = s1b(f"{tag}_ktm", [128, D], BF16)
                    v_tm = s1b(f"{tag}_vtm", [128, D], BF16)
                    kbh32 = s1b(f"{tag}_kbh", [128, D], F32)
                    kk = s1b(f"{tag}_kk", [128, H * 128], BF16)
                    kbh_fm = s1b(f"{tag}_kfm", [128, NC * TT], BF16)
                    hmid_sb = s1b(f"{tag}_hmid", [128, 4 * TT], BF16)
                    k2t = hmid_sb
                    kninv = s1b(f"{tag}_kninv", [128, 4 * H], F32)
                    packT = s1b(f"{tag}_packT", [128, 96], F32)
                    c1t = s1b(f"{tag}_c1", [128, H], F32)

                    for wi, wf_ in ((wk_i, wimg["wk"]), (wv_i, wimg["wv"]), (wb_i, wbf),
                                    (wfd_i, wfdf), (wsd_i, wsdf), (wt1_i, wtf1f), (wt2_i, wtf2f)):
                        P.do("gpsimd", lambda e, wi=wi, wf_=wf_: e.dma_start(out=wi[:], in_=wf_),
                             inc=f"{tag}_wl", n=16)
                    wl1 = P.cur(f"{tag}_wl")

                    pv = dict(pe_st=0, pe_bc=0, pe_hm=0, sc_row=0, pe_tin=0, ve_r2=0,
                              ve_acc=0, pe_fx=0, ve_bce=0, pe_row=0)
                    prev_lnb = [0]
                    arl_base = [P.cur("f_rl0"), P.cur("f_rl1")]
                    sq = kbh_fm.ap()[:, :]

                    def a_r_load(tt):
                        par = tt % 2
                        rt_ = r_t.ap()[:, par * NC * TT:(par + 1) * NC * TT]
                        P.wait("sync", "f_rs")
                        P.wait("sync", f"{tag}_ve_lnb", prev_lnb[0])
                        P.do("sync",
                             lambda e, tt=tt, rt_=rt_: e.dma_start(
                                 out=rt_.rearrange("p (c t) -> p c t", c=NC), in_=r_view_tile(tt)),
                             inc="f_rl" + str(par), n=16)
                        return rt_

                    def a_s1(tt, rt_):
                        par = tt % 2
                        P.wait("vector", "f_rl" + str(par), arl_base[par] + 16 * (tt // 2 + 1))
                        P.wait("vector", f"{tag}_pe_st", pv["pe_st"])
                        P.wait("vector", f"{tag}_pe_hm", pv["pe_hm"])
                        P.do("vector", lambda e, rt_=rt_: e.tensor_copy(rb, rt_))
                        P.drain("vector")
                        P.do("vector", lambda e: e.tensor_mul(sq, rb, rb), inc=f"{tag}_ve_sq")
                        return P.cur(f"{tag}_ve_sq")

                    rt_next = a_r_load(0)
                    sq_next = a_s1(0, rt_next)
                    for tt in range(NTILE):
                        rt = rt_next
                        sq_done = sq_next

                        P.wait("tensor", "cl", 32)
                        P.wait("tensor", f"{tag}_ve_sq", sq_done)
                        P.wait("tensor", f"{tag}_sc_row", pv["sc_row"])
                        for c in range(NC):
                            P.do("tensor",
                                 lambda e, c=c: e.matmul(pb[0].ap()[0:1, :],
                                                         cbf.ap()[:, CB_ONESCOL:CB_ONESCOL + 1],
                                                         rb[:, c * TT:(c + 1) * TT],
                                                         start=(c == 0), stop=(c == NC - 1)))
                        for c in range(NC):
                            P.do("tensor",
                                 lambda e, c=c, sq=sq: e.matmul(
                                     pb[1].ap()[0:1, :], cbf.ap()[:, CB_ONESCOL:CB_ONESCOL + 1],
                                     sq[:, c * TT:(c + 1) * TT],
                                     start=(c == 0), stop=(c == NC - 1)),
                                 inc=f"{tag}_pe_st" if c == NC - 1 else None)
                        pv["pe_st"] = P.cur(f"{tag}_pe_st")

                        P.wait("vector", f"{tag}_pe_st", pv["pe_st"])
                        P.do("vector", lambda e: e.tensor_scalar_mul(mu_row[:], pb[0].ap()[0:1, :], 1.0 / D))
                        P.do("vector", lambda e: e.tensor_scalar_mul(ex2_row[:], pb[1].ap()[0:1, :], 1.0 / D))
                        P.drain("vector")
                        P.do("vector", lambda e: e.tensor_mul(t_row[:], mu_row[:], mu_row[:]))
                        P.drain("vector")
                        P.do("vector", lambda e: e.tensor_sub(ex2_row[:], ex2_row[:], t_row[:]),
                             inc=f"{tag}_ve_var")
                        var_done = P.cur(f"{tag}_ve_var")
                        P.wait("scalar", f"{tag}_ve_var", var_done)
                        P.do("scalar", lambda e: e.activation(ex2_row[:], ex2_row[:], Act.Ln,
                                                              bias=cst.ap()[0:1, C_EPSLN:C_EPSLN + 1]))
                        P.drain("scalar")
                        P.do("scalar", lambda e: e.activation(inv_row[:], ex2_row[:], Act.Exp, scale=-0.5),
                             inc=f"{tag}_sc_inv")
                        inv_done = P.cur(f"{tag}_sc_inv")

                        P.wait("tensor", f"{tag}_ve_var", var_done)
                        P.wait("tensor", f"{tag}_sc_inv", inv_done)
                        P.wait("tensor", f"{tag}_ve_lnb", P.cur(f"{tag}_ve_lnb"))
                        P.wait("tensor", f"{tag}_ve_r2", pv["ve_r2"])
                        P.wait("tensor", f"{tag}_ve_kv", P.cur(f"{tag}_ve_kv"))
                        P.do("tensor", lambda e: e.matmul(pb[2].ap()[:, :], ones_row_f32, mu_row[:],
                                                          start=True, stop=True))
                        P.do("tensor", lambda e: e.matmul(pb[3].ap()[0:16, :], ones_row_f32[0:1, 0:16],
                                                          inv_row[:], start=True, stop=True),
                             inc=f"{tag}_pe_bc")
                        pv["pe_bc"] = P.cur(f"{tag}_pe_bc")

                        P.wait("vector", f"{tag}_pe_bc", pv["pe_bc"])
                        for c in range(NC):
                            P.do("vector",
                                 lambda e, c=c, tt=tt, rt=rt: e.tensor_sub(
                                     lnb_all.ap()[:, c * T + tt * TT:c * T + tt * TT + TT],
                                     rt[:, c * TT:(c + 1) * TT], pb[2].ap()[:, :]),
                                 inc=f"{tag}_ve_lnb" if c == NC - 1 else None)
                        lnb_done = P.cur(f"{tag}_ve_lnb")
                        prev_lnb[0] = lnb_done
                        P.drain("vector")

                        P.wait("tensor", f"{tag}_wl", wl1)
                        P.wait("tensor", f"{tag}_ve_lnb", lnb_done)
                        P.wait("tensor", f"{tag}_ve_tin", P.cur(f"{tag}_ve_tin"))
                        for bi, wi in ((0, wb_i), (1, wfd_i), (4, wsd_i)):
                            for c in range(NC):
                                P.do("tensor",
                                     lambda e, bi=bi, wi=wi, c=c, tt=tt: e.matmul(
                                         pb[bi].ap()[0:16, :],
                                         wi.ap()[:, c * H:(c + 1) * H],
                                         lnb_all.ap()[:, c * T + tt * TT:c * T + tt * TT + TT],
                                         start=(c == 0), stop=(c == NC - 1)),
                                     inc=f"{tag}_pe_row" if (bi == 4 and c == NC - 1) else None)
                        pv["pe_row"] = P.cur(f"{tag}_pe_row")

                        P.wait("scalar", f"{tag}_pe_row", pv["pe_row"])
                        P.wait("scalar", f"{tag}_pe_tin", pv["pe_tin"])
                        P.wait("scalar", f"{tag}_ve_r2", pv["ve_r2"])
                        P.wait("scalar", f"{tag}_gz", P.cur(f"{tag}_gz"))
                        P.do("scalar", lambda e: e.activation(rtile.ap()[0:16, :], pb[0].ap()[0:16, :],
                                                              Act.Sigmoid))
                        P.do("scalar", lambda e: e.activation(rtile.ap()[32:48, :], pb[1].ap()[0:16, :],
                                                              Act.Sigmoid,
                                                              bias=cst.ap()[0:16, C_BFD:C_BFD + 1]))
                        P.do("scalar", lambda e: e.activation(rtile.ap()[64:80, :], pb[4].ap()[0:16, :],
                                                              Act.Sigmoid,
                                                              bias=cst.ap()[0:16, C_BSD:C_BSD + 1]))
                        P.drain("scalar")
                        P.do("scalar", lambda e: e.activation(rtile.ap()[32:48, :], rtile.ap()[32:48, :],
                                                              Act.Ln,
                                                              bias=cst.ap()[32:48, C_EPS6:C_EPS6 + 1]))
                        P.do("scalar", lambda e: e.activation(rtile.ap()[64:80, :], rtile.ap()[64:80, :],
                                                              Act.Ln,
                                                              bias=cst.ap()[64:80, C_EPS6:C_EPS6 + 1]),
                             inc=f"{tag}_sc_row")
                        pv["sc_row"] = P.cur(f"{tag}_sc_row")

                        P.wait("vector", f"{tag}_sc_row", pv["sc_row"])
                        P.wait("vector", f"{tag}_gz", P.cur(f"{tag}_gz"))
                        P.do("vector", lambda e: e.tensor_tensor_scan(
                            rtile.ap()[96:112, :], azeros.ap()[32:48, :], rtile.ap()[32:48, :], 0.0,
                            Alu.add, Alu.add))
                        P.drain("vector")
                        P.do("vector", lambda e: e.tensor_tensor_scan(
                            rtile2.ap()[64:80, :], azeros.ap()[64:80, :], rtile.ap()[64:80, :], 0.0,
                            Alu.add, Alu.add), inc=f"{tag}_ve_cs")
                        cs_done = P.cur(f"{tag}_ve_cs")
                        P.drain("vector")

                        P.wait("scalar", f"{tag}_ve_cs", cs_done)
                        P.do("scalar", lambda e: e.activation(fwr.ap()[:, :], rtile.ap()[96:112, :],
                                                              Act.Exp, scale=-1.0,
                                                              bias=rtile.ap()[96:112, TT - 1:TT]))
                        P.do("scalar", lambda e: e.activation(swr.ap()[:, :], rtile2.ap()[64:80, :],
                                                              Act.Exp, scale=-1.0,
                                                              bias=rtile2.ap()[64:80, TT - 1:TT]))
                        P.drain("scalar")
                        P.do("scalar", lambda e: e.activation(facc.ap()[:, 4:5],
                                                              rtile.ap()[96:112, TT - 1:TT], Act.Exp))
                        P.do("scalar", lambda e: e.activation(facc.ap()[:, 5:6],
                                                              rtile2.ap()[64:80, TT - 1:TT], Act.Exp),
                             inc=f"{tag}_sc_fw")
                        fw_done = P.cur(f"{tag}_sc_fw")

                        P.wait("vector", f"{tag}_sc_fw", fw_done)
                        P.do("vector", lambda e: e.tensor_copy(etr.ap()[0:16, 1:2],
                                                               rtile.ap()[96:112, TT - 1:TT]))
                        P.do("vector", lambda e: e.tensor_copy(etr.ap()[0:16, 2:3],
                                                               rtile2.ap()[64:80, TT - 1:TT]))
                        P.drain("vector")
                        P.do("vector", lambda e: e.tensor_add(facc.ap()[:, 0:1], facc.ap()[:, 0:1],
                                                              etr.ap()[0:16, 1:2]))
                        P.do("vector", lambda e: e.tensor_add(facc.ap()[:, 1:2], facc.ap()[:, 1:2],
                                                              etr.ap()[0:16, 2:3]))
                        P.drain("vector")
                        P.do("vector", lambda e: e.reduce_sum(facc.ap()[:, 6:7], fwr.ap()[:, :],
                                                              mybir.AxisListType.X))
                        P.do("vector", lambda e: e.reduce_sum(facc.ap()[:, 7:8], swr.ap()[:, :],
                                                              mybir.AxisListType.X))
                        P.drain("vector")
                        P.do("vector", lambda e: e.scalar_tensor_tensor(
                            facc.ap()[:, 2:3], facc.ap()[:, 2:3], facc.ap()[:, 4:5],
                            facc.ap()[:, 6:7], Alu.mult, Alu.add))
                        P.do("vector", lambda e: e.scalar_tensor_tensor(
                            facc.ap()[:, 3:4], facc.ap()[:, 3:4], facc.ap()[:, 5:6],
                            facc.ap()[:, 7:8], Alu.mult, Alu.add))
                        P.drain("vector")
                        vb_tile = bc_prep(facc.ap()[:, 4:5], facc.ap()[:, 5:6])
                        P.do("vector", lambda e: e.tensor_mul(rtile.ap()[32:48, :], rtile.ap()[0:16, :],
                                                              fwr.ap()[:, :]))
                        P.do("vector", lambda e: e.tensor_mul(rtile.ap()[64:80, :], rtile.ap()[0:16, :],
                                                              swr.ap()[:, :]))
                        P.drain("vector")
                        P.do("vector", lambda e: e.tensor_mul(rtile.ap()[32:48, :], rtile.ap()[32:48, :],
                                                              pb[3].ap()[0:16, :]))
                        P.do("vector", lambda e: e.tensor_mul(rtile.ap()[64:80, :], rtile.ap()[64:80, :],
                                                              pb[3].ap()[0:16, :]), inc=f"{tag}_ve_r2")
                        pv["ve_r2"] = P.cur(f"{tag}_ve_r2")
                        P.drain("vector")

                        for blk in range(NBLK):
                            lcol = tt * TT + blk * 128
                            P.wait("tensor", f"{tag}_sc_row", pv["sc_row"])
                            P.wait("tensor", f"{tag}_ve_kv", max(0, P.cur(f"{tag}_ve_kv") - 1))
                            for c in range(NC):
                                for half in range(2):
                                    P.do("tensor",
                                         lambda e, c=c, half=half, lcol=lcol: e.matmul(
                                             pb[half].ap()[:, :],
                                             lnb_all.ap()[:, c * T + lcol:c * T + lcol + 128],
                                             wk_i.ap()[:, c * D + half * 512:c * D + (half + 1) * 512],
                                             start=(c == 0), stop=(c == NC - 1)),
                                         inc=f"{tag}_pe_k" if (c == NC - 1 and half == 1) else None)
                            pek = P.cur(f"{tag}_pe_k")
                            P.wait("vector", f"{tag}_pe_k", pek)
                            P.wait("vector", f"{tag}_pe_M", P.cur(f"{tag}_pe_M"))
                            P.do("vector", lambda e: e.tensor_copy(k_tm.ap()[:, 0:512], pb[0].ap()[:, :]))
                            P.do("vector", lambda e: e.tensor_copy(k_tm.ap()[:, 512:1024],
                                                                   pb[1].ap()[:, :]),
                                 inc=f"{tag}_ve_kv")
                            P.drain("vector")
                            P.wait("vector", f"{tag}_pe_fx", pv["pe_fx"])
                            P.do("vector", lambda e: e.tensor_mul(k2t.ap()[:, 0:D], k_tm[:], k_tm[:]))
                            P.drain("vector")
                            P.do("vector", lambda e, blk=blk: e.tensor_reduce(
                                kninv.ap()[:, blk * H:(blk + 1) * H],
                                k2t.ap()[:, 0:D].rearrange("p (h d) -> p h d", h=H),
                                mybir.AxisListType.X, Alu.add))
                            P.drain("vector")
                            P.do("vector", lambda e, blk=blk: e.tensor_scalar_max(
                                kninv.ap()[:, blk * H:(blk + 1) * H],
                                kninv.ap()[:, blk * H:(blk + 1) * H], 1e-18), inc=f"{tag}_ve_kn")
                            kn_done = P.cur(f"{tag}_ve_kn")
                            P.wait("scalar", f"{tag}_ve_kn", kn_done)
                            P.do("scalar", lambda e, blk=blk: e.activation(
                                kninv.ap()[:, blk * H:(blk + 1) * H],
                                kninv.ap()[:, blk * H:(blk + 1) * H], Act.Ln))
                            P.drain("scalar")
                            P.do("scalar", lambda e, blk=blk: e.activation(
                                kninv.ap()[:, blk * H:(blk + 1) * H],
                                kninv.ap()[:, blk * H:(blk + 1) * H], Act.Exp, scale=-0.5),
                                inc=f"{tag}_sc_kn")
                            skn = P.cur(f"{tag}_sc_kn")
                            P.wait("tensor", f"{tag}_ve_lnb", lnb_done)
                            P.wait("tensor", f"{tag}_ve_r2", pv["ve_r2"])
                            P.wait("tensor", f"{tag}_ve_kv", max(0, P.cur(f"{tag}_ve_kv") - 1))
                            for c in range(NC):
                                for half in range(2):
                                    P.do("tensor",
                                         lambda e, c=c, half=half, lcol=lcol: e.matmul(
                                             pb[2 + half].ap()[:, :],
                                             lnb_all.ap()[:, c * T + lcol:c * T + lcol + 128],
                                             wv_i.ap()[:, c * D + half * 512:c * D + (half + 1) * 512],
                                             start=(c == 0), stop=(c == NC - 1)),
                                         inc=f"{tag}_pe_v" if (c == NC - 1 and half == 1) else None)
                            pevd = P.cur(f"{tag}_pe_v")
                            P.wait("vector", f"{tag}_pe_v", pevd)
                            P.do("vector", lambda e: e.tensor_copy(v_tm.ap()[:, 0:512], pb[2].ap()[:, :]))
                            P.do("vector", lambda e: e.tensor_copy(v_tm.ap()[:, 512:1024],
                                                                   pb[3].ap()[:, :]),
                                 inc=f"{tag}_ve_kv")
                            P.drain("vector")
                            P.wait("tensor", f"{tag}_ve_r2", pv["ve_r2"])
                            P.wait("tensor", f"{tag}_ve_tin", P.cur(f"{tag}_ve_tin"))
                            P.wait("tensor", f"{tag}_ve_kf", P.cur(f"{tag}_ve_kf"))
                            P.do("tensor", lambda e, blk=blk: e.transpose(
                                pb[4].ap()[:, 0:96], rtile.ap()[0:96, blk * 128:(blk + 1) * 128],
                                ident[0:96, 0:96]), inc=f"{tag}_pe_tin")
                            pv["pe_tin"] = P.cur(f"{tag}_pe_tin")
                            P.wait("vector", f"{tag}_pe_tin", pv["pe_tin"])
                            P.do("vector", lambda e: e.tensor_copy(packT[:], pb[4].ap()[:, 0:96]),
                                 inc=f"{tag}_ve_tin")
                            P.drain("vector")
                            P.wait("vector", f"{tag}_sc_kn", skn)
                            P.wait("vector", f"{tag}_pe_M", P.cur(f"{tag}_pe_M"))
                            P.do("vector", lambda e, blk=blk: e.tensor_mul(
                                c1t[:], packT.ap()[:, 0:16], kninv.ap()[:, blk * H:(blk + 1) * H]))
                            P.drain("vector")
                            P.do("vector", lambda e: e.tensor_tensor(
                                kbh32.ap().rearrange("p (h d) -> p h d", h=H),
                                k_tm.ap().rearrange("p (h d) -> p h d", h=H),
                                c1t.ap().unsqueeze(2).broadcast_to((128, H, DK)), Alu.mult),
                                inc=f"{tag}_ve_kb")
                            kb_done = P.cur(f"{tag}_ve_kb")
                            P.drain("vector")
                            P.do("vector", lambda e: e.tensor_tensor(
                                kk.ap().rearrange("p (h x d) -> p h x d", h=H, x=2)[:, :, 0, :],
                                kbh32.ap().rearrange("p (h d) -> p h d", h=H),
                                packT.ap()[:, 32:48].unsqueeze(2).broadcast_to((128, H, DK)),
                                Alu.mult))
                            P.do("vector", lambda e: e.tensor_tensor(
                                kk.ap().rearrange("p (h x d) -> p h x d", h=H, x=2)[:, :, 1, :],
                                kbh32.ap().rearrange("p (h d) -> p h d", h=H),
                                packT.ap()[:, 64:80].unsqueeze(2).broadcast_to((128, H, DK)),
                                Alu.mult), inc=f"{tag}_ve_kk")
                            kk_done = P.cur(f"{tag}_ve_kk")
                            P.drain("vector")
                            P.wait("tensor", f"{tag}_ve_kb", kb_done)
                            P.wait("tensor", f"{tag}_ve_tin", P.cur(f"{tag}_ve_tin"))
                            P.wait("tensor", f"{tag}_ve_kf", P.cur(f"{tag}_ve_kf"))
                            P.wait("tensor", f"{tag}_ve_bce", pv["ve_bce"])
                            P.wait("tensor", f"{tag}_sc_fx", P.cur(f"{tag}_sc_fx"))
                            for c in range(NC):
                                P.do("tensor",
                                     lambda e, c=c: e.transpose(
                                         pb[4 + c // 4].ap()[:, (c % 4) * 128:(c % 4) * 128 + 128],
                                         kbh32.ap()[:, c * 128:(c + 1) * 128], ident),
                                     inc=f"{tag}_pe_kt" if c == NC - 1 else None)
                            kt_done = P.cur(f"{tag}_pe_kt")
                            P.wait("vector", f"{tag}_pe_kt", kt_done)
                            for c in range(NC):
                                P.do("vector",
                                     lambda e, c=c, blk=blk: e.tensor_copy(
                                         kbh_fm.ap()[:, c * TT + blk * 128:c * TT + blk * 128 + 128],
                                         pb[4 + c // 4].ap()[:, (c % 4) * 128:(c % 4) * 128 + 128]),
                                     inc=f"{tag}_ve_kf" if c == NC - 1 else None)
                            P.drain("vector")

                            P.wait("tensor", f"{tag}_ve_kk", kk_done)
                            P.wait("tensor", f"{tag}_ve_acc", pv["ve_acc"])
                            for h in range(H):
                                bk = pb[6 + h // 8]
                                st_ = (blk == 0 and h % 8 == 0)
                                sp_ = (blk == 3 and h % 8 == 7)
                                P.do("tensor",
                                     lambda e, h=h, bk=bk, st_=st_, sp_=sp_: e.matmul(
                                         bk.ap()[:, (h % 8) * DK:(h % 8 + 1) * DK],
                                         kk.ap()[:, h * 128:(h + 1) * 128],
                                         v_tm.ap()[:, h * DK:(h + 1) * DK],
                                         start=st_, stop=sp_),
                                     inc=f"{tag}_pe_M" if h == H - 1 else None)
                        kf_all = P.cur(f"{tag}_ve_kf")
                        P.wait("tensor", f"{tag}_ve_kf", kf_all)
                        P.wait("tensor", f"{tag}_sc_hm", P.cur(f"{tag}_sc_hm"))
                        P.wait("tensor", f"{tag}_ve_kv", P.cur(f"{tag}_ve_kv"))
                        P.wait("tensor", f"{tag}_ve_lnb", lnb_done)
                        for h in range(H):
                            c, par = h // 2, h % 2
                            bk = pb[2 + h // 4]
                            P.do("tensor",
                                 lambda e, h=h, c=c, par=par, bk=bk: e.matmul(
                                     bk.ap()[32 * (h % 4):32 * (h % 4) + 32, :],
                                     wt1_i.ap()[64 * par:64 * par + 64, :],
                                     kbh_fm.ap()[64 * par:64 * par + 64, c * TT:(c + 1) * TT],
                                     start=True, stop=True,
                                     tile_position=(64 * par, 32 * (h % 4))),
                                 inc=f"{tag}_pe_hm" if h == H - 1 else None)
                        pv["pe_hm"] = P.cur(f"{tag}_pe_hm")
                        if tt + 1 < NTILE:
                            rt_next = a_r_load(tt + 1)
                            sq_next = a_s1(tt + 1, rt_next)
                        P.wait("scalar", f"{tag}_pe_hm", pv["pe_hm"])
                        P.wait("scalar", f"{tag}_pe_fx", pv["pe_fx"])
                        for g in range(4):
                            P.do("scalar",
                                 lambda e, g=g: e.activation(
                                     hmid_sb.ap()[:, g * TT:(g + 1) * TT], pb[2 + g].ap()[:, :],
                                     Act.Silu, bias=cst.ap()[:, C_BTF1:C_BTF1 + 1]),
                                 inc=f"{tag}_sc_hm" if g == 3 else None)
                        hm_done = P.cur(f"{tag}_sc_hm")
                        P.wait("tensor", f"{tag}_sc_hm", hm_done)
                        P.wait("tensor", f"{tag}_ve_bce", pv["ve_bce"])
                        for g in range(4):
                            P.do("tensor",
                                 lambda e, g=g: e.matmul(
                                     pb[5].ap()[0:16, :], wt2_i.ap()[:, g * H:(g + 1) * H],
                                     hmid_sb.ap()[:, g * TT:(g + 1) * TT],
                                     start=(g == 0), stop=(g == 3)),
                                 inc=f"{tag}_pe_fx" if g == 3 else None)
                        pv["pe_fx"] = P.cur(f"{tag}_pe_fx")
                        P.wait("scalar", f"{tag}_pe_fx", pv["pe_fx"])
                        P.do("scalar", lambda e, tt=tt: e.activation(
                            prows.ap()[:, tt * TT:(tt + 1) * TT], pb[5].ap()[0:16, :], Act.Sigmoid,
                            bias=cst.ap()[0:16, C_BTF2:C_BTF2 + 1]), inc=f"{tag}_sc_fx")
                        fx_done = P.cur(f"{tag}_sc_fx")
                        P.wait("vector", f"{tag}_sc_fx", fx_done)
                        P.do("vector", lambda e, tt=tt: e.tensor_scalar(
                            prows.ap()[:, tt * TT:(tt + 1) * TT],
                            prows.ap()[:, tt * TT:(tt + 1) * TT],
                            0.01, 0.99, Alu.max, Alu.min, accum_out=None), inc=f"{tag}_ve_fx")
                        P.drain("vector")

                        P.wait("tensor", f"{tag}_sc_fx", fx_done)
                        P.wait("tensor", f"{tag}_ve_kf", P.cur(f"{tag}_ve_kf"))
                        pv["ve_bce"] = bc_fs(facc.ap()[:, 4:5], facc.ap()[:, 5:6], pb[5],
                                             ebc.ap()[:, :], vb=vb_tile)
                        P.wait("vector", f"{tag}_pe_M", P.cur(f"{tag}_pe_M"))
                        P.do("vector", lambda e: e.tensor_tensor(
                            acc_M.ap().rearrange("p (h d) -> p h d", h=H),
                            acc_M.ap().rearrange("p (h d) -> p h d", h=H),
                            ebc.ap().unsqueeze(2).broadcast_to((128, H, DK)), Alu.mult))
                        P.drain("vector")
                        P.do("vector", lambda e: e.tensor_add(acc_M.ap()[:, 0:512],
                                                              acc_M.ap()[:, 0:512], pb[6].ap()[:, :]))
                        P.do("vector", lambda e: e.tensor_add(acc_M.ap()[:, 512:1024],
                                                              acc_M.ap()[:, 512:1024], pb[7].ap()[:, :]),
                             inc=f"{tag}_ve_acc")
                        pv["ve_acc"] = P.cur(f"{tag}_ve_acc")
                        P.drain("vector")

                for s_ in ("pe_k", "pe_v", "pe_M", "pe_kt", "pe_hm", "pe_fx", "sc_hm",
                           "ve_kk", "ve_kf", "ve_acc", "ve_kv", "ve_tin", "sc_kn", "ve_kn",
                           "pe_tin", "sc_fx", "ve_fx", "pe_row", "sc_row"):
                    P.wait("gpsimd", f"{tag}_{s_}", P.cur(f"{tag}_{s_}"))
                P.wait("gpsimd", f"{tag}_ve_acc", P.cur(f"{tag}_ve_acc"))
                P.do("gpsimd", lambda e: e.dma_start(out=cc_in[:, 0:1024], in_=acc_M[:]),
                     inc=f"{tag}_ccin", n=16)
                P.do("gpsimd", lambda e: e.dma_start(out=cc_in[0:16, 1024:1028], in_=facc.ap()[:, 0:4]),
                     inc=f"{tag}_ccin", n=16)
                P.wait("gpsimd", "cl", 32)
                P.do("gpsimd", lambda e: e.dma_start(out=cc_in[16:128, 1024:1028],
                                                     in_=cst.ap()[16:128, 272:276]),
                     inc=f"{tag}_ccin", n=16)
                if dbg:
                    P.do("gpsimd", lambda e: e.dma_start(out=d_accM, in_=acc_M[:]),
                         inc=f"{tag}_dbg", n=16)
                    P.do("gpsimd", lambda e: e.dma_start(out=d_facc, in_=facc[:]),
                         inc=f"{tag}_dbg", n=16)
                    P.wait("gpsimd", f"{tag}_ve_fx", P.cur(f"{tag}_ve_fx"))
                    P.do("gpsimd", lambda e: e.dma_start(out=d_flux, in_=prows[:]),
                         inc=f"{tag}_dbg", n=16)
                P.wait("gpsimd", f"{tag}_ccin")
                P.do("gpsimd", lambda e: e.collective_compute(
                    "AllGather", Alu.bypass,
                    replica_groups=[[2 * i, 2 * i + 1] for i in range(n_pairs)],
                    ins=[cc_in.ap()], outs=[cc_out.ap()]), inc=f"{tag}_cc")

                s2 = _ES()
                with s2:
                    def s2b(name, shape, dt):
                        return s2.enter_context(nc.sbuf_tensor(name, shape, dt))

                    wq_i = s2b(f"{tag}_wq", [128, NC * D], BF16)
                    wo_i = s2b(f"{tag}_wo", [128, NC * D], BF16)
                    q_fm = s2b(f"{tag}_qfm", [128, NC * TT], BF16)
                    ma_sb = s2b(f"{tag}_ma", [128, 1024], F32)
                    mb_sb = s2b(f"{tag}_mb", [128, 1024], F32)
                    mn_sb = ma_sb
                    rab = s2b(f"{tag}_rab", [16, 8], F32)
                    bdf = s2b(f"{tag}_bdf", [128, NC * 128], BF16)
                    bds = s2b(f"{tag}_bds", [128, NC * 128], BF16)
                    o32 = s2b(f"{tag}_o32", [128, 1024], F32)
                    gT = s2b(f"{tag}_gT", [128, 64], F32)
                    dcol = s2b(f"{tag}_dcol", [16, 4], F32)
                    dbc = s2b(f"{tag}_dbc", [128, 16], F32)

                    P.do("gpsimd", lambda e: e.dma_start(out=wq_i[:], in_=wimg["wq"]),
                         inc=f"{tag}_wl", n=16)
                    wl2q = P.cur(f"{tag}_wl")
                    P.wait("gpsimd", f"{tag}_wl", wl2q)
                    P.do("gpsimd", lambda e: e.dma_start(out=wo_i[:], in_=wimg["wo"]),
                         inc=f"{tag}_wl", n=16)
                    wl2 = P.cur(f"{tag}_wl")
                    P.wait("gpsimd", f"{tag}_cc")
                    P.do("gpsimd", lambda e: e.dma_start(out=ma_sb[:], in_=cc_out[0:128, 0:1024]),
                         inc=f"{tag}_ccl", n=16)
                    P.do("gpsimd", lambda e: e.dma_start(out=mb_sb[:], in_=cc_out[128:256, 0:1024]),
                         inc=f"{tag}_ccl", n=16)
                    P.do("gpsimd", lambda e: e.dma_start(out=rab.ap()[:, 0:4],
                                                         in_=cc_out[0:16, 1024:1028]),
                         inc=f"{tag}_ccl", n=16)
                    P.do("gpsimd", lambda e: e.dma_start(out=rab.ap()[:, 4:8],
                                                         in_=cc_out[128:144, 1024:1028]),
                         inc=f"{tag}_ccl", n=16)
                    ccl_done = P.cur(f"{tag}_ccl")

                    P.wait("scalar", f"{tag}_ccl", ccl_done)
                    P.do("scalar", lambda e: e.activation(dcol.ap()[:, 0:1], rab.ap()[:, 4:5], Act.Exp))
                    P.do("scalar", lambda e: e.activation(dcol.ap()[:, 1:2], rab.ap()[:, 5:6], Act.Exp),
                         inc=f"{tag}_sc_eb")
                    eb_done = P.cur(f"{tag}_sc_eb")
                    P.wait("vector", f"{tag}_sc_eb", eb_done)
                    P.do("vector", lambda e: e.scalar_tensor_tensor(
                        dcol.ap()[:, 2:3], rab.ap()[:, 2:3], dcol.ap()[:, 0:1], rab.ap()[:, 6:7],
                        Alu.mult, Alu.add))
                    P.do("vector", lambda e: e.scalar_tensor_tensor(
                        dcol.ap()[:, 3:4], rab.ap()[:, 3:4], dcol.ap()[:, 1:2], rab.ap()[:, 7:8],
                        Alu.mult, Alu.add))
                    P.drain("vector")
                    P.do("vector", lambda e: e.tensor_scalar_add(dcol.ap()[:, 2:4], dcol.ap()[:, 2:4],
                                                                 1e-6))
                    P.drain("vector")
                    P.do("vector", lambda e: e.reciprocal(dcol.ap()[:, 2:4], dcol.ap()[:, 2:4]))
                    P.drain("vector")
                    bc_fs(dcol.ap()[:, 0:1], dcol.ap()[:, 1:2], pb[5], ebc.ap()[:, :])
                    dj = bc_fs(dcol.ap()[:, 2:3], dcol.ap()[:, 3:4], pb[5], dbc.ap()[:, :])
                    P.wait("vector", f"{tag}_ccl", ccl_done)
                    P.do("vector", lambda e: e.tensor_tensor(
                        mn_sb.ap().rearrange("p (h d) -> p h d", h=H),
                        mn_sb.ap().rearrange("p (h d) -> p h d", h=H),
                        ebc.ap().unsqueeze(2).broadcast_to((128, H, DK)), Alu.mult))
                    P.drain("vector")
                    P.do("vector", lambda e: e.tensor_add(mn_sb[:], mn_sb[:], mb_sb[:]))
                    P.drain("vector")
                    P.do("vector", lambda e: e.tensor_tensor(
                        mn_sb.ap().rearrange("p (h d) -> p h d", h=H),
                        mn_sb.ap().rearrange("p (h d) -> p h d", h=H),
                        dbc.ap().unsqueeze(2).broadcast_to((128, H, DK)), Alu.mult),
                        inc=f"{tag}_ve_mn")
                    mn_done = P.cur(f"{tag}_ve_mn")
                    P.drain("vector")

                    if dbg:
                        P.wait("gpsimd", f"{tag}_ve_mn", mn_done)
                        P.do("gpsimd", lambda e: e.dma_start(out=d_mn, in_=mn_sb[:]),
                             inc=f"{tag}_dbg", n=16)
                        P.do("gpsimd", lambda e: e.dma_start(out=d_mb, in_=mb_sb[:]),
                             inc=f"{tag}_dbg", n=16)
                        P.do("gpsimd", lambda e: e.dma_start(out=d_rab, in_=rab[:]),
                             inc=f"{tag}_dbg", n=16)
                        P.do("gpsimd", lambda e: e.dma_start(out=d_dcol, in_=dcol[:]),
                             inc=f"{tag}_dbg", n=16)
                        P.do("gpsimd", lambda e: e.dma_start(out=d_ebc, in_=ebc[:]),
                             inc=f"{tag}_dbg", n=16)
                        P.do("gpsimd", lambda e: e.dma_start(out=d_dbc, in_=dbc[:]),
                             inc=f"{tag}_dbg", n=16)
                    P.wait("gpsimd", f"{tag}_ve_mn", mn_done)
                    P.do("gpsimd", lambda e: e.memset(bdf[:], 0.0))
                    P.do("gpsimd", lambda e: e.memset(bds[:], 0.0))
                    P.drain("gpsimd")
                    mn_e = mn_sb.ap().rearrange("p (c x) -> p c x", c=H)
                    P.do("gpsimd", lambda e: e.dma_start(
                        out=bdf.ap().rearrange("p (c x) -> p c x", c=NC)[0:64, :, 0:64],
                        in_=mn_e[0:64, 0::2, :]), inc=f"{tag}_bd", n=16)
                    P.do("gpsimd", lambda e: e.dma_start(
                        out=bdf.ap().rearrange("p (c x) -> p c x", c=NC)[64:128, :, 64:128],
                        in_=mn_e[0:64, 1::2, :]), inc=f"{tag}_bd", n=16)
                    P.do("gpsimd", lambda e: e.dma_start(
                        out=bds.ap().rearrange("p (c x) -> p c x", c=NC)[0:64, :, 0:64],
                        in_=mn_e[64:128, 0::2, :]), inc=f"{tag}_bd", n=16)
                    P.do("gpsimd", lambda e: e.dma_start(
                        out=bds.ap().rearrange("p (c x) -> p c x", c=NC)[64:128, :, 64:128],
                        in_=mn_e[64:128, 1::2, :]), inc=f"{tag}_bd", n=16)
                    bd_done = P.cur(f"{tag}_bd")

                    pv2 = dict(ve_q=0, pe_wo=0, ve_ro=0)
                    rt = r_t.ap()[:, 0:NC * TT]
                    s2rl_base = P.cur("f_rl0")
                    for tt in range(NTILE):
                        if dbg:
                            P.wait("vector", f"{tag}_dbg", P.cur(f"{tag}_dbg"))
                        P.wait("sync", "f_rs")
                        P.wait("sync", f"{tag}_ve_lnb", P.cur(f"{tag}_ve_lnb"))
                        P.do("sync",
                             lambda e, tt=tt, rt=rt: e.dma_start(
                                 out=rt.rearrange("p (c t) -> p c t", c=NC), in_=r_view_tile(tt)),
                             inc="f_rl0", n=16)
                        P.wait("tensor", f"{tag}_wl", wl2q)
                        P.wait("tensor", f"{tag}_ve_ro", pv2["ve_ro"])
                        P.wait("tensor", f"{tag}_ve_o", P.cur(f"{tag}_ve_o"))
                        for oc in range(NC):
                            bk = pb[oc % 2]
                            P.wait("tensor", f"{tag}_ve_q",
                                   max(0, P.cur(f"{tag}_ve_q") - 1))
                            for c in range(NC):
                                P.do("tensor",
                                     lambda e, oc=oc, c=c, tt=tt, bk=bk: e.matmul(
                                         bk.ap()[:, :],
                                         wq_i.ap()[:, c * D + oc * 128:c * D + oc * 128 + 128],
                                         lnb_all.ap()[:, c * T + tt * TT:c * T + tt * TT + TT],
                                         start=(c == 0), stop=(c == NC - 1)),
                                     inc=f"{tag}_pe_q" if c == NC - 1 else None)
                            peq = P.cur(f"{tag}_pe_q")
                            P.wait("vector", f"{tag}_pe_q", peq)
                            P.do("vector",
                                 lambda e, oc=oc, bk=bk: e.tensor_copy(
                                     q_fm.ap()[:, oc * TT:(oc + 1) * TT], bk.ap()[:, :]),
                                 inc=f"{tag}_ve_q")
                        veq = P.cur(f"{tag}_ve_q")
                        P.drain("vector")
                        P.wait("vector", f"{tag}_pe_ss", P.cur(f"{tag}_pe_ss"))
                        for c in range(NC):
                            P.do("vector",
                                 lambda e, c=c: e.tensor_mul(
                                     asc.ap()[:, c * TT:(c + 1) * TT],
                                     q_fm.ap()[:, c * TT:(c + 1) * TT],
                                     q_fm.ap()[:, c * TT:(c + 1) * TT]),
                                 inc=f"{tag}_ve_q2" if c == NC - 1 else None)
                        q2_done = P.cur(f"{tag}_ve_q2")
                        P.drain("vector")
                        P.wait("tensor", f"{tag}_ve_q2", q2_done)
                        P.wait("tensor", f"{tag}_sc_qn", P.cur(f"{tag}_sc_qn"))
                        for c in range(NC):
                            P.do("tensor",
                                 lambda e, c=c: e.matmul(
                                     pb[2].ap()[0:16, :],
                                     cbf.ap()[:, CB_BLK16 + c * 16:CB_BLK16 + (c + 1) * 16],
                                     asc.ap()[:, c * TT:(c + 1) * TT],
                                     start=(c == 0), stop=(c == NC - 1)),
                                 inc=f"{tag}_pe_ss" if c == NC - 1 else None)
                        ss_done = P.cur(f"{tag}_pe_ss")
                        P.wait("vector", f"{tag}_pe_ss", ss_done)
                        P.do("vector", lambda e: e.tensor_scalar_max(qnr.ap()[:, :],
                                                                     pb[2].ap()[0:16, :], 1e-18),
                             inc=f"{tag}_ve_ssm")
                        ssm_done = P.cur(f"{tag}_ve_ssm")
                        P.drain("vector")
                        P.wait("scalar", f"{tag}_ve_ssm", ssm_done)
                        P.do("scalar", lambda e: e.activation(qnr.ap()[:, :], qnr.ap()[:, :], Act.Ln))
                        P.drain("scalar")
                        P.do("scalar", lambda e: e.activation(qnr.ap()[:, :], qnr.ap()[:, :],
                                                              Act.Exp, scale=-0.5), inc=f"{tag}_sc_qn")
                        qn_done = P.cur(f"{tag}_sc_qn")
                        P.wait("vector", f"{tag}_sc_qn", qn_done)
                        P.do("vector", lambda e, tt=tt: e.tensor_scalar(
                            fwr.ap()[:, :], prows.ap()[:, tt * TT:(tt + 1) * TT],
                            0.3, 0.5, Alu.mult, Alu.add, accum_out=None))
                        P.drain("vector")
                        P.do("vector", lambda e: e.tensor_mul(rtile.ap()[0:16, :], fwr.ap()[:, :],
                                                              qnr.ap()[:, :]))
                        P.drain("vector")
                        P.do("vector", lambda e: e.tensor_scalar(
                            fwr.ap()[:, :], fwr.ap()[:, :], -1.0, 1.0,
                            Alu.mult, Alu.add, accum_out=None))
                        P.drain("vector")
                        P.do("vector", lambda e: e.tensor_mul(rtile.ap()[32:48, :], fwr.ap()[:, :],
                                                              qnr.ap()[:, :]), inc=f"{tag}_ve_g")
                        g_done = P.cur(f"{tag}_ve_g")
                        P.drain("vector")

                        for blk in range(NBLK):
                            P.wait("tensor", f"{tag}_ve_g", g_done)
                            P.wait("tensor", f"{tag}_ve_gt", P.cur(f"{tag}_ve_gt"))
                            P.wait("tensor", f"{tag}_ve_of", P.cur(f"{tag}_ve_of"))
                            P.do("tensor", lambda e, blk=blk: e.transpose(
                                pb[3].ap()[:, 0:64], rtile.ap()[0:64, blk * 128:(blk + 1) * 128],
                                ident[0:64, 0:64]), inc=f"{tag}_pe_gt")
                            pegt = P.cur(f"{tag}_pe_gt")
                            P.wait("vector", f"{tag}_pe_gt", pegt)
                            P.do("vector", lambda e: e.tensor_copy(gT[:], pb[3].ap()[:, 0:64]),
                                 inc=f"{tag}_ve_gt")
                            P.drain("vector")
                            P.wait("tensor", f"{tag}_ve_gt", P.cur(f"{tag}_ve_gt"))
                            P.wait("tensor", f"{tag}_ve_o", P.cur(f"{tag}_ve_o"))
                            P.wait("tensor", f"{tag}_bd", bd_done)
                            for c in range(NC):
                                qs = q_fm.ap()[:, c * TT + blk * 128:c * TT + blk * 128 + 128]
                                P.do("tensor",
                                     lambda e, c=c, qs=qs: e.matmul(
                                         pb[4 + c // 4].ap()[:, (c % 4) * 128:(c % 4) * 128 + 128],
                                         qs, bdf.ap()[:, c * 128:(c + 1) * 128],
                                         start=True, stop=True))
                                P.do("tensor",
                                     lambda e, c=c, qs=qs: e.matmul(
                                         pb[6 + c // 4].ap()[:, (c % 4) * 128:(c % 4) * 128 + 128],
                                         qs, bds.ap()[:, c * 128:(c + 1) * 128],
                                         start=True, stop=True),
                                     inc=f"{tag}_pe_bd" if c == NC - 1 else None)
                            bd_mm = P.cur(f"{tag}_pe_bd")
                            P.wait("vector", f"{tag}_pe_bd", bd_mm)
                            for half in range(2):
                                P.do("vector", lambda e, half=half: e.tensor_tensor(
                                    o32.ap()[:, half * 512:(half + 1) * 512]
                                       .rearrange("p (h d) -> p h d", d=DK),
                                    pb[4 + half].ap()[:, :].rearrange("p (h d) -> p h d", d=DK),
                                    gT.ap()[:, half * 8:half * 8 + 8].unsqueeze(2)
                                      .broadcast_to((128, 8, DK)), Alu.mult))
                                P.do("vector", lambda e, half=half: e.tensor_tensor(
                                    pb[half].ap()[:, :].rearrange("p (h d) -> p h d", d=DK),
                                    pb[6 + half].ap()[:, :].rearrange("p (h d) -> p h d", d=DK),
                                    gT.ap()[:, 32 + half * 8:32 + half * 8 + 8].unsqueeze(2)
                                      .broadcast_to((128, 8, DK)), Alu.mult))
                            P.drain("vector")
                            for half in range(2):
                                P.do("vector", lambda e, half=half: e.tensor_add(
                                    o32.ap()[:, half * 512:(half + 1) * 512],
                                    o32.ap()[:, half * 512:(half + 1) * 512],
                                    pb[half].ap()[:, :]),
                                    inc=f"{tag}_ve_o" if half == 1 else None)
                            veo = P.cur(f"{tag}_ve_o")
                            P.drain("vector")
                            P.wait("tensor", f"{tag}_ve_o", veo)
                            P.wait("tensor", f"{tag}_ve_of", P.cur(f"{tag}_ve_of"))
                            for c in range(NC):
                                P.do("tensor",
                                     lambda e, c=c: e.transpose(
                                         pb[2 + c // 4].ap()[:, (c % 4) * 128:(c % 4) * 128 + 128],
                                         o32.ap()[:, c * 128:(c + 1) * 128], ident),
                                     inc=f"{tag}_pe_ot" if c == NC - 1 else None)
                            peot = P.cur(f"{tag}_pe_ot")
                            P.wait("vector", f"{tag}_pe_ot", peot)
                            for c in range(NC):
                                P.do("vector",
                                     lambda e, c=c, blk=blk: e.tensor_copy(
                                         asc.ap()[:, c * TT + blk * 128:c * TT + blk * 128 + 128],
                                         pb[2 + c // 4].ap()[:, (c % 4) * 128:(c % 4) * 128 + 128]),
                                     inc=f"{tag}_ve_of" if c == NC - 1 else None)
                            P.drain("vector")

                        of_all = P.cur(f"{tag}_ve_of")
                        if dbg and tt == 0:
                            P.wait("gpsimd", f"{tag}_ve_q", P.cur(f"{tag}_ve_q"))
                            P.do("gpsimd", lambda e: e.dma_start(out=d_qfm, in_=q_fm[:]),
                                 inc=f"{tag}_dbg", n=16)
                            P.wait("gpsimd", f"{tag}_ve_of", of_all)
                            P.do("gpsimd", lambda e: e.dma_start(out=d_ofm, in_=asc[:]),
                                 inc=f"{tag}_dbg", n=16)
                        P.wait("tensor", f"{tag}_wl", wl2)
                        P.wait("tensor", f"{tag}_ve_of", of_all)
                        for oc in range(NC):
                            bk = pb[oc % 2]
                            P.wait("tensor", f"{tag}_ve_ro", max(0, P.cur(f"{tag}_ve_ro") - 1))
                            for c in range(NC):
                                P.do("tensor",
                                     lambda e, oc=oc, c=c, bk=bk: e.matmul(
                                         bk.ap()[:, :],
                                         wo_i.ap()[:, c * D + oc * 128:c * D + oc * 128 + 128],
                                         asc.ap()[:, c * TT:(c + 1) * TT],
                                         start=(c == 0), stop=(c == NC - 1)),
                                     inc=f"{tag}_pe_wo" if c == NC - 1 else None)
                            pewo = P.cur(f"{tag}_pe_wo")
                            P.wait("vector", f"{tag}_pe_wo", pewo)
                            P.wait("vector", "f_rl0", s2rl_base + 16 * (tt + 1))
                            P.do("vector",
                                 lambda e, oc=oc, bk=bk, rt=rt: e.tensor_add(
                                     rt[:, oc * TT:(oc + 1) * TT], rt[:, oc * TT:(oc + 1) * TT],
                                     bk.ap()[:, :]),
                                 inc=f"{tag}_ve_ro")
                            P.drain("vector")
                        pv2["ve_ro"] = P.cur(f"{tag}_ve_ro")
                        P.wait("scalar", f"{tag}_ve_ro", pv2["ve_ro"])
                        P.do("scalar",
                             lambda e, tt=tt, rt=rt: e.dma_start(
                                 out=r_view_tile(tt), in_=rt.rearrange("p (c t) -> p c t", c=NC)),
                             inc="f_rs", n=16)

                P.wait("sync", "f_rs")

        if "ffn1" in enable or "ffn2" in enable:
            for e9 in range(9):
                P.do("gpsimd",
                     lambda e, e9=e9: e.dma_start(
                         out=w2bf[:, e9 * 4 * D:(e9 + 1) * 4 * D],
                         in_=w2f[:, e9 * 4 * D:(e9 + 1) * 4 * D]),
                     inc="w2cv", n=16)

        if "p0" in enable:
            transpose_pipe(
                "p0", pb[0:4],
                lambda e, blk, buf: e.dma_start(out=buf, in_=x_in[blk * 128:(blk + 1) * 128, :]),
                lambda e, blk, buf: e.dma_start(out=r_view_blk(blk),
                                                in_=buf.rearrange("p (c t) -> p c t", c=NC)),
            )
            P.fence()

        if "ffn1" in enable:
            ffn_phase("f1", first=True)
            P.fence()

        if "attn" in enable:
            attn_phase()
            P.fence()

        if "ffn2" in enable:
            ffn_phase("f2")
            P.fence()

        if "p4" in enable:
            transpose_pipe(
                "p4", pb[0:4],
                lambda e, blk, buf: e.dma_start(out=buf.rearrange("p (c t) -> p c t", c=NC),
                                                in_=r_view_blk(blk)),
                lambda e, blk, buf: e.dma_start(out=y_out[blk * 128:(blk + 1) * 128, :], in_=buf),
            )

        P.wait("sync", "tp_st0")
        P.wait("sync", "tp_st1")

        block = es.enter_context(nc.Block())
        P.emit(block)

    return nc



def _prep_inputs(L0_attn_x, L0_attn_Wq, L0_attn_Wk, L0_attn_Wv, L0_attn_Wb,
                 L0_attn_Wo, L0_attn_Wfd, L0_attn_bfd, L0_attn_Wsd, L0_attn_bsd,
                 L0_attn_Wtf1, L0_attn_btf1, L0_attn_Wtf2, L0_attn_btf2,
                 L0_ffn_Ws1, L0_ffn_Ws2, L0_ffn_W1, L0_ffn_W2, L0_ffn_Wr):
    f32 = np.float32

    def img_kc(W):
        K, M = W.shape
        return np.ascontiguousarray(
            W.reshape(K // 128, 128, M).transpose(1, 0, 2).reshape(128, (K // 128) * M)
        ).astype(f32)

    wq = img_kc(np.asarray(L0_attn_Wq))
    wk = img_kc(np.asarray(L0_attn_Wk))
    wv = img_kc(np.asarray(L0_attn_Wv))
    wo = img_kc(np.asarray(L0_attn_Wo))
    wbf = img_kc(np.asarray(L0_attn_Wb))
    wfdf = img_kc(np.asarray(L0_attn_Wfd))
    wsdf = img_kc(np.asarray(L0_attn_Wsd))
    wrf = img_kc(np.asarray(L0_ffn_Wr))

    W1 = np.asarray(L0_ffn_W1)
    Ws1 = np.asarray(L0_ffn_Ws1)
    w1all = np.concatenate([W1, Ws1[None]], axis=0)
    w1f = np.ascontiguousarray(
        w1all.reshape(9, NC, 128, ED).transpose(2, 0, 1, 3).reshape(128, 9 * NC * ED)
    ).astype(f32)

    W2 = np.asarray(L0_ffn_W2)
    Ws2 = np.asarray(L0_ffn_Ws2)
    w2all = np.concatenate([W2, Ws2[None]], axis=0)
    w2f = np.ascontiguousarray(
        w2all.reshape(9, 4, 128, D).transpose(2, 0, 1, 3).reshape(128, 9 * 4 * D)
    ).astype(f32)

    Wtf1 = np.asarray(L0_attn_Wtf1).astype(f32)
    wtf1f = np.concatenate([Wtf1, Wtf1], axis=0)

    Wtf2 = np.asarray(L0_attn_Wtf2).astype(f32).reshape(32)
    bd = np.zeros((512, 16), f32)
    for h in range(16):
        bd[h * 32:(h + 1) * 32, h] = Wtf2
    wtf2f = np.ascontiguousarray(
        bd.reshape(4, 128, 16).transpose(1, 0, 2).reshape(128, 64)
    )

    consts = np.zeros((128, CW), f32)
    consts[:, C_ID:C_ID + 128] = np.eye(128, dtype=f32)
    consts[:, C_ONES:C_ONES + 128] = 1.0
    consts[:, C_BFD] = np.tile(np.asarray(L0_attn_bfd).astype(f32), 8)
    consts[:, C_BSD] = np.tile(np.asarray(L0_attn_bsd).astype(f32), 8)
    consts[:, C_BTF1] = np.tile(np.asarray(L0_attn_btf1).astype(f32), 4)
    consts[:, C_BTF2] = float(np.asarray(L0_attn_btf2).reshape(-1)[0])
    consts[:, C_EPSLN] = EPS_LN
    consts[:, C_ZERO] = 0.0
    consts[:, C_EPS6] = 1e-6
    consts[:, C_EPS18] = 1e-18

    cbf = np.zeros((128, CBW), ml_dtypes.bfloat16)
    cbf[:, CB_ONESCOL] = 1.0
    cbf[0:64, CB_BLK] = 1.0
    cbf[64:128, CB_BLK + 1] = 1.0
    cbf[:, CB_ONESROW:CB_ONESROW + 128] = 1.0
    for e in range(NE):
        cbf[e, CB_SEL8 + e * 128:CB_SEL8 + (e + 1) * 128] = 1.0
    for c in range(NC):
        cbf[0:64, CB_BLK16 + c * 16 + 2 * c] = 1.0
        cbf[64:128, CB_BLK16 + c * 16 + 2 * c + 1] = 1.0

    x = np.asarray(L0_attn_x).astype(f32)
    weights = dict(wq=wq, wk=wk, wv=wv, wo=wo, w1f=w1f, w2f=w2f, wrf=wrf,
                   wbf=wbf, wfdf=wfdf, wsdf=wsdf, wtf1f=wtf1f, wtf2f=wtf2f,
                   consts=consts, cbf=cbf)
    shards = []
    for c in range(NCORE):
        b, h = c // 2, c % 2
        shard = np.ascontiguousarray(x[b, h * T:(h + 1) * T, :])
        m = dict(weights)
        m["x"] = shard
        shards.append(m)
    return shards


def _run(inputs, enable=("p0", "ffn1", "attn", "ffn2", "p4")):
    in_maps = _prep_inputs(**inputs)
    nc = build_program(enable)
    res = run_bass_kernel_spmd(nc, in_maps, core_ids=list(range(NCORE)))
    out = np.empty((B, L, D), np.float32)
    for c in range(NCORE):
        b, h = c // 2, c % 2
        out[b, h * T:(h + 1) * T, :] = res.results[c]["y"]
    return out


def kernel(**inputs):
    return _run(inputs)


# version 08 (final)
# speedup vs baseline: 1.1207x; 1.0256x over previous
import numpy as np
import ml_dtypes
import concourse.bass as bass
import concourse.mybir as mybir
from concourse.bass_utils import run_bass_kernel_spmd

F32 = mybir.dt.float32
BF16 = mybir.dt.bfloat16
Act = mybir.ActivationFunctionType
Alu = mybir.AluOpType

B, L, D = 4, 8192, 1024
H, DK = 16, 64
ED, NE = 512, 8
NCORE = 8
T = 4096
TT = 512
NTILE = T // TT
NBLK = TT // 128
NC = 8
EPS_LN = 1e-5
W2_SCRATCH = False

ENGS = ("sync", "gpsimd", "tensor", "vector", "scalar")

C_ID = 0
C_ONES = 128
C_BFD = 256
C_BSD = 257
C_BTF1 = 258
C_BTF2 = 259
C_EPSLN = 260
C_ZERO = 261
C_EPS6 = 262
C_EPS18 = 263
CW = 288

CB_ONESCOL = 0
CB_BLK = 8
CB_ONESROW = 24
CB_SEL8 = 152
CB_BLK16 = 152 + 8 * 128
CBW = CB_BLK16 + 8 * 16


class Prog:

    def __init__(self, nc, es):
        self.nc = nc
        self.es = es
        self.ops = {e: [] for e in ENGS}
        self.sems = {}
        self.cnt = {}
        self.dma_sems = set()
        self._hw = {e: {} for e in ENGS}

    def sem(self, name):
        if name not in self.sems:
            self.sems[name] = self.es.enter_context(self.nc.semaphore(name))
            self.cnt[name] = 0
        return name

    def cur(self, name):
        self.sem(name)
        return self.cnt[name]

    def wait(self, eng, sem, val=None):
        self.sem(sem)
        v = self.cnt[sem] if val is None else val
        if v <= 0:
            return
        if self._hw[eng].get(sem, 0) >= v:
            return
        self._hw[eng][sem] = v
        self.ops[eng].append(["w", sem, v])

    def drain(self, eng):
        ops = self.ops[eng]
        if ops and ops[-1][0] == "d":
            return
        ops.append(["d"])

    def do(self, eng, fn, inc=None, n=1):
        if inc is not None:
            self.sem(inc)
            self.cnt[inc] += n
            if n == 16:
                self.dma_sems.add(inc)
        self.ops[eng].append(["o", fn, inc, n, self.cnt[inc] if inc else 0])
        return self.cnt[inc] if inc is not None else None

    def fence(self, tag=""):
        rel = {}
        for e in ENGS:
            for it in reversed(self.ops[e]):
                if it[0] == "o":
                    if it[2] is None:
                        self.sem(f"fence_{e}")
                        self.cnt[f"fence_{e}"] += 1
                        it[2] = f"fence_{e}"
                        it[3] = 1
                        it[4] = self.cnt[f"fence_{e}"]
                    rel[e] = (it[2], it[4])
                    break
        for e in ENGS:
            for o in ENGS:
                if o != e and o in rel:
                    self.wait(e, rel[o][0], rel[o][1])
            for s in sorted(self.dma_sems):
                self.wait(e, s)

    def emit(self, block):
        for eng_name in ENGS:
            ops = self.ops[eng_name]

            def make(ops=ops):
                def section(e):
                    for item in ops:
                        k = item[0]
                        if k == "w":
                            e.wait_ge(self.sems[item[1]], item[2])
                        elif k == "d":
                            e.drain()
                        else:
                            inst = item[1](e)
                            if item[2] is not None:
                                inst.then_inc(self.sems[item[2]], item[3])
                return section

            getattr(block, eng_name)(make())


def build_program(enable=("p0", "ffn1", "attn", "ffn2", "p4"), n_pairs=4):
    nc = bass.Bass("TRN2", target_bir_lowering=False)

    x_in = nc.dram_tensor("x", [T, D], F32, kind="ExternalInput").ap()
    y_out = nc.dram_tensor("y", [T, D], F32, kind="ExternalOutput").ap()

    wimg = {}
    for name in ("wq", "wk", "wv", "wo"):
        wimg[name] = nc.dram_tensor(name, [128, NC * D], F32, kind="ExternalInput").ap()
    w1f = nc.dram_tensor("w1f", [128, 9 * NC * ED], F32, kind="ExternalInput").ap()
    w2f = nc.dram_tensor("w2f", [128, 9 * 4 * D], F32, kind="ExternalInput").ap()
    wrf = nc.dram_tensor("wrf", [128, NC * NE], F32, kind="ExternalInput").ap()
    wbf = nc.dram_tensor("wbf", [128, NC * H], F32, kind="ExternalInput").ap()
    wfdf = nc.dram_tensor("wfdf", [128, NC * H], F32, kind="ExternalInput").ap()
    wsdf = nc.dram_tensor("wsdf", [128, NC * H], F32, kind="ExternalInput").ap()
    wtf1f = nc.dram_tensor("wtf1f", [128, 32], F32, kind="ExternalInput").ap()
    wtf2f = nc.dram_tensor("wtf2f", [128, 4 * H], F32, kind="ExternalInput").ap()
    consts = nc.dram_tensor("consts", [128, CW], F32, kind="ExternalInput").ap()
    cbf_in = nc.dram_tensor("cbf", [128, CBW], BF16, kind="ExternalInput").ap()

    r_sc = nc.dram_tensor("r_sc", [128, NC * T], F32)
    w2bf = nc.dram_tensor("w2bf", [128, 9 * 4 * D], BF16)

    cc_in = nc.dram_tensor("cc_in", [128, 1028], F32)
    cc_out = nc.dram_tensor("cc_out", [256, 1028], F32)

    dbg = "dbg" in enable
    if dbg:
        d_flux = nc.dram_tensor("d_flux", [16, T], F32, kind="ExternalOutput").ap()
        d_facc = nc.dram_tensor("d_facc", [16, 8], F32, kind="ExternalOutput").ap()
        d_accM = nc.dram_tensor("d_accM", [128, 1024], F32, kind="ExternalOutput").ap()
        d_mn = nc.dram_tensor("d_mn", [128, 1024], F32, kind="ExternalOutput").ap()
        d_qfm = nc.dram_tensor("d_qfm", [128, NC * TT], F32, kind="ExternalOutput").ap()
        d_ofm = nc.dram_tensor("d_ofm", [128, NC * TT], F32, kind="ExternalOutput").ap()
        d_mb = nc.dram_tensor("d_mb", [128, 1024], F32, kind="ExternalOutput").ap()
        d_rab = nc.dram_tensor("d_rab", [16, 8], F32, kind="ExternalOutput").ap()
        d_dcol = nc.dram_tensor("d_dcol", [16, 4], F32, kind="ExternalOutput").ap()
        d_ebc = nc.dram_tensor("d_ebc", [128, 16], F32, kind="ExternalOutput").ap()
        d_dbc = nc.dram_tensor("d_dbc", [128, 16], F32, kind="ExternalOutput").ap()

    from contextlib import ExitStack
    with ExitStack() as es:
        def sb(name, shape, dt):
            return es.enter_context(nc.sbuf_tensor(name, shape, dt))

        P = Prog(nc, es)

        cst = sb("cst", [128, CW], F32)
        cbf = sb("cbfs", [128, CBW], BF16)
        ident = cst.ap()[:, C_ID:C_ID + 128]
        ones_col_f32 = cst.ap()[:, C_ONES:C_ONES + 1]
        ones_row_f32 = cst.ap()[0:1, C_ONES:C_ONES + 128]
        ones_row_bf = cbf.ap()[0:1, CB_ONESROW:CB_ONESROW + 128]
        pb = [es.enter_context(nc.psum_tensor(f"bank{i}", [128, 512], F32)) for i in range(8)]

        P.do("sync", lambda e: e.dma_start(out=cst[:], in_=consts), inc="cl", n=16)
        P.do("sync", lambda e: e.dma_start(out=cbf[:], in_=cbf_in), inc="cl", n=16)

        def r_view_blk(blk):
            return r_sc.ap().rearrange("p (c t) -> p c t", c=NC)[:, :, blk * 128:(blk + 1) * 128]

        def r_view_tile(tt):
            return r_sc.ap().rearrange("p (c t) -> p c t", c=NC)[:, :, tt * TT:(tt + 1) * TT]

        from contextlib import ExitStack as _ES0

        def transpose_pipe(tag, banks, src_dma, dst_dma):
            sbtag, tag = tag, "tp"
            with _ES0() as tpes:
                a_buf = tpes.enter_context(nc.sbuf_tensor(f"{sbtag}_a", [128, 2 * D], F32))
                b_buf = tpes.enter_context(nc.sbuf_tensor(f"{sbtag}_b", [128, 2 * D], F32))
                for blk in range(T // 128):
                    par = blk % 2
                    asl = a_buf.ap()[:, par * D:(par + 1) * D]
                    bsl = b_buf.ap()[:, par * D:(par + 1) * D]
                    P.wait("sync", f"{tag}_pe{par}")
                    P.do("sync", lambda e, blk=blk, asl=asl: src_dma(e, blk, asl),
                         inc=f"{tag}_ld{par}", n=16)
                    P.wait("tensor", f"{tag}_ld{par}")
                    P.wait("tensor", "cl", 32)
                    P.wait("tensor", f"{tag}_ve{par}")
                    for c in range(NC):
                        pt = banks[par * 2 + c // 4].ap()[:, (c % 4) * 128:(c % 4) * 128 + 128]
                        P.do("tensor",
                             lambda e, pt=pt, asl=asl, c=c: e.transpose(pt, asl[:, c * 128:(c + 1) * 128], ident),
                             inc=f"{tag}_pe{par}")
                    P.wait("vector", f"{tag}_pe{par}")
                    P.wait("vector", f"{tag}_st{par}")
                    for c in range(NC):
                        pt = banks[par * 2 + c // 4].ap()[:, (c % 4) * 128:(c % 4) * 128 + 128]
                        P.do("vector",
                             lambda e, pt=pt, bsl=bsl, c=c: e.tensor_copy(bsl[:, c * 128:(c + 1) * 128], pt),
                             inc=f"{tag}_ve{par}" if c == NC - 1 else None)
                    P.wait("scalar", f"{tag}_ve{par}")
                    P.do("scalar", lambda e, blk=blk, bsl=bsl: dst_dma(e, blk, bsl),
                         inc=f"{tag}_st{par}", n=16)
                P.wait("sync", f"{tag}_st0")
                P.wait("sync", f"{tag}_st1")

        from contextlib import ExitStack as _ES

        def ffn_phase(tag, first=False):
            pes = _ES()
            with pes:
                def psb(name, shape, dt):
                    return pes.enter_context(nc.sbuf_tensor(name, shape, dt))
                sbtag, tag = tag, "f"

                w1i = psb(f"{sbtag}_w1i", [128, 9 * NC * ED], BF16)
                wri = psb(f"{sbtag}_wri", [128, NC * NE], BF16)
                w2b = psb(f"{sbtag}_w2b", [128, 2 * 4 * D], BF16)
                r_t = psb(f"{sbtag}_rt", [128, 2 * NC * TT], F32)
                lnb = psb(f"{sbtag}_lnb", [128, NC * TT], BF16)
                probb = psb(f"{sbtag}_probb", [128, NE * TT], BF16)
                h_sb = psb(f"{sbtag}_h", [128, 9 * 4 * TT], BF16)
                tmpa = psb(f"{sbtag}_tmpa", [128, 4 * TT], BF16)
                invs = psb(f"{sbtag}_invs", [128, 2 * TT], F32)
                ltm = psb(f"{sbtag}_ltm", [128, 64], F32)
                prow = psb(f"{sbtag}_prow", [8, TT], BF16)
                tmpo = psb(f"{sbtag}_tmpo", [128, TT], F32)
                mu_row = psb(f"{sbtag}_mu", [1, TT], F32)
                ex2_row = psb(f"{sbtag}_ex2", [1, TT], F32)
                inv_row = psb(f"{sbtag}_inv", [1, TT], F32)
                m1v = psb(f"{sbtag}_m1", [128, 8], F32)

                rb = lnb.ap()[:, :]
                sq = probb.ap()[:, :]
                invb_sb = invs.ap()[:, 0:TT]
                inv2h_sb = invs.ap()[:, TT:2 * TT]

                wl_base0 = P.cur(f"{tag}_wl")
                P.do("gpsimd", lambda e: e.dma_start(out=wri[:], in_=wrf), inc=f"{tag}_wl", n=16)
                w1_order = [8, 0, 1, 2, 3, 4, 5, 6, 7]
                wl_at = {}
                for pos, ee in enumerate(w1_order):
                    P.wait("gpsimd", f"{tag}_wl")
                    P.do("gpsimd",
                         lambda e, ee=ee: e.dma_start(
                             out=w1i.ap()[:, ee * NC * ED:(ee + 1) * NC * ED],
                             in_=w1f[:, ee * NC * ED:(ee + 1) * NC * ED]),
                         inc=f"{tag}_wl", n=16)
                    wl_at[ee] = P.cur(f"{tag}_wl")
                wl_done = P.cur(f"{tag}_wl")
                wl_router = wl_base0 + 16

                w2cnt = [0]
                grel = []
                w2l_base = [P.cur(f"{tag}_w2l0"), P.cur(f"{tag}_w2l1")]
                w2ls_base = [P.cur(f"{tag}_w2ls0"), P.cur(f"{tag}_w2ls1")]
                w2_src = {}
                w2w_base = P.cur(f"{tag}_w2w")

                use_bf = (not first) and ("attn" in enable)

                def w2_load():
                    g_idx = w2cnt[0]
                    par = g_idx % 2
                    e_idx = g_idx % 9
                    if g_idx >= 2:
                        s, v = grel[g_idx - 2]
                        P.wait("gpsimd", s, v)
                    if use_bf:
                        P.wait("gpsimd", "w2cv", 16 * (e_idx + 1))
                        srcv = w2bf[:, e_idx * 4 * D:(e_idx + 1) * 4 * D]
                    else:
                        srcv = w2f[:, e_idx * 4 * D:(e_idx + 1) * 4 * D]
                    P.do("gpsimd",
                         lambda e, par=par, srcv=srcv: e.dma_start(
                             out=w2b.ap()[:, par * 4 * D:(par + 1) * 4 * D], in_=srcv),
                         inc=f"{tag}_w2l{par}", n=16)
                    w2_src[g_idx] = (f"{tag}_w2l{par}", P.cur(f"{tag}_w2l{par}"))
                    w2cnt[0] += 1

                w2_load()
                w2_load()
                prev_veout = 0
                prev_veout_b = 0
                prev_peout = 0
                e_order = [8, 0, 1, 2, 3, 4, 5, 6, 7]
                tile_ctx = {}

                def h_mm(e_i, idx):
                    bks = pb[0:4] if idx % 2 == 0 else pb[4:8]
                    P.wait("tensor", f"{tag}_wl", tile_ctx["wl_at"][e_i])
                    P.wait("tensor", f"{tag}_sc_relu", max(0, P.cur(f"{tag}_sc_relu") - 1))
                    if idx == 0:
                        P.wait("tensor", f"{tag}_ve_var", tile_ctx["var_done"])
                    elif idx == 1:
                        P.wait("tensor", f"{tag}_ve_lg", tile_ctx["velg"])
                        P.wait("tensor", f"{tag}_ve_out", tile_ctx["veout_g"])
                    elif idx % 2 == 0:
                        P.wait("tensor", f"{tag}_ve_lnb", tile_ctx["lnb_done"])
                        P.wait("tensor", f"{tag}_ve_prow", tile_ctx["prow_done"])
                        P.wait("tensor", f"{tag}_b5free", P.cur(f"{tag}_b5free"))
                        P.wait("tensor", f"{tag}_b67free0", P.cur(f"{tag}_b67free0"))
                        P.wait("tensor", f"{tag}_b67free1", P.cur(f"{tag}_b67free1"))
                    lnb = tile_ctx["lnb"]
                    w1i = tile_ctx["w1i"]
                    tmpa = tile_ctx["tmpa"]
                    for hc in range(4):
                        for c in range(NC):
                            col = (e_i * NC + c) * ED + hc * 128
                            if idx <= 1 and hc == 0:
                                P.wait("tensor", f"{tag}_ve_lnb", tile_ctx["lnb_base"] + c + 1)
                            P.do("tensor",
                                 lambda e, hc=hc, col=col, bks=bks, c=c: e.matmul(
                                     bks[hc].ap()[:, :],
                                     w1i.ap()[:, col:col + 128],
                                     lnb.ap()[:, c * TT:(c + 1) * TT],
                                     start=(c == 0), stop=(c == NC - 1)),
                                 inc=f"{tag}_pe_h" if (hc == 3 and c == NC - 1) else None)
                    peh_done = P.cur(f"{tag}_pe_h")
                    P.wait("scalar", f"{tag}_pe_h", peh_done)
                    P.wait("scalar", f"{tag}_ve_h", P.cur(f"{tag}_ve_h"))
                    for hc in range(4):
                        P.do("scalar",
                             lambda e, hc=hc, bks=bks: e.activation(
                                 tmpa.ap()[:, hc * TT:(hc + 1) * TT], bks[hc].ap()[:, :], Act.Relu),
                             inc=f"{tag}_sc_relu" if hc == 3 else None)
                    return P.cur(f"{tag}_sc_relu")

                def h_evac(e_i, idx, relu_done):
                    tmpa = tile_ctx["tmpa"]
                    probb = tile_ctx["probb"]
                    h_sb = tile_ctx["h_sb"]
                    P.wait("vector", f"{tag}_sc_relu", relu_done)
                    hsl = h_sb.ap()[:, e_i * 4 * TT:(e_i + 1) * 4 * TT]
                    if e_i < NE:
                        pbb = probb.ap()[:, e_i * TT:(e_i + 1) * TT].unsqueeze(1) \
                                   .broadcast_to((128, 4, TT))
                    else:
                        pbb = inv2h_sb.unsqueeze(1).broadcast_to((128, 4, TT))
                    P.do("vector",
                         lambda e, pbb=pbb, hsl=hsl: e.tensor_tensor(
                             hsl.rearrange("p (a t) -> p a t", a=4),
                             tmpa.ap()[:, :].rearrange("p (a t) -> p a t", a=4),
                             pbb, Alu.mult))
                    P.drain("vector")
                    P.do("vector",
                         lambda e, hsl=hsl: e.tensor_mul(hsl, hsl, tmpa.ap()[:, :]),
                         inc=f"{tag}_ve_h")
                    P.drain("vector")

                def h_stage(e_i, idx):
                    h_evac(e_i, idx, h_mm(e_i, idx))

                rs_base = P.cur(f"{tag}_rs")
                rl_base = [P.cur(f"{tag}_rl0"), P.cur(f"{tag}_rl1")]

                def r_load(tt):
                    par = tt % 2
                    rt_ = r_t.ap()[:, par * NC * TT:(par + 1) * NC * TT]
                    P.wait("sync", f"{tag}_rs", rs_base + 16 * max(0, tt - 1))
                    P.do("sync",
                         lambda e, tt=tt, rt_=rt_: e.dma_start(
                             out=rt_.rearrange("p (c t) -> p c t", c=NC), in_=r_view_tile(tt)),
                         inc=f"{tag}_rl{par}", n=16)
                    return rt_

                def s1_rbsq(tt, rt_, peh_guard, veh_guard):
                    par = tt % 2
                    P.wait("vector", f"{tag}_rl{par}", rl_base[par] + 16 * (tt // 2 + 1))
                    P.wait("vector", f"{tag}_pe_h", peh_guard)
                    P.wait("vector", f"{tag}_ve_h", veh_guard)
                    P.do("vector", lambda e, rt_=rt_: e.tensor_copy(rb, rt_))
                    P.drain("vector")
                    P.do("vector", lambda e: e.tensor_mul(sq, rb, rb), inc=f"{tag}_ve_sq")
                    return P.cur(f"{tag}_ve_sq")

                rt_next = r_load(0)
                sq_next = s1_rbsq(0, rt_next, 0, 0)
                for tt in range(NTILE):
                    rt = rt_next
                    sq_done = sq_next

                    P.wait("tensor", "cl", 32)
                    P.wait("tensor", f"{tag}_ve_sq", sq_done)
                    P.wait("tensor", f"{tag}_ve_out", min(prev_veout, prev_veout_b + 2))
                    for c in range(NC):
                        P.do("tensor",
                             lambda e, c=c: e.matmul(pb[0].ap()[0:1, :],
                                                     cbf.ap()[:, CB_ONESCOL:CB_ONESCOL + 1],
                                                     rb[:, c * TT:(c + 1) * TT],
                                                     start=(c == 0), stop=(c == NC - 1)))
                    for c in range(NC):
                        P.do("tensor",
                             lambda e, c=c: e.matmul(pb[1].ap()[0:1, :],
                                                     cbf.ap()[:, CB_ONESCOL:CB_ONESCOL + 1],
                                                     sq[:, c * TT:(c + 1) * TT],
                                                     start=(c == 0), stop=(c == NC - 1)),
                             inc=f"{tag}_pe_st" if c == NC - 1 else None)
                    stats_done = P.cur(f"{tag}_pe_st")

                    P.wait("vector", f"{tag}_pe_st", stats_done)
                    P.do("vector", lambda e: e.tensor_scalar_mul(mu_row[:], pb[0].ap()[0:1, :], 1.0 / D))
                    P.do("vector", lambda e: e.tensor_scalar_mul(ex2_row[:], pb[1].ap()[0:1, :], 1.0 / D))
                    P.drain("vector")
                    P.do("vector", lambda e: e.tensor_mul(tmpo.ap()[0:1, :], mu_row[:], mu_row[:]))
                    P.drain("vector")
                    P.do("vector", lambda e: e.tensor_sub(ex2_row[:], ex2_row[:], tmpo.ap()[0:1, :]),
                         inc=f"{tag}_ve_var")
                    var_done = P.cur(f"{tag}_ve_var")
                    P.wait("scalar", f"{tag}_ve_var", var_done)
                    P.do("scalar", lambda e: e.activation(ex2_row[:], ex2_row[:], Act.Ln,
                                                          bias=cst.ap()[0:1, C_EPSLN:C_EPSLN + 1]))
                    P.drain("scalar")
                    P.do("scalar", lambda e: e.activation(inv_row[:], ex2_row[:], Act.Exp, scale=-0.5),
                         inc=f"{tag}_sc_inv")
                    inv_done = P.cur(f"{tag}_sc_inv")

                    P.wait("tensor", f"{tag}_ve_var", var_done)
                    P.wait("tensor", f"{tag}_sc_inv", inv_done)
                    P.wait("tensor", f"{tag}_ve_out", min(prev_veout, prev_veout_b + 4))
                    P.do("tensor", lambda e: e.matmul(pb[2].ap()[:, :], ones_row_f32, mu_row[:],
                                                      start=True, stop=True))
                    P.do("tensor", lambda e: e.matmul(pb[3].ap()[:, :], ones_row_f32, inv_row[:],
                                                      start=True, stop=True), inc=f"{tag}_pe_bc")
                    bc_done = P.cur(f"{tag}_pe_bc")

                    P.wait("vector", f"{tag}_pe_bc", bc_done)
                    P.wait("vector", f"{tag}_pe_st", stats_done)
                    P.do("vector", lambda e: e.tensor_copy(invb_sb, pb[3].ap()[:, :]))
                    P.drain("vector")
                    P.do("vector", lambda e: e.scalar_tensor_tensor(inv2h_sb, pb[3].ap()[:, :], 0.5,
                                                                    invb_sb, Alu.mult, Alu.mult))
                    lnb_base = P.cur(f"{tag}_ve_lnb")
                    for c in range(NC):
                        P.do("vector",
                             lambda e, c=c, rt=rt: e.tensor_sub(lnb.ap()[:, c * TT:(c + 1) * TT],
                                                                rt[:, c * TT:(c + 1) * TT],
                                                                pb[2].ap()[:, :]),
                             inc=f"{tag}_ve_lnb")
                        P.drain("vector")
                    lnb_done = P.cur(f"{tag}_ve_lnb")
                    tile_ctx.update(wl_done=wl_done, wl_at=wl_at, var_done=var_done,
                                    lnb_done=lnb_done, lnb_base=lnb_base, lnb=lnb, w1i=w1i,
                                    tmpa=tmpa, probb=probb, h_sb=h_sb)

                    P.wait("tensor", f"{tag}_wl", wl_router)
                    P.wait("tensor", f"{tag}_ve_out", min(prev_veout, prev_veout_b + 5))
                    for c in range(NC):
                        P.wait("tensor", f"{tag}_ve_lnb", lnb_base + c + 1)
                        P.do("tensor",
                             lambda e, c=c: e.matmul(pb[4].ap()[0:8, :], wri.ap()[:, c * NE:(c + 1) * NE],
                                                     lnb.ap()[:, c * TT:(c + 1) * TT],
                                                     start=(c == 0), stop=(c == NC - 1)),
                             inc=f"{tag}_pe_lg" if c == NC - 1 else None)
                    lg_done = P.cur(f"{tag}_pe_lg")
                    P.wait("vector", f"{tag}_pe_lg", lg_done)
                    P.drain("vector")
                    P.do("vector", lambda e: e.tensor_mul(tmpo.ap()[0:8, :], pb[4].ap()[0:8, :],
                                                          invb_sb[0:8, :]),
                         inc=f"{tag}_ve_lg")
                    velg_done = P.cur(f"{tag}_ve_lg")
                    tile_ctx["velg"] = velg_done
                    tile_ctx["veout_g"] = min(prev_veout, prev_veout_b + 8)
                    relu8 = h_mm(8, 0)
                    h_evac(8, 0, relu8)
                    relu0 = h_mm(0, 1)
                    P.wait("tensor", f"{tag}_ve_lg", velg_done)
                    P.wait("tensor", f"{tag}_b5free", P.cur(f"{tag}_b5free"))
                    P.wait("tensor", f"{tag}_sc_relu", relu8)
                    for j in range(4):
                        P.do("tensor",
                             lambda e, j=j: e.transpose(pb[0].ap()[:, j * 8:(j + 1) * 8],
                                                        tmpo.ap()[0:8, j * 128:(j + 1) * 128],
                                                        ident[0:8, 0:8]),
                             inc=f"{tag}_pe_lt" if j == 3 else None)
                    lt_done = P.cur(f"{tag}_pe_lt")
                    P.wait("vector", f"{tag}_pe_lt", lt_done)
                    P.do("vector", lambda e: e.tensor_copy(ltm.ap()[:, 0:32], pb[0].ap()[:, 0:32]),
                         inc=f"{tag}_b5free")
                    P.drain("vector")
                    lv = ltm.ap()[:, 0:32].rearrange("p (a b) -> p a b", b=8)
                    mask = ltm.ap()[:, 32:64]
                    P.do("vector", lambda e: e.tensor_reduce(m1v.ap()[:, 0:4], lv,
                                                             mybir.AxisListType.X, Alu.max))
                    P.drain("vector")
                    m1rep = m1v.ap()[:, 0:4].unsqueeze(2).broadcast_to((128, 4, 8))
                    P.do("vector", lambda e: e.tensor_tensor(
                        mask.rearrange("p (a b) -> p a b", b=8), lv, m1rep, Alu.is_equal))
                    P.drain("vector")
                    P.do("vector", lambda e: e.scalar_tensor_tensor(
                        ltm.ap()[:, 0:32], mask, -1e30, ltm.ap()[:, 0:32], Alu.mult, Alu.add))
                    P.drain("vector")
                    P.do("vector", lambda e: e.tensor_reduce(
                        m1v.ap()[:, 4:8], ltm.ap()[:, 0:32].rearrange("p (a b) -> p a b", b=8),
                        mybir.AxisListType.X, Alu.max))
                    P.drain("vector")
                    P.do("vector", lambda e: e.tensor_sub(m1v.ap()[:, 4:8], m1v.ap()[:, 4:8],
                                                          m1v.ap()[:, 0:4]),
                         inc=f"{tag}_ve_m2")
                    m2_done = P.cur(f"{tag}_ve_m2")
                    P.wait("scalar", f"{tag}_ve_m2", m2_done)
                    P.do("scalar", lambda e: e.activation(m1v.ap()[:, 4:8], m1v.ap()[:, 4:8], Act.Exp),
                         inc=f"{tag}_sc_g")
                    g_done = P.cur(f"{tag}_sc_g")
                    P.wait("vector", f"{tag}_sc_g", g_done)
                    P.do("vector", lambda e: e.tensor_scalar_add(m1v.ap()[:, 4:8], m1v.ap()[:, 4:8], 1.0))
                    P.drain("vector")
                    P.do("vector", lambda e: e.reciprocal(m1v.ap()[:, 0:4], m1v.ap()[:, 4:8]))
                    P.drain("vector")
                    P.do("vector", lambda e: e.tensor_single_scalar(m1v.ap()[:, 0:4], m1v.ap()[:, 0:4],
                                                                    1.0, Alu.is_equal))
                    P.drain("vector")
                    is1rep = m1v.ap()[:, 0:4].unsqueeze(2).broadcast_to((128, 4, 8))
                    P.do("vector", lambda e: e.tensor_tensor(
                        ltm.ap()[:, 0:32].rearrange("p (a b) -> p a b", b=8),
                        mask.rearrange("p (a b) -> p a b", b=8), is1rep, Alu.mult),
                        inc=f"{tag}_ve_ptm")
                    ptm_done = P.cur(f"{tag}_ve_ptm")
                    P.wait("tensor", f"{tag}_ve_ptm", ptm_done)
                    for j in range(4):
                        P.do("tensor",
                             lambda e, j=j: e.transpose(pb[1].ap()[0:8, j * 128:(j + 1) * 128],
                                                        ltm.ap()[:, j * 8:(j + 1) * 8], ident),
                             inc=f"{tag}_pe_pt" if j == 3 else None)
                    pt_done = P.cur(f"{tag}_pe_pt")
                    P.wait("vector", f"{tag}_pe_pt", pt_done)
                    P.do("vector", lambda e: e.tensor_copy(prow[:], pb[1].ap()[0:8, 0:TT]),
                         inc=f"{tag}_ve_prow")
                    prow_done = P.cur(f"{tag}_ve_prow")
                    tile_ctx["prow_done"] = prow_done
                    P.wait("tensor", f"{tag}_ve_prow", prow_done)
                    P.wait("tensor", f"{tag}_ve_out", prev_veout)
                    P.wait("vector", f"{tag}_pe_st", stats_done)
                    P.wait("tensor", f"{tag}_sc_relu", relu8)
                    for e_i in range(NE):
                        bk = pb[2 + e_i % 2]
                        P.wait("tensor", f"{tag}_b67free{e_i % 2}", P.cur(f"{tag}_b67free{e_i % 2}"))
                        sel = cbf.ap()[0:8, CB_SEL8 + e_i * 128:CB_SEL8 + (e_i + 1) * 128]
                        P.do("tensor",
                             lambda e, bk=bk, sel=sel: e.matmul(bk.ap()[:, :], sel, prow.ap()[:, :],
                                                                start=True, stop=True),
                             inc=f"{tag}_pe_pb{e_i % 2}")
                        P.wait("vector", f"{tag}_pe_pb{e_i % 2}", P.cur(f"{tag}_pe_pb{e_i % 2}"))
                        P.do("vector",
                             lambda e, bk=bk, e_i=e_i: e.tensor_mul(
                                 probb.ap()[:, e_i * TT:(e_i + 1) * TT], bk.ap()[:, :], inv2h_sb),
                             inc=f"{tag}_b67free{e_i % 2}")
                    P.drain("vector")
                    h_evac(0, 1, relu0)

                    for idx, e_i in enumerate(e_order):
                        if idx <= 1:
                            continue
                        h_stage(e_i, idx)
                    if tt + 1 < NTILE:
                        rt_next = r_load(tt + 1)
                        sq_next = s1_rbsq(tt + 1, rt_next, P.cur(f"{tag}_pe_h"),
                                          P.cur(f"{tag}_ve_h"))
                    veh_done = P.cur(f"{tag}_ve_h")

                    P.wait("tensor", f"{tag}_ve_h", veh_done)
                    peout_b8 = P.cur(f"{tag}_pe_out")
                    for e_i in range(9):
                        g_idx = tt * 9 + e_i
                        wpar = g_idx % 2
                        P.wait("tensor", *w2_src[g_idx])
                        for oc in range(NC):
                            for hc in range(4):
                                wcol = wpar * 4 * D + hc * D + oc * 128
                                last_of_group = (oc == NC - 1 and hc == 3)
                                if e_i == 8 and hc == 3:
                                    inc = f"{tag}_pe_out"
                                elif e_i < 8 and last_of_group:
                                    inc = f"{tag}_w2c"
                                else:
                                    inc = None
                                P.do("tensor",
                                     lambda e, oc=oc, hc=hc, wcol=wcol, e_i=e_i: e.matmul(
                                         pb[oc].ap()[:, :],
                                         w2b.ap()[:, wcol:wcol + 128],
                                         h_sb.ap()[:, (e_i * 4 + hc) * TT:(e_i * 4 + hc + 1) * TT],
                                         start=(e_i == 0 and hc == 0), stop=(e_i == 8 and hc == 3)),
                                     inc=inc)
                        if e_i < 8:
                            grel.append((f"{tag}_w2c", P.cur(f"{tag}_w2c")))
                        else:
                            grel.append((f"{tag}_pe_out", P.cur(f"{tag}_pe_out")))
                        if w2cnt[0] < NTILE * 9:
                            w2_load()
                    peout_done = P.cur(f"{tag}_pe_out")

                    veout_base = P.cur(f"{tag}_ve_out")
                    for oc in range(NC):
                        P.wait("vector", f"{tag}_pe_out", peout_b8 + oc + 1)
                        P.do("vector",
                             lambda e, oc=oc, rt=rt: e.tensor_add(rt[:, oc * TT:(oc + 1) * TT],
                                                                  rt[:, oc * TT:(oc + 1) * TT],
                                                                  pb[oc].ap()[:, :]),
                             inc=f"{tag}_ve_out")
                    veout_done = P.cur(f"{tag}_ve_out")
                    prev_veout = veout_done
                    prev_veout_b = veout_base
                    prev_peout = peout_done

                    P.wait("scalar", f"{tag}_ve_out", veout_done)
                    P.wait("scalar", f"{tag}_rs", rs_base + 16 * tt)
                    P.do("scalar",
                         lambda e, tt=tt, rt=rt: e.dma_start(
                             out=r_view_tile(tt), in_=rt.rearrange("p (c t) -> p c t", c=NC)),
                         inc=f"{tag}_rs", n=16)

                P.wait("sync", f"{tag}_rs")

        def attn_phase(tag="a"):
            pes = _ES()
            with pes:
                def psb(name, shape, dt):
                    return pes.enter_context(nc.sbuf_tensor(name, shape, dt))
                sbtag = tag

                lnb_all = psb(f"{tag}_lnb", [128, NC * T], BF16)
                prows = psb(f"{tag}_prows", [16, T], F32)
                acc_M = psb(f"{tag}_accM", [128, H * DK], F32)
                facc = psb(f"{tag}_facc", [16, 8], F32)
                r_t = psb(f"{sbtag}_rt", [128, 2 * NC * TT], F32)
                asc = psb(f"{tag}_asc", [128, NC * TT], BF16)
                rtile = psb(f"{tag}_rtile", [128, TT], F32)
                rtile2 = psb(f"{tag}_rtile2", [128, TT], F32)
                fwr = psb(f"{tag}_fwr", [16, TT], F32)
                swr = psb(f"{tag}_swr", [16, TT], F32)
                qnr = psb(f"{tag}_qnr", [16, TT], F32)
                mu_row = psb(f"{sbtag}_mu", [1, TT], F32)
                ex2_row = psb(f"{sbtag}_ex2", [1, TT], F32)
                t_row = psb(f"{tag}_trow", [1, TT], F32)
                inv_row = psb(f"{sbtag}_inv", [1, TT], F32)
                ebc = psb(f"{tag}_ebc", [128, 16], F32)
                etr = psb(f"{tag}_etr", [32, 128], F32)
                azeros = psb(f"{tag}_az", [128, TT], F32)

                rb = asc.ap()[:, :]

                P.do("gpsimd", lambda e: e.memset(acc_M[:], 0.0), inc=f"{tag}_gz")
                P.do("gpsimd", lambda e: e.memset(facc[:], 0.0), inc=f"{tag}_gz")
                P.do("gpsimd", lambda e: e.memset(azeros[:], 0.0), inc=f"{tag}_gz")
                P.do("gpsimd", lambda e: e.memset(etr[:], 0.0), inc=f"{tag}_gz")
                P.do("gpsimd", lambda e: e.memset(rtile[:], 0.0), inc=f"{tag}_gz")
                P.do("gpsimd", lambda e: e.memset(rtile2[:], 0.0), inc=f"{tag}_gz")

                def bc_prep(col_f, col_s):
                    P.do("vector", lambda e, col_f=col_f: e.tensor_copy(etr.ap()[0:16, 0:1], col_f))
                    P.do("vector", lambda e, col_s=col_s: e.tensor_copy(etr.ap()[0:16, 32:33], col_s))
                    P.drain("vector")
                    P.do("vector", lambda e: e.transpose(etr.ap()[:, 64:96], etr.ap()[:, 0:32]))
                    P.do("vector", lambda e: e.transpose(etr.ap()[:, 96:128], etr.ap()[:, 32:64]),
                         inc=f"{tag}_ve_bc")
                    vb = P.cur(f"{tag}_ve_bc")
                    P.drain("vector")
                    return vb

                def bc_fs(col_f, col_s, bank, out_sb, vb=None):
                    if vb is None:
                        vb = bc_prep(col_f, col_s)
                    P.wait("tensor", f"{tag}_ve_bc", vb)
                    P.do("tensor", lambda e, bank=bank: e.matmul(
                        bank.ap()[0:64, 0:16], ones_row_f32[0:1, 0:64], etr.ap()[0:1, 64:80],
                        start=True, stop=True))
                    P.do("tensor", lambda e, bank=bank: e.matmul(
                        bank.ap()[64:128, 0:16], ones_row_f32[0:1, 0:64], etr.ap()[0:1, 96:112],
                        start=True, stop=True, tile_position=(0, 64)), inc=f"{tag}_pe_bcm")
                    pbcm = P.cur(f"{tag}_pe_bcm")
                    P.wait("vector", f"{tag}_pe_bcm", pbcm)
                    P.do("vector", lambda e, bank=bank, out_sb=out_sb: e.tensor_copy(
                        out_sb, bank.ap()[:, 0:16]), inc=f"{tag}_ve_bce")
                    P.drain("vector")
                    return P.cur(f"{tag}_ve_bce")

                def ln_stats_lnb(tt):
                    pass

                s1 = _ES()
                with s1:
                    def s1b(name, shape, dt):
                        return s1.enter_context(nc.sbuf_tensor(name, shape, dt))

                    wk_i = s1b(f"{tag}_wk", [128, NC * D], BF16)
                    wv_i = s1b(f"{tag}_wv", [128, NC * D], BF16)
                    wb_i = s1b(f"{tag}_wb", [128, NC * H], BF16)
                    wfd_i = s1b(f"{tag}_wfd", [128, NC * H], BF16)
                    wsd_i = s1b(f"{tag}_wsd", [128, NC * H], BF16)
                    wt1_i = s1b(f"{tag}_wt1", [128, 32], BF16)
                    wt2_i = s1b(f"{tag}_wt2", [128, 4 * H], BF16)
                    k_tm = s1b(f"{tag}_ktm", [128, D], BF16)
                    v_tm = s1b(f"{tag}_vtm", [128, D], BF16)
                    kbh32 = s1b(f"{tag}_kbh", [128, D], F32)
                    kk = s1b(f"{tag}_kk", [128, H * 128], BF16)
                    kbh_fm = s1b(f"{tag}_kfm", [128, NC * TT], BF16)
                    hmid_sb = s1b(f"{tag}_hmid", [128, 4 * TT], BF16)
                    k2t = hmid_sb
                    kninv = s1b(f"{tag}_kninv", [128, 4 * H], F32)
                    packT = s1b(f"{tag}_packT", [128, 96], F32)
                    c1t = s1b(f"{tag}_c1", [128, H], F32)

                    for wi, wf_ in ((wk_i, wimg["wk"]), (wv_i, wimg["wv"]), (wb_i, wbf),
                                    (wfd_i, wfdf), (wsd_i, wsdf), (wt1_i, wtf1f), (wt2_i, wtf2f)):
                        P.do("gpsimd", lambda e, wi=wi, wf_=wf_: e.dma_start(out=wi[:], in_=wf_),
                             inc=f"{tag}_wl", n=16)
                    wl1 = P.cur(f"{tag}_wl")
                    for e9 in range(9):
                        P.do("gpsimd",
                             lambda e, e9=e9: e.dma_start(
                                 out=w2bf[:, e9 * 4 * D:(e9 + 1) * 4 * D],
                                 in_=w2f[:, e9 * 4 * D:(e9 + 1) * 4 * D]),
                             inc="w2cv", n=16)

                    pv = dict(pe_st=0, pe_bc=0, pe_hm=0, sc_row=0, pe_tin=0, ve_r2=0,
                              ve_acc=0, pe_fx=0, ve_bce=0, pe_row=0)
                    prev_lnb = [0]
                    arl_base = [P.cur("f_rl0"), P.cur("f_rl1")]
                    sq = kbh_fm.ap()[:, :]

                    def a_r_load(tt):
                        par = tt % 2
                        rt_ = r_t.ap()[:, par * NC * TT:(par + 1) * NC * TT]
                        P.wait("sync", "f_rs")
                        P.wait("sync", f"{tag}_ve_lnb", prev_lnb[0])
                        P.do("sync",
                             lambda e, tt=tt, rt_=rt_: e.dma_start(
                                 out=rt_.rearrange("p (c t) -> p c t", c=NC), in_=r_view_tile(tt)),
                             inc="f_rl" + str(par), n=16)
                        return rt_

                    def a_s1(tt, rt_):
                        par = tt % 2
                        P.wait("vector", "f_rl" + str(par), arl_base[par] + 16 * (tt // 2 + 1))
                        P.wait("vector", f"{tag}_pe_st", pv["pe_st"])
                        P.wait("vector", f"{tag}_pe_hm", pv["pe_hm"])
                        P.do("vector", lambda e, rt_=rt_: e.tensor_copy(rb, rt_))
                        P.drain("vector")
                        P.do("vector", lambda e: e.tensor_mul(sq, rb, rb), inc=f"{tag}_ve_sq")
                        return P.cur(f"{tag}_ve_sq")

                    rt_next = a_r_load(0)
                    sq_next = a_s1(0, rt_next)
                    for tt in range(NTILE):
                        rt = rt_next
                        sq_done = sq_next

                        P.wait("tensor", "cl", 32)
                        P.wait("tensor", f"{tag}_ve_sq", sq_done)
                        P.wait("tensor", f"{tag}_sc_row", pv["sc_row"])
                        for c in range(NC):
                            P.do("tensor",
                                 lambda e, c=c: e.matmul(pb[0].ap()[0:1, :],
                                                         cbf.ap()[:, CB_ONESCOL:CB_ONESCOL + 1],
                                                         rb[:, c * TT:(c + 1) * TT],
                                                         start=(c == 0), stop=(c == NC - 1)))
                        for c in range(NC):
                            P.do("tensor",
                                 lambda e, c=c, sq=sq: e.matmul(
                                     pb[1].ap()[0:1, :], cbf.ap()[:, CB_ONESCOL:CB_ONESCOL + 1],
                                     sq[:, c * TT:(c + 1) * TT],
                                     start=(c == 0), stop=(c == NC - 1)),
                                 inc=f"{tag}_pe_st" if c == NC - 1 else None)
                        pv["pe_st"] = P.cur(f"{tag}_pe_st")

                        P.wait("vector", f"{tag}_pe_st", pv["pe_st"])
                        P.do("vector", lambda e: e.tensor_scalar_mul(mu_row[:], pb[0].ap()[0:1, :], 1.0 / D))
                        P.do("vector", lambda e: e.tensor_scalar_mul(ex2_row[:], pb[1].ap()[0:1, :], 1.0 / D))
                        P.drain("vector")
                        P.do("vector", lambda e: e.tensor_mul(t_row[:], mu_row[:], mu_row[:]))
                        P.drain("vector")
                        P.do("vector", lambda e: e.tensor_sub(ex2_row[:], ex2_row[:], t_row[:]),
                             inc=f"{tag}_ve_var")
                        var_done = P.cur(f"{tag}_ve_var")
                        P.wait("scalar", f"{tag}_ve_var", var_done)
                        P.do("scalar", lambda e: e.activation(ex2_row[:], ex2_row[:], Act.Ln,
                                                              bias=cst.ap()[0:1, C_EPSLN:C_EPSLN + 1]))
                        P.drain("scalar")
                        P.do("scalar", lambda e: e.activation(inv_row[:], ex2_row[:], Act.Exp, scale=-0.5),
                             inc=f"{tag}_sc_inv")
                        inv_done = P.cur(f"{tag}_sc_inv")

                        P.wait("tensor", f"{tag}_ve_var", var_done)
                        P.wait("tensor", f"{tag}_sc_inv", inv_done)
                        P.wait("tensor", f"{tag}_ve_lnb", P.cur(f"{tag}_ve_lnb"))
                        P.wait("tensor", f"{tag}_ve_r2", pv["ve_r2"])
                        P.wait("tensor", f"{tag}_ve_kv", P.cur(f"{tag}_ve_kv"))
                        P.do("tensor", lambda e: e.matmul(pb[2].ap()[:, :], ones_row_f32, mu_row[:],
                                                          start=True, stop=True))
                        P.do("tensor", lambda e: e.matmul(pb[3].ap()[0:16, :], ones_row_f32[0:1, 0:16],
                                                          inv_row[:], start=True, stop=True),
                             inc=f"{tag}_pe_bc")
                        pv["pe_bc"] = P.cur(f"{tag}_pe_bc")

                        P.wait("vector", f"{tag}_pe_bc", pv["pe_bc"])
                        for c in range(NC):
                            P.do("vector",
                                 lambda e, c=c, tt=tt, rt=rt: e.tensor_sub(
                                     lnb_all.ap()[:, c * T + tt * TT:c * T + tt * TT + TT],
                                     rt[:, c * TT:(c + 1) * TT], pb[2].ap()[:, :]),
                                 inc=f"{tag}_ve_lnb" if c == NC - 1 else None)
                        lnb_done = P.cur(f"{tag}_ve_lnb")
                        prev_lnb[0] = lnb_done
                        P.drain("vector")

                        P.wait("tensor", f"{tag}_wl", wl1)
                        P.wait("tensor", f"{tag}_ve_lnb", lnb_done)
                        P.wait("tensor", f"{tag}_ve_tin", P.cur(f"{tag}_ve_tin"))
                        for bi, wi in ((0, wb_i), (1, wfd_i), (4, wsd_i)):
                            for c in range(NC):
                                P.do("tensor",
                                     lambda e, bi=bi, wi=wi, c=c, tt=tt: e.matmul(
                                         pb[bi].ap()[0:16, :],
                                         wi.ap()[:, c * H:(c + 1) * H],
                                         lnb_all.ap()[:, c * T + tt * TT:c * T + tt * TT + TT],
                                         start=(c == 0), stop=(c == NC - 1)),
                                     inc=f"{tag}_pe_row" if (bi == 4 and c == NC - 1) else None)
                        pv["pe_row"] = P.cur(f"{tag}_pe_row")

                        P.wait("scalar", f"{tag}_pe_row", pv["pe_row"])
                        P.wait("scalar", f"{tag}_pe_tin", pv["pe_tin"])
                        P.wait("scalar", f"{tag}_ve_r2", pv["ve_r2"])
                        P.wait("scalar", f"{tag}_gz", P.cur(f"{tag}_gz"))
                        P.do("scalar", lambda e: e.activation(rtile.ap()[0:16, :], pb[0].ap()[0:16, :],
                                                              Act.Sigmoid))
                        P.do("scalar", lambda e: e.activation(rtile.ap()[32:48, :], pb[1].ap()[0:16, :],
                                                              Act.Sigmoid,
                                                              bias=cst.ap()[0:16, C_BFD:C_BFD + 1]))
                        P.do("scalar", lambda e: e.activation(rtile.ap()[64:80, :], pb[4].ap()[0:16, :],
                                                              Act.Sigmoid,
                                                              bias=cst.ap()[0:16, C_BSD:C_BSD + 1]))
                        P.drain("scalar")
                        P.do("scalar", lambda e: e.activation(rtile.ap()[32:48, :], rtile.ap()[32:48, :],
                                                              Act.Ln,
                                                              bias=cst.ap()[32:48, C_EPS6:C_EPS6 + 1]))
                        P.do("scalar", lambda e: e.activation(rtile.ap()[64:80, :], rtile.ap()[64:80, :],
                                                              Act.Ln,
                                                              bias=cst.ap()[64:80, C_EPS6:C_EPS6 + 1]),
                             inc=f"{tag}_sc_row")
                        pv["sc_row"] = P.cur(f"{tag}_sc_row")

                        P.wait("vector", f"{tag}_sc_row", pv["sc_row"])
                        P.wait("vector", f"{tag}_gz", P.cur(f"{tag}_gz"))
                        P.do("vector", lambda e: e.tensor_tensor_scan(
                            rtile.ap()[96:112, :], azeros.ap()[32:48, :], rtile.ap()[32:48, :], 0.0,
                            Alu.add, Alu.add))
                        P.drain("vector")
                        P.do("vector", lambda e: e.tensor_tensor_scan(
                            rtile2.ap()[64:80, :], azeros.ap()[64:80, :], rtile.ap()[64:80, :], 0.0,
                            Alu.add, Alu.add), inc=f"{tag}_ve_cs")
                        cs_done = P.cur(f"{tag}_ve_cs")
                        P.drain("vector")

                        P.wait("scalar", f"{tag}_ve_cs", cs_done)
                        P.do("scalar", lambda e: e.activation(fwr.ap()[:, :], rtile.ap()[96:112, :],
                                                              Act.Exp, scale=-1.0,
                                                              bias=rtile.ap()[96:112, TT - 1:TT]))
                        P.do("scalar", lambda e: e.activation(swr.ap()[:, :], rtile2.ap()[64:80, :],
                                                              Act.Exp, scale=-1.0,
                                                              bias=rtile2.ap()[64:80, TT - 1:TT]))
                        P.drain("scalar")
                        P.do("scalar", lambda e: e.activation(facc.ap()[:, 4:5],
                                                              rtile.ap()[96:112, TT - 1:TT], Act.Exp))
                        P.do("scalar", lambda e: e.activation(facc.ap()[:, 5:6],
                                                              rtile2.ap()[64:80, TT - 1:TT], Act.Exp),
                             inc=f"{tag}_sc_fw")
                        fw_done = P.cur(f"{tag}_sc_fw")

                        P.wait("vector", f"{tag}_sc_fw", fw_done)
                        P.do("vector", lambda e: e.tensor_copy(etr.ap()[0:16, 1:2],
                                                               rtile.ap()[96:112, TT - 1:TT]))
                        P.do("vector", lambda e: e.tensor_copy(etr.ap()[0:16, 2:3],
                                                               rtile2.ap()[64:80, TT - 1:TT]))
                        P.drain("vector")
                        P.do("vector", lambda e: e.tensor_add(facc.ap()[:, 0:1], facc.ap()[:, 0:1],
                                                              etr.ap()[0:16, 1:2]))
                        P.do("vector", lambda e: e.tensor_add(facc.ap()[:, 1:2], facc.ap()[:, 1:2],
                                                              etr.ap()[0:16, 2:3]))
                        P.drain("vector")
                        P.do("vector", lambda e: e.reduce_sum(facc.ap()[:, 6:7], fwr.ap()[:, :],
                                                              mybir.AxisListType.X))
                        P.do("vector", lambda e: e.reduce_sum(facc.ap()[:, 7:8], swr.ap()[:, :],
                                                              mybir.AxisListType.X))
                        P.drain("vector")
                        P.do("vector", lambda e: e.scalar_tensor_tensor(
                            facc.ap()[:, 2:3], facc.ap()[:, 2:3], facc.ap()[:, 4:5],
                            facc.ap()[:, 6:7], Alu.mult, Alu.add))
                        P.do("vector", lambda e: e.scalar_tensor_tensor(
                            facc.ap()[:, 3:4], facc.ap()[:, 3:4], facc.ap()[:, 5:6],
                            facc.ap()[:, 7:8], Alu.mult, Alu.add))
                        P.drain("vector")
                        vb_tile = bc_prep(facc.ap()[:, 4:5], facc.ap()[:, 5:6])
                        P.do("vector", lambda e: e.tensor_mul(rtile.ap()[32:48, :], rtile.ap()[0:16, :],
                                                              fwr.ap()[:, :]))
                        P.do("vector", lambda e: e.tensor_mul(rtile.ap()[64:80, :], rtile.ap()[0:16, :],
                                                              swr.ap()[:, :]))
                        P.drain("vector")
                        P.do("vector", lambda e: e.tensor_mul(rtile.ap()[32:48, :], rtile.ap()[32:48, :],
                                                              pb[3].ap()[0:16, :]))
                        P.do("vector", lambda e: e.tensor_mul(rtile.ap()[64:80, :], rtile.ap()[64:80, :],
                                                              pb[3].ap()[0:16, :]), inc=f"{tag}_ve_r2")
                        pv["ve_r2"] = P.cur(f"{tag}_ve_r2")
                        P.drain("vector")

                        for blk in range(NBLK):
                            lcol = tt * TT + blk * 128
                            P.wait("tensor", f"{tag}_sc_row", pv["sc_row"])
                            P.wait("tensor", f"{tag}_ve_kv", max(0, P.cur(f"{tag}_ve_kv") - 1))
                            for c in range(NC):
                                for half in range(2):
                                    P.do("tensor",
                                         lambda e, c=c, half=half, lcol=lcol: e.matmul(
                                             pb[half].ap()[:, :],
                                             lnb_all.ap()[:, c * T + lcol:c * T + lcol + 128],
                                             wk_i.ap()[:, c * D + half * 512:c * D + (half + 1) * 512],
                                             start=(c == 0), stop=(c == NC - 1)),
                                         inc=f"{tag}_pe_k" if (c == NC - 1 and half == 1) else None)
                            pek = P.cur(f"{tag}_pe_k")
                            P.wait("vector", f"{tag}_pe_k", pek)
                            P.wait("vector", f"{tag}_pe_M", P.cur(f"{tag}_pe_M"))
                            P.do("vector", lambda e: e.tensor_copy(k_tm.ap()[:, 0:512], pb[0].ap()[:, :]))
                            P.do("vector", lambda e: e.tensor_copy(k_tm.ap()[:, 512:1024],
                                                                   pb[1].ap()[:, :]),
                                 inc=f"{tag}_ve_kv")
                            P.drain("vector")
                            P.wait("vector", f"{tag}_pe_fx", pv["pe_fx"])
                            P.do("vector", lambda e: e.tensor_mul(k2t.ap()[:, 0:D], k_tm[:], k_tm[:]))
                            P.drain("vector")
                            P.do("vector", lambda e, blk=blk: e.tensor_reduce(
                                kninv.ap()[:, blk * H:(blk + 1) * H],
                                k2t.ap()[:, 0:D].rearrange("p (h d) -> p h d", h=H),
                                mybir.AxisListType.X, Alu.add))
                            P.drain("vector")
                            P.do("vector", lambda e, blk=blk: e.tensor_scalar_max(
                                kninv.ap()[:, blk * H:(blk + 1) * H],
                                kninv.ap()[:, blk * H:(blk + 1) * H], 1e-18), inc=f"{tag}_ve_kn")
                            kn_done = P.cur(f"{tag}_ve_kn")
                            P.wait("scalar", f"{tag}_ve_kn", kn_done)
                            P.do("scalar", lambda e, blk=blk: e.activation(
                                kninv.ap()[:, blk * H:(blk + 1) * H],
                                kninv.ap()[:, blk * H:(blk + 1) * H], Act.Ln))
                            P.drain("scalar")
                            P.do("scalar", lambda e, blk=blk: e.activation(
                                kninv.ap()[:, blk * H:(blk + 1) * H],
                                kninv.ap()[:, blk * H:(blk + 1) * H], Act.Exp, scale=-0.5),
                                inc=f"{tag}_sc_kn")
                            skn = P.cur(f"{tag}_sc_kn")
                            P.wait("tensor", f"{tag}_ve_lnb", lnb_done)
                            P.wait("tensor", f"{tag}_ve_r2", pv["ve_r2"])
                            P.wait("tensor", f"{tag}_ve_kv", max(0, P.cur(f"{tag}_ve_kv") - 1))
                            for c in range(NC):
                                for half in range(2):
                                    P.do("tensor",
                                         lambda e, c=c, half=half, lcol=lcol: e.matmul(
                                             pb[2 + half].ap()[:, :],
                                             lnb_all.ap()[:, c * T + lcol:c * T + lcol + 128],
                                             wv_i.ap()[:, c * D + half * 512:c * D + (half + 1) * 512],
                                             start=(c == 0), stop=(c == NC - 1)),
                                         inc=f"{tag}_pe_v" if (c == NC - 1 and half == 1) else None)
                            pevd = P.cur(f"{tag}_pe_v")
                            P.wait("vector", f"{tag}_pe_v", pevd)
                            P.do("vector", lambda e: e.tensor_copy(v_tm.ap()[:, 0:512], pb[2].ap()[:, :]))
                            P.do("vector", lambda e: e.tensor_copy(v_tm.ap()[:, 512:1024],
                                                                   pb[3].ap()[:, :]),
                                 inc=f"{tag}_ve_kv")
                            P.drain("vector")
                            P.wait("tensor", f"{tag}_ve_r2", pv["ve_r2"])
                            P.wait("tensor", f"{tag}_ve_tin", P.cur(f"{tag}_ve_tin"))
                            P.wait("tensor", f"{tag}_ve_kf", P.cur(f"{tag}_ve_kf"))
                            P.do("tensor", lambda e, blk=blk: e.transpose(
                                pb[4].ap()[:, 0:96], rtile.ap()[0:96, blk * 128:(blk + 1) * 128],
                                ident[0:96, 0:96]), inc=f"{tag}_pe_tin")
                            pv["pe_tin"] = P.cur(f"{tag}_pe_tin")
                            P.wait("vector", f"{tag}_pe_tin", pv["pe_tin"])
                            P.do("vector", lambda e: e.tensor_copy(packT[:], pb[4].ap()[:, 0:96]),
                                 inc=f"{tag}_ve_tin")
                            P.drain("vector")
                            P.wait("vector", f"{tag}_sc_kn", skn)
                            P.wait("vector", f"{tag}_pe_M", P.cur(f"{tag}_pe_M"))
                            P.do("vector", lambda e, blk=blk: e.tensor_mul(
                                c1t[:], packT.ap()[:, 0:16], kninv.ap()[:, blk * H:(blk + 1) * H]))
                            P.drain("vector")
                            P.do("vector", lambda e: e.tensor_tensor(
                                kbh32.ap().rearrange("p (h d) -> p h d", h=H),
                                k_tm.ap().rearrange("p (h d) -> p h d", h=H),
                                c1t.ap().unsqueeze(2).broadcast_to((128, H, DK)), Alu.mult),
                                inc=f"{tag}_ve_kb")
                            kb_done = P.cur(f"{tag}_ve_kb")
                            P.drain("vector")
                            P.do("vector", lambda e: e.tensor_tensor(
                                kk.ap().rearrange("p (h x d) -> p h x d", h=H, x=2)[:, :, 0, :],
                                kbh32.ap().rearrange("p (h d) -> p h d", h=H),
                                packT.ap()[:, 32:48].unsqueeze(2).broadcast_to((128, H, DK)),
                                Alu.mult))
                            P.do("vector", lambda e: e.tensor_tensor(
                                kk.ap().rearrange("p (h x d) -> p h x d", h=H, x=2)[:, :, 1, :],
                                kbh32.ap().rearrange("p (h d) -> p h d", h=H),
                                packT.ap()[:, 64:80].unsqueeze(2).broadcast_to((128, H, DK)),
                                Alu.mult), inc=f"{tag}_ve_kk")
                            kk_done = P.cur(f"{tag}_ve_kk")
                            P.drain("vector")
                            P.wait("tensor", f"{tag}_ve_kb", kb_done)
                            P.wait("tensor", f"{tag}_ve_tin", P.cur(f"{tag}_ve_tin"))
                            P.wait("tensor", f"{tag}_ve_kf", P.cur(f"{tag}_ve_kf"))
                            P.wait("tensor", f"{tag}_ve_bce", pv["ve_bce"])
                            P.wait("tensor", f"{tag}_sc_fx", P.cur(f"{tag}_sc_fx"))
                            for c in range(NC):
                                P.do("tensor",
                                     lambda e, c=c: e.transpose(
                                         pb[4 + c // 4].ap()[:, (c % 4) * 128:(c % 4) * 128 + 128],
                                         kbh32.ap()[:, c * 128:(c + 1) * 128], ident),
                                     inc=f"{tag}_pe_kt" if c == NC - 1 else None)
                            kt_done = P.cur(f"{tag}_pe_kt")
                            P.wait("vector", f"{tag}_pe_kt", kt_done)
                            for c in range(NC):
                                P.do("vector",
                                     lambda e, c=c, blk=blk: e.tensor_copy(
                                         kbh_fm.ap()[:, c * TT + blk * 128:c * TT + blk * 128 + 128],
                                         pb[4 + c // 4].ap()[:, (c % 4) * 128:(c % 4) * 128 + 128]),
                                     inc=f"{tag}_ve_kf" if c == NC - 1 else None)
                            P.drain("vector")

                            P.wait("tensor", f"{tag}_ve_kk", kk_done)
                            P.wait("tensor", f"{tag}_ve_acc", pv["ve_acc"])
                            for h in range(H):
                                bk = pb[6 + h // 8]
                                st_ = (blk == 0 and h % 8 == 0)
                                sp_ = (blk == 3 and h % 8 == 7)
                                P.do("tensor",
                                     lambda e, h=h, bk=bk, st_=st_, sp_=sp_: e.matmul(
                                         bk.ap()[:, (h % 8) * DK:(h % 8 + 1) * DK],
                                         kk.ap()[:, h * 128:(h + 1) * 128],
                                         v_tm.ap()[:, h * DK:(h + 1) * DK],
                                         start=st_, stop=sp_),
                                     inc=f"{tag}_pe_M" if h == H - 1 else None)
                        kf_all = P.cur(f"{tag}_ve_kf")
                        P.wait("tensor", f"{tag}_ve_kf", kf_all)
                        P.wait("tensor", f"{tag}_sc_hm", P.cur(f"{tag}_sc_hm"))
                        P.wait("tensor", f"{tag}_ve_kv", P.cur(f"{tag}_ve_kv"))
                        P.wait("tensor", f"{tag}_ve_lnb", lnb_done)
                        for h in range(H):
                            c, par = h // 2, h % 2
                            bk = pb[2 + h // 4]
                            P.do("tensor",
                                 lambda e, h=h, c=c, par=par, bk=bk: e.matmul(
                                     bk.ap()[32 * (h % 4):32 * (h % 4) + 32, :],
                                     wt1_i.ap()[64 * par:64 * par + 64, :],
                                     kbh_fm.ap()[64 * par:64 * par + 64, c * TT:(c + 1) * TT],
                                     start=True, stop=True,
                                     tile_position=(64 * par, 32 * (h % 4))),
                                 inc=f"{tag}_pe_hm" if h == H - 1 else None)
                        pv["pe_hm"] = P.cur(f"{tag}_pe_hm")
                        if tt + 1 < NTILE:
                            rt_next = a_r_load(tt + 1)
                            sq_next = a_s1(tt + 1, rt_next)
                        P.wait("scalar", f"{tag}_pe_hm", pv["pe_hm"])
                        P.wait("scalar", f"{tag}_pe_fx", pv["pe_fx"])
                        for g in range(4):
                            P.do("scalar",
                                 lambda e, g=g: e.activation(
                                     hmid_sb.ap()[:, g * TT:(g + 1) * TT], pb[2 + g].ap()[:, :],
                                     Act.Silu, bias=cst.ap()[:, C_BTF1:C_BTF1 + 1]),
                                 inc=f"{tag}_sc_hm" if g == 3 else None)
                        hm_done = P.cur(f"{tag}_sc_hm")
                        P.wait("tensor", f"{tag}_sc_hm", hm_done)
                        P.wait("tensor", f"{tag}_ve_bce", pv["ve_bce"])
                        for g in range(4):
                            P.do("tensor",
                                 lambda e, g=g: e.matmul(
                                     pb[5].ap()[0:16, :], wt2_i.ap()[:, g * H:(g + 1) * H],
                                     hmid_sb.ap()[:, g * TT:(g + 1) * TT],
                                     start=(g == 0), stop=(g == 3)),
                                 inc=f"{tag}_pe_fx" if g == 3 else None)
                        pv["pe_fx"] = P.cur(f"{tag}_pe_fx")
                        P.wait("scalar", f"{tag}_pe_fx", pv["pe_fx"])
                        P.do("scalar", lambda e, tt=tt: e.activation(
                            prows.ap()[:, tt * TT:(tt + 1) * TT], pb[5].ap()[0:16, :], Act.Sigmoid,
                            bias=cst.ap()[0:16, C_BTF2:C_BTF2 + 1]), inc=f"{tag}_sc_fx")
                        fx_done = P.cur(f"{tag}_sc_fx")
                        P.wait("vector", f"{tag}_sc_fx", fx_done)
                        P.do("vector", lambda e, tt=tt: e.tensor_scalar(
                            prows.ap()[:, tt * TT:(tt + 1) * TT],
                            prows.ap()[:, tt * TT:(tt + 1) * TT],
                            0.01, 0.99, Alu.max, Alu.min, accum_out=None), inc=f"{tag}_ve_fx")
                        P.drain("vector")

                        P.wait("tensor", f"{tag}_sc_fx", fx_done)
                        P.wait("tensor", f"{tag}_ve_kf", P.cur(f"{tag}_ve_kf"))
                        pv["ve_bce"] = bc_fs(facc.ap()[:, 4:5], facc.ap()[:, 5:6], pb[5],
                                             ebc.ap()[:, :], vb=vb_tile)
                        P.wait("vector", f"{tag}_pe_M", P.cur(f"{tag}_pe_M"))
                        P.do("vector", lambda e: e.tensor_tensor(
                            acc_M.ap().rearrange("p (h d) -> p h d", h=H),
                            acc_M.ap().rearrange("p (h d) -> p h d", h=H),
                            ebc.ap().unsqueeze(2).broadcast_to((128, H, DK)), Alu.mult))
                        P.drain("vector")
                        P.do("vector", lambda e: e.tensor_add(acc_M.ap()[:, 0:512],
                                                              acc_M.ap()[:, 0:512], pb[6].ap()[:, :]))
                        P.do("vector", lambda e: e.tensor_add(acc_M.ap()[:, 512:1024],
                                                              acc_M.ap()[:, 512:1024], pb[7].ap()[:, :]),
                             inc=f"{tag}_ve_acc")
                        pv["ve_acc"] = P.cur(f"{tag}_ve_acc")
                        P.drain("vector")

                for s_ in ("pe_k", "pe_v", "pe_M", "pe_kt", "pe_hm", "pe_fx", "sc_hm",
                           "ve_kk", "ve_kf", "ve_acc", "ve_kv", "ve_tin", "sc_kn", "ve_kn",
                           "pe_tin", "sc_fx", "ve_fx", "pe_row", "sc_row"):
                    P.wait("gpsimd", f"{tag}_{s_}", P.cur(f"{tag}_{s_}"))
                P.wait("gpsimd", f"{tag}_ve_acc", P.cur(f"{tag}_ve_acc"))
                P.do("gpsimd", lambda e: e.dma_start(out=cc_in[:, 0:1024], in_=acc_M[:]),
                     inc=f"{tag}_ccin", n=16)
                P.do("gpsimd", lambda e: e.dma_start(out=cc_in[0:16, 1024:1028], in_=facc.ap()[:, 0:4]),
                     inc=f"{tag}_ccin", n=16)
                P.wait("gpsimd", "cl", 32)
                P.do("gpsimd", lambda e: e.dma_start(out=cc_in[16:128, 1024:1028],
                                                     in_=cst.ap()[16:128, 272:276]),
                     inc=f"{tag}_ccin", n=16)
                if dbg:
                    P.do("gpsimd", lambda e: e.dma_start(out=d_accM, in_=acc_M[:]),
                         inc=f"{tag}_dbg", n=16)
                    P.do("gpsimd", lambda e: e.dma_start(out=d_facc, in_=facc[:]),
                         inc=f"{tag}_dbg", n=16)
                    P.wait("gpsimd", f"{tag}_ve_fx", P.cur(f"{tag}_ve_fx"))
                    P.do("gpsimd", lambda e: e.dma_start(out=d_flux, in_=prows[:]),
                         inc=f"{tag}_dbg", n=16)
                P.wait("gpsimd", f"{tag}_ccin")
                P.do("gpsimd", lambda e: e.collective_compute(
                    "AllGather", Alu.bypass,
                    replica_groups=[[2 * i, 2 * i + 1] for i in range(n_pairs)],
                    ins=[cc_in.ap()], outs=[cc_out.ap()]), inc=f"{tag}_cc")

                s2 = _ES()
                with s2:
                    def s2b(name, shape, dt):
                        return s2.enter_context(nc.sbuf_tensor(name, shape, dt))

                    wq_i = s2b(f"{tag}_wq", [128, NC * D], BF16)
                    wo_i = s2b(f"{tag}_wo", [128, NC * D], BF16)
                    q_fm = s2b(f"{tag}_qfm", [128, NC * TT], BF16)
                    ma_sb = s2b(f"{tag}_ma", [128, 1024], F32)
                    mb_sb = s2b(f"{tag}_mb", [128, 1024], F32)
                    mn_sb = ma_sb
                    rab = s2b(f"{tag}_rab", [16, 8], F32)
                    bdf = s2b(f"{tag}_bdf", [128, NC * 128], BF16)
                    bds = s2b(f"{tag}_bds", [128, NC * 128], BF16)
                    o32 = s2b(f"{tag}_o32", [128, 1024], F32)
                    gT = s2b(f"{tag}_gT", [128, 64], F32)
                    dcol = s2b(f"{tag}_dcol", [16, 4], F32)
                    dbc = s2b(f"{tag}_dbc", [128, 16], F32)

                    P.do("gpsimd", lambda e: e.dma_start(out=wq_i[:], in_=wimg["wq"]),
                         inc=f"{tag}_wl", n=16)
                    wl2q = P.cur(f"{tag}_wl")
                    P.wait("gpsimd", f"{tag}_wl", wl2q)
                    P.do("gpsimd", lambda e: e.dma_start(out=wo_i[:], in_=wimg["wo"]),
                         inc=f"{tag}_wl", n=16)
                    wl2 = P.cur(f"{tag}_wl")
                    P.wait("gpsimd", f"{tag}_cc")
                    P.do("gpsimd", lambda e: e.dma_start(out=ma_sb[:], in_=cc_out[0:128, 0:1024]),
                         inc=f"{tag}_ccl", n=16)
                    P.do("gpsimd", lambda e: e.dma_start(out=mb_sb[:], in_=cc_out[128:256, 0:1024]),
                         inc=f"{tag}_ccl", n=16)
                    P.do("gpsimd", lambda e: e.dma_start(out=rab.ap()[:, 0:4],
                                                         in_=cc_out[0:16, 1024:1028]),
                         inc=f"{tag}_ccl", n=16)
                    P.do("gpsimd", lambda e: e.dma_start(out=rab.ap()[:, 4:8],
                                                         in_=cc_out[128:144, 1024:1028]),
                         inc=f"{tag}_ccl", n=16)
                    ccl_done = P.cur(f"{tag}_ccl")

                    P.wait("scalar", f"{tag}_ccl", ccl_done)
                    P.do("scalar", lambda e: e.activation(dcol.ap()[:, 0:1], rab.ap()[:, 4:5], Act.Exp))
                    P.do("scalar", lambda e: e.activation(dcol.ap()[:, 1:2], rab.ap()[:, 5:6], Act.Exp),
                         inc=f"{tag}_sc_eb")
                    eb_done = P.cur(f"{tag}_sc_eb")
                    P.wait("vector", f"{tag}_sc_eb", eb_done)
                    P.do("vector", lambda e: e.scalar_tensor_tensor(
                        dcol.ap()[:, 2:3], rab.ap()[:, 2:3], dcol.ap()[:, 0:1], rab.ap()[:, 6:7],
                        Alu.mult, Alu.add))
                    P.do("vector", lambda e: e.scalar_tensor_tensor(
                        dcol.ap()[:, 3:4], rab.ap()[:, 3:4], dcol.ap()[:, 1:2], rab.ap()[:, 7:8],
                        Alu.mult, Alu.add))
                    P.drain("vector")
                    P.do("vector", lambda e: e.tensor_scalar_add(dcol.ap()[:, 2:4], dcol.ap()[:, 2:4],
                                                                 1e-6))
                    P.drain("vector")
                    P.do("vector", lambda e: e.reciprocal(dcol.ap()[:, 2:4], dcol.ap()[:, 2:4]))
                    P.drain("vector")
                    bc_fs(dcol.ap()[:, 0:1], dcol.ap()[:, 1:2], pb[5], ebc.ap()[:, :])
                    dj = bc_fs(dcol.ap()[:, 2:3], dcol.ap()[:, 3:4], pb[5], dbc.ap()[:, :])
                    P.wait("vector", f"{tag}_ccl", ccl_done)
                    P.do("vector", lambda e: e.tensor_tensor(
                        mn_sb.ap().rearrange("p (h d) -> p h d", h=H),
                        mn_sb.ap().rearrange("p (h d) -> p h d", h=H),
                        ebc.ap().unsqueeze(2).broadcast_to((128, H, DK)), Alu.mult))
                    P.drain("vector")
                    P.do("vector", lambda e: e.tensor_add(mn_sb[:], mn_sb[:], mb_sb[:]))
                    P.drain("vector")
                    P.do("vector", lambda e: e.tensor_tensor(
                        mn_sb.ap().rearrange("p (h d) -> p h d", h=H),
                        mn_sb.ap().rearrange("p (h d) -> p h d", h=H),
                        dbc.ap().unsqueeze(2).broadcast_to((128, H, DK)), Alu.mult),
                        inc=f"{tag}_ve_mn")
                    mn_done = P.cur(f"{tag}_ve_mn")
                    P.drain("vector")

                    if dbg:
                        P.wait("gpsimd", f"{tag}_ve_mn", mn_done)
                        P.do("gpsimd", lambda e: e.dma_start(out=d_mn, in_=mn_sb[:]),
                             inc=f"{tag}_dbg", n=16)
                        P.do("gpsimd", lambda e: e.dma_start(out=d_mb, in_=mb_sb[:]),
                             inc=f"{tag}_dbg", n=16)
                        P.do("gpsimd", lambda e: e.dma_start(out=d_rab, in_=rab[:]),
                             inc=f"{tag}_dbg", n=16)
                        P.do("gpsimd", lambda e: e.dma_start(out=d_dcol, in_=dcol[:]),
                             inc=f"{tag}_dbg", n=16)
                        P.do("gpsimd", lambda e: e.dma_start(out=d_ebc, in_=ebc[:]),
                             inc=f"{tag}_dbg", n=16)
                        P.do("gpsimd", lambda e: e.dma_start(out=d_dbc, in_=dbc[:]),
                             inc=f"{tag}_dbg", n=16)
                    P.wait("gpsimd", f"{tag}_ve_mn", mn_done)
                    P.do("gpsimd", lambda e: e.memset(bdf[:], 0.0))
                    P.do("gpsimd", lambda e: e.memset(bds[:], 0.0))
                    P.drain("gpsimd")
                    mn_e = mn_sb.ap().rearrange("p (c x) -> p c x", c=H)
                    P.do("gpsimd", lambda e: e.dma_start(
                        out=bdf.ap().rearrange("p (c x) -> p c x", c=NC)[0:64, :, 0:64],
                        in_=mn_e[0:64, 0::2, :]), inc=f"{tag}_bd", n=16)
                    P.do("gpsimd", lambda e: e.dma_start(
                        out=bdf.ap().rearrange("p (c x) -> p c x", c=NC)[64:128, :, 64:128],
                        in_=mn_e[0:64, 1::2, :]), inc=f"{tag}_bd", n=16)
                    P.do("gpsimd", lambda e: e.dma_start(
                        out=bds.ap().rearrange("p (c x) -> p c x", c=NC)[0:64, :, 0:64],
                        in_=mn_e[64:128, 0::2, :]), inc=f"{tag}_bd", n=16)
                    P.do("gpsimd", lambda e: e.dma_start(
                        out=bds.ap().rearrange("p (c x) -> p c x", c=NC)[64:128, :, 64:128],
                        in_=mn_e[64:128, 1::2, :]), inc=f"{tag}_bd", n=16)
                    bd_done = P.cur(f"{tag}_bd")

                    pv2 = dict(ve_q=0, pe_wo=0, ve_ro=0)
                    rt = r_t.ap()[:, 0:NC * TT]
                    s2rl_base = P.cur("f_rl0")
                    for tt in range(NTILE):
                        if dbg:
                            P.wait("vector", f"{tag}_dbg", P.cur(f"{tag}_dbg"))
                        P.wait("sync", "f_rs")
                        P.wait("sync", f"{tag}_ve_lnb", P.cur(f"{tag}_ve_lnb"))
                        P.do("sync",
                             lambda e, tt=tt, rt=rt: e.dma_start(
                                 out=rt.rearrange("p (c t) -> p c t", c=NC), in_=r_view_tile(tt)),
                             inc="f_rl0", n=16)
                        P.wait("tensor", f"{tag}_wl", wl2q)
                        P.wait("tensor", f"{tag}_ve_ro", pv2["ve_ro"])
                        P.wait("tensor", f"{tag}_ve_o", P.cur(f"{tag}_ve_o"))
                        for oc in range(NC):
                            bk = pb[oc % 2]
                            P.wait("tensor", f"{tag}_ve_q",
                                   max(0, P.cur(f"{tag}_ve_q") - 1))
                            for c in range(NC):
                                P.do("tensor",
                                     lambda e, oc=oc, c=c, tt=tt, bk=bk: e.matmul(
                                         bk.ap()[:, :],
                                         wq_i.ap()[:, c * D + oc * 128:c * D + oc * 128 + 128],
                                         lnb_all.ap()[:, c * T + tt * TT:c * T + tt * TT + TT],
                                         start=(c == 0), stop=(c == NC - 1)),
                                     inc=f"{tag}_pe_q" if c == NC - 1 else None)
                            peq = P.cur(f"{tag}_pe_q")
                            P.wait("vector", f"{tag}_pe_q", peq)
                            P.do("vector",
                                 lambda e, oc=oc, bk=bk: e.tensor_copy(
                                     q_fm.ap()[:, oc * TT:(oc + 1) * TT], bk.ap()[:, :]),
                                 inc=f"{tag}_ve_q")
                        veq = P.cur(f"{tag}_ve_q")
                        P.drain("vector")
                        P.wait("vector", f"{tag}_pe_ss", P.cur(f"{tag}_pe_ss"))
                        for c in range(NC):
                            P.do("vector",
                                 lambda e, c=c: e.tensor_mul(
                                     asc.ap()[:, c * TT:(c + 1) * TT],
                                     q_fm.ap()[:, c * TT:(c + 1) * TT],
                                     q_fm.ap()[:, c * TT:(c + 1) * TT]),
                                 inc=f"{tag}_ve_q2" if c == NC - 1 else None)
                        q2_done = P.cur(f"{tag}_ve_q2")
                        P.drain("vector")
                        P.wait("tensor", f"{tag}_ve_q2", q2_done)
                        P.wait("tensor", f"{tag}_sc_qn", P.cur(f"{tag}_sc_qn"))
                        for c in range(NC):
                            P.do("tensor",
                                 lambda e, c=c: e.matmul(
                                     pb[2].ap()[0:16, :],
                                     cbf.ap()[:, CB_BLK16 + c * 16:CB_BLK16 + (c + 1) * 16],
                                     asc.ap()[:, c * TT:(c + 1) * TT],
                                     start=(c == 0), stop=(c == NC - 1)),
                                 inc=f"{tag}_pe_ss" if c == NC - 1 else None)
                        ss_done = P.cur(f"{tag}_pe_ss")
                        P.wait("vector", f"{tag}_pe_ss", ss_done)
                        P.do("vector", lambda e: e.tensor_scalar_max(qnr.ap()[:, :],
                                                                     pb[2].ap()[0:16, :], 1e-18),
                             inc=f"{tag}_ve_ssm")
                        ssm_done = P.cur(f"{tag}_ve_ssm")
                        P.drain("vector")
                        P.wait("scalar", f"{tag}_ve_ssm", ssm_done)
                        P.do("scalar", lambda e: e.activation(qnr.ap()[:, :], qnr.ap()[:, :], Act.Ln))
                        P.drain("scalar")
                        P.do("scalar", lambda e: e.activation(qnr.ap()[:, :], qnr.ap()[:, :],
                                                              Act.Exp, scale=-0.5), inc=f"{tag}_sc_qn")
                        qn_done = P.cur(f"{tag}_sc_qn")
                        P.wait("vector", f"{tag}_sc_qn", qn_done)
                        P.do("vector", lambda e, tt=tt: e.tensor_scalar(
                            fwr.ap()[:, :], prows.ap()[:, tt * TT:(tt + 1) * TT],
                            0.3, 0.5, Alu.mult, Alu.add, accum_out=None))
                        P.drain("vector")
                        P.do("vector", lambda e: e.tensor_mul(rtile.ap()[0:16, :], fwr.ap()[:, :],
                                                              qnr.ap()[:, :]))
                        P.drain("vector")
                        P.do("vector", lambda e: e.tensor_scalar(
                            fwr.ap()[:, :], fwr.ap()[:, :], -1.0, 1.0,
                            Alu.mult, Alu.add, accum_out=None))
                        P.drain("vector")
                        P.do("vector", lambda e: e.tensor_mul(rtile.ap()[32:48, :], fwr.ap()[:, :],
                                                              qnr.ap()[:, :]), inc=f"{tag}_ve_g")
                        g_done = P.cur(f"{tag}_ve_g")
                        P.drain("vector")

                        for blk in range(NBLK):
                            P.wait("tensor", f"{tag}_ve_g", g_done)
                            P.wait("tensor", f"{tag}_ve_gt", P.cur(f"{tag}_ve_gt"))
                            P.wait("tensor", f"{tag}_ve_of", P.cur(f"{tag}_ve_of"))
                            P.do("tensor", lambda e, blk=blk: e.transpose(
                                pb[3].ap()[:, 0:64], rtile.ap()[0:64, blk * 128:(blk + 1) * 128],
                                ident[0:64, 0:64]), inc=f"{tag}_pe_gt")
                            pegt = P.cur(f"{tag}_pe_gt")
                            P.wait("vector", f"{tag}_pe_gt", pegt)
                            P.do("vector", lambda e: e.tensor_copy(gT[:], pb[3].ap()[:, 0:64]),
                                 inc=f"{tag}_ve_gt")
                            P.drain("vector")
                            P.wait("tensor", f"{tag}_ve_gt", P.cur(f"{tag}_ve_gt"))
                            P.wait("tensor", f"{tag}_ve_o", P.cur(f"{tag}_ve_o"))
                            P.wait("tensor", f"{tag}_bd", bd_done)
                            for c in range(NC):
                                qs = q_fm.ap()[:, c * TT + blk * 128:c * TT + blk * 128 + 128]
                                P.do("tensor",
                                     lambda e, c=c, qs=qs: e.matmul(
                                         pb[4 + c // 4].ap()[:, (c % 4) * 128:(c % 4) * 128 + 128],
                                         qs, bdf.ap()[:, c * 128:(c + 1) * 128],
                                         start=True, stop=True))
                                P.do("tensor",
                                     lambda e, c=c, qs=qs: e.matmul(
                                         pb[6 + c // 4].ap()[:, (c % 4) * 128:(c % 4) * 128 + 128],
                                         qs, bds.ap()[:, c * 128:(c + 1) * 128],
                                         start=True, stop=True),
                                     inc=f"{tag}_pe_bd" if c == NC - 1 else None)
                            bd_mm = P.cur(f"{tag}_pe_bd")
                            P.wait("vector", f"{tag}_pe_bd", bd_mm)
                            for half in range(2):
                                P.do("vector", lambda e, half=half: e.tensor_tensor(
                                    o32.ap()[:, half * 512:(half + 1) * 512]
                                       .rearrange("p (h d) -> p h d", d=DK),
                                    pb[4 + half].ap()[:, :].rearrange("p (h d) -> p h d", d=DK),
                                    gT.ap()[:, half * 8:half * 8 + 8].unsqueeze(2)
                                      .broadcast_to((128, 8, DK)), Alu.mult))
                                P.do("vector", lambda e, half=half: e.tensor_tensor(
                                    pb[half].ap()[:, :].rearrange("p (h d) -> p h d", d=DK),
                                    pb[6 + half].ap()[:, :].rearrange("p (h d) -> p h d", d=DK),
                                    gT.ap()[:, 32 + half * 8:32 + half * 8 + 8].unsqueeze(2)
                                      .broadcast_to((128, 8, DK)), Alu.mult))
                            P.drain("vector")
                            for half in range(2):
                                P.do("vector", lambda e, half=half: e.tensor_add(
                                    o32.ap()[:, half * 512:(half + 1) * 512],
                                    o32.ap()[:, half * 512:(half + 1) * 512],
                                    pb[half].ap()[:, :]),
                                    inc=f"{tag}_ve_o" if half == 1 else None)
                            veo = P.cur(f"{tag}_ve_o")
                            P.drain("vector")
                            P.wait("tensor", f"{tag}_ve_o", veo)
                            P.wait("tensor", f"{tag}_ve_of", P.cur(f"{tag}_ve_of"))
                            for c in range(NC):
                                P.do("tensor",
                                     lambda e, c=c: e.transpose(
                                         pb[2 + c // 4].ap()[:, (c % 4) * 128:(c % 4) * 128 + 128],
                                         o32.ap()[:, c * 128:(c + 1) * 128], ident),
                                     inc=f"{tag}_pe_ot" if c == NC - 1 else None)
                            peot = P.cur(f"{tag}_pe_ot")
                            P.wait("vector", f"{tag}_pe_ot", peot)
                            for c in range(NC):
                                P.do("vector",
                                     lambda e, c=c, blk=blk: e.tensor_copy(
                                         asc.ap()[:, c * TT + blk * 128:c * TT + blk * 128 + 128],
                                         pb[2 + c // 4].ap()[:, (c % 4) * 128:(c % 4) * 128 + 128]),
                                     inc=f"{tag}_ve_of" if c == NC - 1 else None)
                            P.drain("vector")

                        of_all = P.cur(f"{tag}_ve_of")
                        if dbg and tt == 0:
                            P.wait("gpsimd", f"{tag}_ve_q", P.cur(f"{tag}_ve_q"))
                            P.do("gpsimd", lambda e: e.dma_start(out=d_qfm, in_=q_fm[:]),
                                 inc=f"{tag}_dbg", n=16)
                            P.wait("gpsimd", f"{tag}_ve_of", of_all)
                            P.do("gpsimd", lambda e: e.dma_start(out=d_ofm, in_=asc[:]),
                                 inc=f"{tag}_dbg", n=16)
                        P.wait("tensor", f"{tag}_wl", wl2)
                        P.wait("tensor", f"{tag}_ve_of", of_all)
                        for oc in range(NC):
                            bk = pb[oc % 2]
                            P.wait("tensor", f"{tag}_ve_ro", max(0, P.cur(f"{tag}_ve_ro") - 1))
                            for c in range(NC):
                                P.do("tensor",
                                     lambda e, oc=oc, c=c, bk=bk: e.matmul(
                                         bk.ap()[:, :],
                                         wo_i.ap()[:, c * D + oc * 128:c * D + oc * 128 + 128],
                                         asc.ap()[:, c * TT:(c + 1) * TT],
                                         start=(c == 0), stop=(c == NC - 1)),
                                     inc=f"{tag}_pe_wo" if c == NC - 1 else None)
                            pewo = P.cur(f"{tag}_pe_wo")
                            P.wait("vector", f"{tag}_pe_wo", pewo)
                            P.wait("vector", "f_rl0", s2rl_base + 16 * (tt + 1))
                            P.do("vector",
                                 lambda e, oc=oc, bk=bk, rt=rt: e.tensor_add(
                                     rt[:, oc * TT:(oc + 1) * TT], rt[:, oc * TT:(oc + 1) * TT],
                                     bk.ap()[:, :]),
                                 inc=f"{tag}_ve_ro")
                            P.drain("vector")
                        pv2["ve_ro"] = P.cur(f"{tag}_ve_ro")
                        P.wait("scalar", f"{tag}_ve_ro", pv2["ve_ro"])
                        P.do("scalar",
                             lambda e, tt=tt, rt=rt: e.dma_start(
                                 out=r_view_tile(tt), in_=rt.rearrange("p (c t) -> p c t", c=NC)),
                             inc="f_rs", n=16)

                P.wait("sync", "f_rs")

        if "p0" in enable:
            transpose_pipe(
                "p0", pb[0:4],
                lambda e, blk, buf: e.dma_start(out=buf, in_=x_in[blk * 128:(blk + 1) * 128, :]),
                lambda e, blk, buf: e.dma_start(out=r_view_blk(blk),
                                                in_=buf.rearrange("p (c t) -> p c t", c=NC)),
            )
            P.fence()

        if "ffn1" in enable:
            ffn_phase("f1", first=True)
            P.fence()

        if "attn" in enable:
            attn_phase()
            P.fence()

        if "ffn2" in enable:
            ffn_phase("f2")
            P.fence()

        if "p4" in enable:
            transpose_pipe(
                "p4", pb[0:4],
                lambda e, blk, buf: e.dma_start(out=buf.rearrange("p (c t) -> p c t", c=NC),
                                                in_=r_view_blk(blk)),
                lambda e, blk, buf: e.dma_start(out=y_out[blk * 128:(blk + 1) * 128, :], in_=buf),
            )

        P.wait("sync", "tp_st0")
        P.wait("sync", "tp_st1")

        block = es.enter_context(nc.Block())
        P.emit(block)

    return nc



def _prep_inputs(L0_attn_x, L0_attn_Wq, L0_attn_Wk, L0_attn_Wv, L0_attn_Wb,
                 L0_attn_Wo, L0_attn_Wfd, L0_attn_bfd, L0_attn_Wsd, L0_attn_bsd,
                 L0_attn_Wtf1, L0_attn_btf1, L0_attn_Wtf2, L0_attn_btf2,
                 L0_ffn_Ws1, L0_ffn_Ws2, L0_ffn_W1, L0_ffn_W2, L0_ffn_Wr):
    f32 = np.float32

    def img_kc(W):
        K, M = W.shape
        return np.ascontiguousarray(
            W.reshape(K // 128, 128, M).transpose(1, 0, 2).reshape(128, (K // 128) * M)
        ).astype(f32)

    wq = img_kc(np.asarray(L0_attn_Wq))
    wk = img_kc(np.asarray(L0_attn_Wk))
    wv = img_kc(np.asarray(L0_attn_Wv))
    wo = img_kc(np.asarray(L0_attn_Wo))
    wbf = img_kc(np.asarray(L0_attn_Wb))
    wfdf = img_kc(np.asarray(L0_attn_Wfd))
    wsdf = img_kc(np.asarray(L0_attn_Wsd))
    wrf = img_kc(np.asarray(L0_ffn_Wr))

    W1 = np.asarray(L0_ffn_W1)
    Ws1 = np.asarray(L0_ffn_Ws1)
    w1all = np.concatenate([W1, Ws1[None]], axis=0)
    w1f = np.ascontiguousarray(
        w1all.reshape(9, NC, 128, ED).transpose(2, 0, 1, 3).reshape(128, 9 * NC * ED)
    ).astype(f32)

    W2 = np.asarray(L0_ffn_W2)
    Ws2 = np.asarray(L0_ffn_Ws2)
    w2all = np.concatenate([W2, Ws2[None]], axis=0)
    w2f = np.ascontiguousarray(
        w2all.reshape(9, 4, 128, D).transpose(2, 0, 1, 3).reshape(128, 9 * 4 * D)
    ).astype(f32)

    Wtf1 = np.asarray(L0_attn_Wtf1).astype(f32)
    wtf1f = np.concatenate([Wtf1, Wtf1], axis=0)

    Wtf2 = np.asarray(L0_attn_Wtf2).astype(f32).reshape(32)
    bd = np.zeros((512, 16), f32)
    for h in range(16):
        bd[h * 32:(h + 1) * 32, h] = Wtf2
    wtf2f = np.ascontiguousarray(
        bd.reshape(4, 128, 16).transpose(1, 0, 2).reshape(128, 64)
    )

    consts = np.zeros((128, CW), f32)
    consts[:, C_ID:C_ID + 128] = np.eye(128, dtype=f32)
    consts[:, C_ONES:C_ONES + 128] = 1.0
    consts[:, C_BFD] = np.tile(np.asarray(L0_attn_bfd).astype(f32), 8)
    consts[:, C_BSD] = np.tile(np.asarray(L0_attn_bsd).astype(f32), 8)
    consts[:, C_BTF1] = np.tile(np.asarray(L0_attn_btf1).astype(f32), 4)
    consts[:, C_BTF2] = float(np.asarray(L0_attn_btf2).reshape(-1)[0])
    consts[:, C_EPSLN] = EPS_LN
    consts[:, C_ZERO] = 0.0
    consts[:, C_EPS6] = 1e-6
    consts[:, C_EPS18] = 1e-18

    cbf = np.zeros((128, CBW), ml_dtypes.bfloat16)
    cbf[:, CB_ONESCOL] = 1.0
    cbf[0:64, CB_BLK] = 1.0
    cbf[64:128, CB_BLK + 1] = 1.0
    cbf[:, CB_ONESROW:CB_ONESROW + 128] = 1.0
    for e in range(NE):
        cbf[e, CB_SEL8 + e * 128:CB_SEL8 + (e + 1) * 128] = 1.0
    for c in range(NC):
        cbf[0:64, CB_BLK16 + c * 16 + 2 * c] = 1.0
        cbf[64:128, CB_BLK16 + c * 16 + 2 * c + 1] = 1.0

    x = np.asarray(L0_attn_x).astype(f32)
    weights = dict(wq=wq, wk=wk, wv=wv, wo=wo, w1f=w1f, w2f=w2f, wrf=wrf,
                   wbf=wbf, wfdf=wfdf, wsdf=wsdf, wtf1f=wtf1f, wtf2f=wtf2f,
                   consts=consts, cbf=cbf)
    shards = []
    for c in range(NCORE):
        b, h = c // 2, c % 2
        shard = np.ascontiguousarray(x[b, h * T:(h + 1) * T, :])
        m = dict(weights)
        m["x"] = shard
        shards.append(m)
    return shards


def _run(inputs, enable=("p0", "ffn1", "attn", "ffn2", "p4")):
    in_maps = _prep_inputs(**inputs)
    nc = build_program(enable)
    res = run_bass_kernel_spmd(nc, in_maps, core_ids=list(range(NCORE)))
    out = np.empty((B, L, D), np.float32)
    for c in range(NCORE):
        b, h = c // 2, c % 2
        out[b, h * T:(h + 1) * T, :] = res.results[c]["y"]
    return out


def kernel(**inputs):
    return _run(inputs)
